# version 02
# baseline (speedup 1.0000x reference)
import math
from contextlib import ExitStack

import numpy as np
import ml_dtypes

import concourse.bass as bass
import concourse.mybir as mybir
from concourse.bass_types import AP as bass_types_AP
from concourse.alu_op_type import AluOpType as ALU
from concourse.bass_utils import run_bass_kernel_spmd

F32 = mybir.dt.float32
BF16 = mybir.dt.bfloat16
AF = mybir.ActivationFunctionType
AX = mybir.AxisListType
BF16NP = ml_dtypes.bfloat16

P = 128
D = 1024
DT = D // P
T = 2048
B_, SEQ = 4, 4096

SEM_ROT = 24000
DMA_K = 4

QUEUES = ("pe", "act", "dve", "pool", "sp")
_ENG_ATTR = {"pe": "tensor", "act": "scalar", "dve": "vector", "pool": "gpsimd", "sp": "sync"}


class VOp:
    __slots__ = ("queue", "fn", "kind", "idx", "deps", "mark", "waits", "tag",
                 "dma_idx", "cc_idx", "inc_info")

    def __init__(self, queue, fn, kind, idx, tag=""):
        self.queue = queue
        self.fn = fn
        self.kind = kind
        self.idx = idx
        self.deps = set()
        self.mark = False
        self.waits = []
        self.tag = tag


class VB:

    def __init__(self, nc):
        self.nc = nc
        self.q = {qn: [] for qn in QUEUES}
        self.dma_count = {qn: 0 for qn in QUEUES}
        self.cc_count = 0
        self.rw = {}
        self.rr = {}

    def op(self, queue, fn, reads=(), writes=(), tag=""):
        return self._add(queue, fn, "e", reads, writes, tag)

    def dma(self, queue, fn, reads=(), writes=(), tag=""):
        return self._add(queue, fn, "d", reads, writes, tag)

    def cc(self, fn, reads=(), writes=(), tag=""):
        return self._add("pool", fn, "c", reads, writes, tag)

    PSUM_PREFIXES = ("psf", "psb", "ps")

    def _add(self, queue, fn, kind, reads, writes, tag):
        reads = list(reads)
        writes = list(writes)
        ps_reads = [r for r in reads
                    if isinstance(r, tuple) and isinstance(r[0], str) and r[0] in self.PSUM_PREFIXES]
        if ps_reads:
            reads = [r for r in reads if r not in ps_reads]
            writes = writes + ps_reads
        ops = self.q[queue]
        idx = len(ops)
        o = VOp(queue, fn, kind, idx, tag)
        if kind == "d":
            o.dma_idx = self.dma_count[queue]
            self.dma_count[queue] += 1
            myref = ("d", queue, o.dma_idx)
        elif kind == "c":
            o.cc_idx = self.cc_count
            self.cc_count += 1
            myref = ("c", o.cc_idx)
        else:
            myref = ("e", queue, idx)
        deps = set()
        for r in reads:
            w = self.rw.get(r)
            if w is not None:
                deps.add(w)
        for r in writes:
            w = self.rw.get(r)
            if w is not None:
                deps.add(w)
            for rd in self.rr.get(r, ()):
                deps.add(rd)
        for r in reads:
            self.rr.setdefault(r, []).append(myref)
        for r in writes:
            self.rw[r] = myref
            self.rr[r] = []
        if queue == "pe" and kind == "e":
            o.deps = {d for d in deps if not (d[0] == "e" and d[1] == "pe")}
        else:
            o.deps = {d for d in deps if d != myref}
        ops.append(o)
        return myref

    def barrier_all(self):
        raise NotImplementedError

    def finalize_and_emit(self, es, final_queue="pool"):
        nc = self.nc
        for qn in QUEUES:
            for o in self.q[qn]:
                for d in o.deps:
                    if d[0] == "e":
                        self.q[d[1]][d[2]].mark = True
        mark_val = {}
        n_sems = {}
        for qn in QUEUES:
            cnt = 0
            for o in self.q[qn]:
                if o.kind == "e" and o.mark:
                    mark_val[("e", qn, o.idx)] = (cnt // SEM_ROT, cnt % SEM_ROT + 1)
                    cnt += 1
            n_sems[qn] = cnt // SEM_ROT + 1
        qsems = {qn: [es.enter_context(nc.semaphore(f"sq_{qn}_{i}"))
                      for i in range(n_sems[qn])] for qn in QUEUES}
        dsems = {qn: [es.enter_context(nc.semaphore(f"sd_{qn}_{i}"))
                      for i in range(min(DMA_K, max(1, self.dma_count[qn])))]
                 for qn in QUEUES}
        ccsems = [es.enter_context(nc.semaphore(f"scc_{i}")) for i in range(self.cc_count)]
        for qn in QUEUES:
            for o in self.q[qn]:
                if o.kind == "d" and o.dma_idx >= DMA_K:
                    o.deps.add(("d", qn, o.dma_idx - DMA_K))

        def resolve(d):
            if d[0] == "e":
                g, v = mark_val[d]
                return (qsems[d[1]][g], v)
            if d[0] == "d":
                qn, k = d[1], d[2]
                return (dsems[qn][k % DMA_K], (k // DMA_K + 1) * 16)
            return (ccsems[d[1]], 1)

        def sort_key(d):
            return (d[0], str(d[1]), d[2] if len(d) > 2 else 0)

        for qn in QUEUES:
            floor = {}
            for o in self.q[qn]:
                for d in sorted(o.deps, key=sort_key):
                    sem, val = resolve(d)
                    key = id(sem)
                    if floor.get(key, 0) >= val:
                        continue
                    o.waits.append((sem, val))
                    floor[key] = val

        final_waits = []
        for qn in QUEUES:
            nd = self.dma_count[qn]
            for kk in range(min(DMA_K, nd)):
                cnt = (nd - kk + DMA_K - 1) // DMA_K
                final_waits.append((dsems[qn][kk], cnt * 16))

        block = es.enter_context(nc.Block())
        for qn in QUEUES:
            ops = self.q[qn]
            if not ops and not (qn == final_queue and final_waits):
                continue

            def body(eng, ops=ops, qn=qn):
                for o in ops:
                    for sem, val in o.waits:
                        eng.wait_ge(sem, val)
                    ins = o.fn(eng)
                    if o.kind == "d":
                        ins.then_inc(dsems[qn][o.dma_idx % DMA_K], 16)
                    elif o.kind == "c":
                        ins.then_inc(ccsems[o.cc_idx], 1)
                    elif o.mark:
                        g, _ = mark_val[("e", qn, o.idx)]
                        ins.then_inc(qsems[qn][g], 1)
                if qn == final_queue:
                    for sem, val in final_waits:
                        eng.wait_ge(sem, val)

            getattr(block, _ENG_ATTR[qn])(body)

        stats = {qn: (len(self.q[qn]), sum(1 for o in self.q[qn] if o.mark),
                      sum(len(o.waits) for o in self.q[qn])) for qn in QUEUES}
        return stats


class Buf:
    RGRAN = 512

    def __init__(self, arena, arena_name, off, cols, dtype, tag):
        self.arena = arena
        self.aname = arena_name
        self.off = off
        self.cols = cols
        self.dtype = dtype
        self.tag = tag

    def ap(self, c0=0, c1=None):
        c1 = self.cols if c1 is None else c1
        return self.arena[:, self.off + c0: self.off + c1]

    def prt(self, p0, p1, c0=0, c1=None):
        c1 = self.cols if c1 is None else c1
        return self.arena[p0:p1, self.off + c0: self.off + c1]

    def reg(self, c0=0, c1=None):
        c1 = self.cols if c1 is None else c1
        b0 = (self.off + c0) // self.RGRAN
        b1 = (self.off + c1 - 1) // self.RGRAN
        return [(self.aname, b) for b in range(b0, b1 + 1)]


class Arena:
    def __init__(self, nc, es, name, cols, dtype):
        self.t = es.enter_context(nc.sbuf_tensor(name, [P, cols], dtype))
        self.name = name
        self.cols = cols
        self.ptr = 0
        self.marks = []

    def alloc(self, cols, tag=""):
        assert self.ptr + cols <= self.cols, \
            f"arena {self.name} overflow: {self.ptr}+{cols}>{self.cols} ({tag})"
        b = Buf(self.t, self.name, self.ptr, cols,
                None, tag)
        self.ptr += cols
        return b

    def push(self):
        self.marks.append(self.ptr)

    def pop(self):
        self.ptr = self.marks.pop()


def ceil_div(a, b):
    return (a + b - 1) // b


class Consts:

    def __init__(self, vb, arena, n=16):
        self.vb = vb
        self.buf = arena.alloc(n, "consts")
        self.n = n
        self.map = {}

    def col(self, value):
        value = float(value)
        if value not in self.map:
            i = len(self.map)
            assert i < self.n
            ap = self.buf.ap(i, i + 1)
            rg = self.buf.reg(i, i + 1)
            self.vb.op("dve", (lambda e, ap=ap, value=value: e.memset(ap, value)),
                       reads=[], writes=rg, tag=f"cst_{value}")
            self.map[value] = (ap, rg)
        return self.map[value]



def emit_ln_fm(vb, pb, psr, x_buf, y_buf, rows, ab, bb, ones_col, ones_col_reg,
               ones_row, ones_row_reg, TT, eps, consts=None,
               g_aps=None, b_aps=None, gb_regs=(), sq_tmp=None, prefix="ln"):
    eps_ap, eps_rg = consts.col(float(eps))
    """pb: list of >=4 psum tensors [128,512] f32; psr their region keys.
    rows: Buf with >= 2*TT cols (partition-0 scratch rows).
    ab, bb: Bufs [128, TT] f32 broadcast planes. sq_tmp: Buf [128,512] f32."""
    NCH = TT // 512
    for ch in range(NCH):
        c0 = 512 * ch
        bs, bq, bba, bbb = pb[0], pb[1], pb[2], pb[3]
        rs, rq, rba, rbb = psr[0], psr[1], psr[2], psr[3]
        for d in range(DT):
            vb.op("pe", (lambda e, d=d, c0=c0, bs=bs: e.matmul(
                out=bs[0:1, 0:512], lhsT=ones_col,
                rhs=x_buf.ap(d * TT + c0, d * TT + c0 + 512),
                start=(d == 0), stop=(d == DT - 1))),
                reads=x_buf.reg(d * TT + c0, d * TT + c0 + 512) + ones_col_reg,
                writes=[rs], tag=f"{prefix}sx{ch}")
        for d in range(DT):
            vb.op("act", (lambda e, d=d, c0=c0: e.activation(
                out=sq_tmp.ap(0, 512), in_=x_buf.ap(d * TT + c0, d * TT + c0 + 512),
                func=AF.Square)),
                reads=x_buf.reg(d * TT + c0, d * TT + c0 + 512),
                writes=sq_tmp.reg(0, 512), tag=f"{prefix}sq{ch}_{d}")
            vb.op("pe", (lambda e, d=d, bq=bq: e.matmul(
                out=bq[0:1, 0:512], lhsT=ones_col, rhs=sq_tmp.ap(0, 512),
                start=(d == 0), stop=(d == DT - 1))),
                reads=sq_tmp.reg(0, 512) + ones_col_reg,
                writes=[rq], tag=f"{prefix}sxx{ch}")
        mu_sl = (rows.prt(0, 1, c0, c0 + 512), rows.reg(c0, c0 + 512))
        va_sl = (rows.prt(0, 1, TT + c0, TT + c0 + 512), rows.reg(TT + c0, TT + c0 + 512))
        m_rg = mu_sl[1]; v_rg = va_sl[1]
        m_ap, m_rg = mu_sl
        v_ap, v_rg = va_sl
        vb.op("dve", (lambda e, m_ap=m_ap, bs=bs: e.tensor_scalar(
            out=m_ap, in0=bs[0:1, 0:512], scalar1=1.0 / D, scalar2=None,
            op0=ALU.mult)),
            reads=[rs], writes=m_rg, tag=f"{prefix}mu{ch}")
        vb.op("dve", (lambda e, m_ap=m_ap, v_ap=v_ap: e.tensor_tensor(
            out=v_ap, in0=m_ap, in1=m_ap, op=ALU.mult)),
            reads=m_rg, writes=v_rg, tag=f"{prefix}m2{ch}")
        vb.op("dve", (lambda e, v_ap=v_ap, bq=bq: e.scalar_tensor_tensor(
            out=v_ap, in0=bq[0:1, 0:512], scalar=1.0 / D, in1=v_ap,
            op0=ALU.mult, op1=ALU.subtract)),
            reads=[rq] + v_rg, writes=v_rg, tag=f"{prefix}var{ch}")
        vb.op("act", (lambda e, v_ap=v_ap: e.activation(
            out=v_ap, in_=v_ap, func=AF.Sqrt, bias=eps_ap[0:1, :])),
            reads=v_rg + eps_rg, writes=v_rg, tag=f"{prefix}std{ch}")
        vb.op("dve", (lambda e, v_ap=v_ap: e.reciprocal(out=v_ap, in_=v_ap)),
            reads=v_rg, writes=v_rg, tag=f"{prefix}rstd{ch}")
        vb.op("dve", (lambda e, m_ap=m_ap, v_ap=v_ap: e.scalar_tensor_tensor(
            out=m_ap, in0=m_ap, scalar=-1.0, in1=v_ap,
            op0=ALU.mult, op1=ALU.mult)),
            reads=m_rg + v_rg, writes=m_rg, tag=f"{prefix}nm{ch}")
        vb.op("pe", (lambda e, v_ap=v_ap, bba=bba: e.matmul(
            out=bba[:, 0:512], lhsT=ones_row, rhs=v_ap, start=True, stop=True)),
            reads=v_rg + ones_row_reg, writes=[rba], tag=f"{prefix}bca{ch}")
        vb.op("dve", (lambda e, c0=c0, bba=bba: e.tensor_copy(
            out=ab.ap(c0, c0 + 512), in_=bba[:, 0:512])),
            reads=[rba], writes=ab.reg(c0, c0 + 512), tag=f"{prefix}eva{ch}")
        vb.op("pe", (lambda e, m_ap=m_ap, bbb=bbb: e.matmul(
            out=bbb[:, 0:512], lhsT=ones_row, rhs=m_ap, start=True, stop=True)),
            reads=m_rg + ones_row_reg, writes=[rbb], tag=f"{prefix}bcb{ch}")
        vb.op("dve", (lambda e, c0=c0, bbb=bbb: e.tensor_copy(
            out=bb.ap(c0, c0 + 512), in_=bbb[:, 0:512])),
            reads=[rbb], writes=bb.reg(c0, c0 + 512), tag=f"{prefix}evb{ch}")

    for d in range(DT):
        yr = y_buf.reg(d * TT, (d + 1) * TT)
        vb.op("dve", (lambda e, d=d: e.tensor_tensor(
            out=y_buf.ap(d * TT, (d + 1) * TT), in0=x_buf.ap(d * TT, (d + 1) * TT),
            in1=ab.ap(), op=ALU.mult)),
            reads=x_buf.reg(d * TT, (d + 1) * TT) + ab.reg(), writes=yr,
            tag=f"{prefix}ap1_{d}")
        if g_aps is None:
            vb.op("dve", (lambda e, d=d: e.tensor_tensor(
                out=y_buf.ap(d * TT, (d + 1) * TT), in0=y_buf.ap(d * TT, (d + 1) * TT),
                in1=bb.ap(), op=ALU.add)),
                reads=yr + bb.reg(), writes=yr, tag=f"{prefix}ap2_{d}")
        else:
            vb.op("dve", (lambda e, d=d: e.tensor_tensor(
                out=y_buf.ap(d * TT, (d + 1) * TT), in0=y_buf.ap(d * TT, (d + 1) * TT),
                in1=bb.ap(), op=ALU.add)),
                reads=yr + bb.reg(), writes=yr, tag=f"{prefix}ap2_{d}")
            vb.op("dve", (lambda e, d=d: e.tensor_scalar(
                out=y_buf.ap(d * TT, (d + 1) * TT), in0=y_buf.ap(d * TT, (d + 1) * TT),
                scalar1=g_aps[d], scalar2=b_aps[d], op0=ALU.mult, op1=ALU.add)),
                reads=yr + list(gb_regs), writes=yr, tag=f"{prefix}ap3_{d}")



def build_ln_program(TT=T, eps=1e-5):
    nc = bass.Bass("TRN2", target_bir_lowering=False, debug=False)
    x_d = nc.dram_tensor("x_fm", [D, TT], F32, kind="ExternalInput").ap()
    y_d = nc.dram_tensor("y_fm", [D, TT], F32, kind="ExternalOutput").ap()
    ones_d = nc.dram_tensor("cst_ones", [P, P], F32, kind="ExternalInput").ap()

    es = ExitStack()
    with es:
        af = Arena(nc, es, "a_f32", 2 * (DT * TT) + 2 * TT + 2 * TT + P + 512 + 16, F32)
        x = af.alloc(DT * TT, "x")
        y = af.alloc(DT * TT, "y")
        ab = af.alloc(TT, "Ab")
        bb = af.alloc(TT, "Bb")
        rows = af.alloc(2 * TT, "rows")
        ones = af.alloc(P, "ones")
        sq = af.alloc(512, "sq")
        pb = [es.enter_context(nc.psum_tensor(f"ps{i}", [P, 512], F32)) for i in range(4)]
        psr = [("ps", i) for i in range(4)]

        vb = VB(nc)
        cst = Consts(vb, af)
        vb.dma("sp", lambda e: e.dma_start(out=x.ap(), in_=x_d),
               reads=[("dram", "x")], writes=x.reg(), tag="ld_x")
        vb.dma("sp", lambda e: e.dma_start(out=ones.ap(), in_=ones_d),
               reads=[("dram", "ones")], writes=ones.reg(), tag="ld_ones")

        emit_ln_fm(vb, pb, psr, x, y, rows, ab, bb,
                   ones_col=ones.prt(0, P, 0, 1), ones_col_reg=ones.reg(0, 1),
                   ones_row=ones.prt(0, 1, 0, P), ones_row_reg=ones.reg(0, P),
                   TT=TT, eps=eps, consts=cst, sq_tmp=sq)

        vb.dma("sp", lambda e: e.dma_start(out=y_d, in_=y.ap()),
               reads=y.reg(), writes=[("dram", "y")], tag="st_y")
        vb.finalize_and_emit(es, final_queue="pool")
    return nc


def _ln_np(x, axis=-1, eps=1e-5):
    m = x.mean(axis, keepdims=True)
    v = x.var(axis, keepdims=True)
    return (x - m) / np.sqrt(v + eps)


def _cst_ones():
    return np.ones((P, P), np.float32)



NTT = 19
TE = NTT * P
HALO = 384
QT0 = 2
NQT = NTT - QT0
XMC = NQT * P
DIP = 2816
NPT = DIP // P
W1C = 2 * DIP
WIN = 384
QK_SCALE = 8.0


def build_l0layer_program(dbg=False, it_dbg=5, h_dbg=3, stage=99, sub=9, rsub=9, ntl=NTT):
    nc = bass.Bass("TRN2", target_bir_lowering=False, debug=False)
    xe_d = nc.dram_tensor("xe_fm", [D, TE], F32, kind="ExternalInput").ap()
    wqkv_d = nc.dram_tensor("w_qkv", [D, 3 * D], F32, kind="ExternalInput").ap()
    wo_d = nc.dram_tensor("w_o", [D, D], F32, kind="ExternalInput").ap()
    w1_d = nc.dram_tensor("w1p", [D, W1C], F32, kind="ExternalInput").ap()
    w2_d = nc.dram_tensor("w2p", [DIP, D], F32, kind="ExternalInput").ap()
    qs_d = nc.dram_tensor("qscale", [1, 64], F32, kind="ExternalInput").ap()
    ks_d = nc.dram_tensor("kscale", [1, 64], F32, kind="ExternalInput").ap()
    lng_d = nc.dram_tensor("ln_g", [P, DT], F32, kind="ExternalInput").ap()
    lnb_d = nc.dram_tensor("ln_b", [P, DT], F32, kind="ExternalInput").ap()
    b1_d = nc.dram_tensor("b1p", [P, 2 * NPT], F32, kind="ExternalInput").ap()
    b2_d = nc.dram_tensor("b2v", [P, DT], F32, kind="ExternalInput").ap()
    tbl_d = nc.dram_tensor("tbl", [4, TE, 64], F32, kind="ExternalInput").ap()
    msk_d = nc.dram_tensor("masks", [NTT, P, WIN], F32, kind="ExternalInput").ap()
    eye_d = nc.dram_tensor("cst_eye", [P, P], F32, kind="ExternalInput").ap()
    ones_d = nc.dram_tensor("cst_ones", [P, P], F32, kind="ExternalInput").ap()
    xo_d = nc.dram_tensor("xo_fm", [D, T], F32, kind="ExternalOutput").ap()
    if dbg:
        dbg_qn = nc.dram_tensor("dbg_qn", [P, D], BF16, kind="ExternalOutput").ap()
        dbg_kn = nc.dram_tensor("dbg_kn", [P, D], BF16, kind="ExternalOutput").ap()
        dbg_prm = nc.dram_tensor("dbg_prm", [P, WIN], BF16, kind="ExternalOutput").ap()
        dbg_aot = nc.dram_tensor("dbg_aot", [P, D], BF16, kind="ExternalOutput").ap()
        dbg_xmid = nc.dram_tensor("dbg_xmid", [P, DT * XMC], F32, kind="ExternalOutput").ap()
        dbg_xln = nc.dram_tensor("dbg_xln", [P, DT * T], BF16, kind="ExternalOutput").ap()
        dbg_v = nc.dram_tensor("dbg_v", [P, 1040], BF16, kind="ExternalOutput").ap()
        dbg_h = nc.dram_tensor("dbg_h", [P, NPT * 1024], BF16, kind="ExternalOutput").ap()
        dbg_w2 = nc.dram_tensor("dbg_w2", [P, NPT * P], BF16, kind="ExternalOutput").ap()
        dbg_ps = nc.dram_tensor("dbg_ps", [P, 512], F32, kind="ExternalOutput").ap()

    es = ExitStack()
    with es:
        A = Arena(nc, es, "arn", 103 * 1024, BF16)
        vb = VB(nc)

        def falloc(cols, tag):
            b = A.alloc(2 * cols, tag)
            return b

        def fap(buf, c0=0, c1=None):
            c1 = buf.cols // 2 if c1 is None else c1
            return buf.arena[:, buf.off + 2 * c0: buf.off + 2 * c1].bitcast(F32)

        def fprt(buf, p0, p1, c0=0, c1=None):
            c1 = buf.cols // 2 if c1 is None else c1
            return buf.arena[p0:p1, buf.off + 2 * c0: buf.off + 2 * c1].bitcast(F32)

        def freg(buf, c0=0, c1=None):
            c1 = buf.cols // 2 if c1 is None else c1
            return buf.reg(2 * c0, 2 * c1)

        cstb = falloc(16, "consts")
        ones = falloc(P, "ones")
        idn = A.alloc(P, "idn")
        lng = falloc(DT, "lng")
        lnb = falloc(DT, "lnb")
        b1c = falloc(2 * NPT, "b1")
        b2c = falloc(DT, "b2")
        qsb = A.alloc(64, "qsb")
        ksb = A.alloc(64, "ksb")
        teff = [A.alloc(NTT * 64, f"tble{k}") for k in range(4)]
        wo = A.alloc(DT * D, "wo")
        mskb = [A.alloc(WIN, f"msk{s}") for s in range(2)]
        xmid = falloc(DT * XMC, "xmid")

        cst_map = {}

        def cst(value):
            value = float(value)
            if value not in cst_map:
                i = len(cst_map)
                assert i < 16
                ap = fap(cstb, i, i + 1)
                rg = freg(cstb, i, i + 1)
                vb.op("dve", (lambda e, ap=ap, value=value: e.memset(ap, value)),
                      reads=[], writes=rg, tag=f"cst{value}")
                cst_map[value] = (ap, rg)
            return cst_map[value]

        pbf = [es.enter_context(nc.psum_tensor(f"pf{i}", [P, 512], F32)) for i in range(6)]
        pbb = [es.enter_context(nc.psum_tensor(f"pb{i}", [P, 1024], BF16)) for i in range(2)]
        PSF = [("psf", i) for i in range(6)]
        PSB = [("psb", i) for i in range(2)]
        rot = {"mm": 0, "sim": 0, "av": 0, "tp": 0}

        def mmbank():
            i = rot["mm"] % 2; rot["mm"] += 1
            return pbf[i], PSF[i]

        def simbank():
            i = 2 + rot["sim"] % 2; rot["sim"] += 1
            return pbf[i], PSF[i]

        def avbank():
            i = 4 + rot["av"] % 2; rot["av"] += 1
            return pbf[i], PSF[i]

        def tpbank():
            i = rot["tp"] % 2; rot["tp"] += 1
            return pbb[i], PSB[i]

        for dst, srcd, tg in ((ones, ones_d, "ones"), (lng, lng_d, "lng"),
                              (lnb, lnb_d, "lnb"), (b1c, b1_d, "b1"), (b2c, b2_d, "b2")):
            vb.dma("sp", (lambda e, dst=dst, srcd=srcd: e.dma_start(out=fap(dst), in_=srcd)),
                   writes=freg(dst), tag=f"ld_{tg}")
        vb.dma("pool", (lambda e: e.dma_start(out=idn.ap(), in_=eye_d)),
               writes=idn.reg(), tag="ld_idn")

        A.push()
        tbl_raw = [A.alloc(NTT * 64, f"tblr{k}") for k in range(4)]
        wqkv = A.alloc(DT * 3 * D, "wqkv")
        knf = A.alloc(4 * DT * P, "knf")
        vbuf = A.alloc(4 * 16 * 65, "vbuf")
        xf = [falloc(DT * 129, f"xf{s}") for s in range(3)]
        xbf = [A.alloc(DT * 129, f"xbf{s}") for s in range(2)]
        qnst = A.alloc(D, "qnst")
        knst = A.alloc(D, "knst")
        qnf = A.alloc(DT * P, "qnf")
        u1 = A.alloc(512, "u1")
        u2 = A.alloc(512, "u2")
        sqt = falloc(512, "sq")
        nrm = falloc(48, "nrm")
        prb = A.alloc(WIN, "prb")
        prm = A.alloc(WIN, "prm")
        ptb = A.alloc(WIN, "ptb")
        aot = A.alloc(D, "aot")
        aof = A.alloc(DT * P, "aof")
        rcp = falloc(4, "rcp")
        qsr = falloc(64, "qsr")
        ksr = falloc(64, "ksr")

        for k in range(4):
            vb.dma("pool", (lambda e, k=k: e.dma_start(
                out=tbl_raw[k].ap().rearrange("p (tt d) -> p tt d", d=64),
                in_=tbl_d[k].rearrange("(tt p) d -> p tt d", p=P))),
                writes=tbl_raw[k].reg(), tag=f"ld_tbl{k}")
        for oc in range(6):
            vb.dma("pool", (lambda e, oc=oc: e.dma_start(
                out=wqkv.ap().rearrange("p (d o) -> p d o", o=3 * D)[:, :, 512 * oc: 512 * oc + 512],
                in_=wqkv_d[:, 512 * oc: 512 * oc + 512].rearrange("(dt p) o -> p dt o", p=P))),
                writes=wqkv.reg(), tag=f"ld_wqkv{oc}")
        vb.dma("pool", (lambda e: e.dma_start(
            out=wo.ap().rearrange("p (kt o) -> p kt o", kt=DT),
            in_=wo_d.rearrange("(kt p) o -> p kt o", p=P))),
            writes=wo.reg(), tag="ld_wo")
        vb.dma("sp", lambda e: e.dma_start(out=fprt(qsr, 0, 1), in_=qs_d),
               writes=freg(qsr), tag="ld_qs")
        vb.dma("sp", lambda e: e.dma_start(out=fprt(ksr, 0, 1), in_=ks_d),
               writes=freg(ksr), tag="ld_ks")
        for name, srcb, dst in (("q", qsr, qsb), ("k", ksr, ksb)):
            bank, breg = mmbank()
            vb.op("pe", (lambda e, srcb=srcb, bank=bank: e.matmul(
                out=bank[:, 0:64], lhsT=fprt(ones, 0, 1), rhs=fprt(srcb, 0, 1),
                start=True, stop=True)),
                reads=freg(srcb) + freg(ones), writes=[breg], tag=f"bc_{name}s")
            vb.op("dve", (lambda e, dst=dst, bank=bank: e.tensor_copy(
                out=dst.ap(), in_=bank[:, 0:64])),
                reads=[breg], writes=dst.reg(), tag=f"ev_{name}s")
        for k, (scl, rotd) in enumerate(((qsb, False), (qsb, True), (ksb, False), (ksb, True))):
            for tt in range(NTT):
                c0 = tt * 64
                if not rotd:
                    vb.op("dve", (lambda e, k=k, c0=c0, scl=scl: e.tensor_tensor(
                        out=teff[k].ap(c0, c0 + 64), in0=tbl_raw[k].ap(c0, c0 + 64),
                        in1=scl.ap(), op=ALU.mult)),
                        reads=tbl_raw[k].reg(c0, c0 + 64) + scl.reg(),
                        writes=teff[k].reg(c0, c0 + 64), tag=f"te{k}_{tt}")
                else:
                    vb.op("dve", (lambda e, k=k, c0=c0, scl=scl: e.tensor_tensor(
                        out=teff[k].ap(c0, c0 + 32), in0=tbl_raw[k].ap(c0, c0 + 32),
                        in1=scl.ap(32, 64), op=ALU.mult)),
                        reads=tbl_raw[k].reg(c0, c0 + 64) + scl.reg(),
                        writes=teff[k].reg(c0, c0 + 64), tag=f"te{k}a_{tt}")
                    vb.op("dve", (lambda e, k=k, c0=c0, scl=scl: e.tensor_tensor(
                        out=teff[k].ap(c0 + 32, c0 + 64), in0=tbl_raw[k].ap(c0 + 32, c0 + 64),
                        in1=scl.ap(0, 32), op=ALU.mult)),
                        reads=tbl_raw[k].reg(c0, c0 + 64) + scl.reg(),
                        writes=teff[k].reg(c0, c0 + 64), tag=f"te{k}b_{tt}")

        sc8, sc8_rg = cst(QK_SCALE)

        def emit_rotary(it, oc, bank, breg, is_q, rsub=rsub):
            st_buf = qnst if is_q else knst
            base = (oc % 2) * 512
            ta = teff[0] if is_q else teff[2]
            tb = teff[1] if is_q else teff[3]
            tc0 = it * 64
            vb.op("dve", (lambda e, bank=bank, st_buf=st_buf, base=base: e.tensor_copy(
                out=st_buf.ap(base, base + 512), in_=bank[:, 0:512])),
                reads=[breg], writes=st_buf.reg(base, base + 512), tag=f"r{it}_{oc}raw")
            qb = st_buf.ap(base, base + 512)
            qb3 = qb.rearrange("p (h d) -> p h d", d=64)
            qbr = st_buf.reg(base, base + 512)
            if rsub < 2:
                return
            vb.op("act", (lambda e, bank=bank: e.activation(
                out=fap(sqt), in_=bank[:, 0:512], func=AF.Square)),
                reads=[breg], writes=freg(sqt), tag=f"r{it}_{oc}sq")
            if rsub == 12:
                return
            nA = fap(nrm, 0, 8)
            nAr = freg(nrm, 0, 8)
            vb.op("dve", (lambda e, nA=nA: e.reduce_sum(
                out=nA, in_=fap(sqt).rearrange("p (h d) -> p h d", d=64), axis=AX.X)),
                reads=freg(sqt), writes=nAr, tag=f"r{it}_{oc}ss")
            if rsub == 13:
                return
            vb.op("act", (lambda e, nA=nA: e.activation(out=nA, in_=nA, func=AF.Sqrt)),
                reads=nAr, writes=nAr, tag=f"r{it}_{oc}nr")
            if rsub == 14:
                return
            vb.op("dve", (lambda e, nA=nA: e.tensor_scalar(
                out=nA, in0=nA, scalar1=1e-12, scalar2=None, op0=ALU.max)),
                reads=nAr, writes=nAr, tag=f"r{it}_{oc}cl")
            vb.op("dve", (lambda e, nA=nA: e.reciprocal(out=nA, in_=nA)),
                reads=nAr, writes=nAr, tag=f"r{it}_{oc}rc")
            if rsub < 3:
                return
            aab = ta.ap(tc0, tc0 + 64).unsqueeze(1).broadcast_to([P, 8, 64])
            vb.op("dve", (lambda e, qb3=qb3, aab=aab: e.tensor_tensor(
                out=u1.ap().rearrange("p (h d) -> p h d", d=64), in0=qb3, in1=aab,
                op=ALU.mult)),
                reads=qbr + ta.reg(tc0, tc0 + 64), writes=u1.reg(), tag=f"r{it}_{oc}u1")
            if rsub < 4:
                return
            bb1 = tb.ap(tc0, tc0 + 32).unsqueeze(1).broadcast_to([P, 8, 32])
            bb2 = tb.ap(tc0 + 32, tc0 + 64).unsqueeze(1).broadcast_to([P, 8, 32])
            vb.op("dve", (lambda e, qb3=qb3, bb1=bb1: e.tensor_tensor(
                out=u2.ap().rearrange("p (h d) -> p h d", d=64)[:, :, 0:32],
                in0=qb3[:, :, 32:64], in1=bb1, op=ALU.mult)),
                reads=qbr + tb.reg(tc0, tc0 + 64), writes=u2.reg(), tag=f"r{it}_{oc}u2a")
            vb.op("dve", (lambda e, qb3=qb3, bb2=bb2: e.tensor_tensor(
                out=u2.ap().rearrange("p (h d) -> p h d", d=64)[:, :, 32:64],
                in0=qb3[:, :, 0:32], in1=bb2, op=ALU.mult)),
                reads=qbr + tb.reg(tc0, tc0 + 64), writes=u2.reg(), tag=f"r{it}_{oc}u2b")
            if rsub < 5:
                return
            vb.op("dve", (lambda e: e.tensor_tensor(
                out=u1.ap(), in0=u1.ap(), in1=u2.ap(), op=ALU.add)),
                reads=u1.reg() + u2.reg(), writes=u1.reg(), tag=f"r{it}_{oc}add")
            nb = nA.unsqueeze(2).broadcast_to([P, 8, 64])
            vb.op("dve", (lambda e, qb3=qb3, nb=nb: e.tensor_tensor(
                out=qb3, in0=u1.ap().rearrange("p (h d) -> p h d", d=64), in1=nb,
                op=ALU.mult)),
                reads=u1.reg() + nAr, writes=qbr, tag=f"r{it}_{oc}fin")

        for it in range(ntl if stage >= 1 else 0):
            t0 = it * P
            sx = it % 3
            sbi = it % 2
            sl = it % 4
            if it == 0:
                vb.dma("sp", (lambda e, sx=sx: e.dma_start(
                    out=fap(xf[sx]).rearrange("p (dt c) -> p dt c", c=129)[:, :, 1:129],
                    in_=xe_d[:, 0:128].rearrange("(dt p) c -> p dt c", p=P))),
                    reads=[("dram", "xe")], writes=freg(xf[sx]), tag="ldx0")
                vb.op("dve", (lambda e, sx=sx: e.tensor_copy(
                    out=fap(xf[sx]).rearrange("p (dt c) -> p dt c", c=129)[:, :, 0:1],
                    in_=fap(xf[sx]).rearrange("p (dt c) -> p dt c", c=129)[:, :, 1:2])),
                    reads=freg(xf[sx]), writes=freg(xf[sx]), tag="fix0")
            else:
                vb.dma("sp", (lambda e, sx=sx, t0=t0: e.dma_start(
                    out=fap(xf[sx]).rearrange("p (dt c) -> p dt c", c=129),
                    in_=xe_d[:, t0 - 1: t0 + 128].rearrange("(dt p) c -> p dt c", p=P))),
                    reads=[("dram", "xe")], writes=freg(xf[sx]), tag=f"ldx{it}")
            vb.op("dve", (lambda e, sx=sx, sbi=sbi: e.tensor_copy(
                out=xbf[sbi].ap(), in_=fap(xf[sx]))),
                reads=freg(xf[sx]), writes=xbf[sbi].reg(), tag=f"cx{it}")

            for oc in range(6):
                if oc < 2 and it < QT0:
                    continue
                if sub < 2:
                    continue
                bank, breg = mmbank()
                for dt in range(DT):
                    cofs = dt * 129 + (0 if dt >= 4 else 1)
                    wc0 = dt * 3 * D + 512 * oc
                    vb.op("pe", (lambda e, bank=bank, sbi=sbi, cofs=cofs, wc0=wc0, dt=dt:
                          e.matmul(out=bank[:, 0:512],
                                   lhsT=xbf[sbi].ap(cofs, cofs + 128),
                                   rhs=wqkv.ap(wc0, wc0 + 512),
                                   start=(dt == 0), stop=(dt == DT - 1))),
                          reads=xbf[sbi].reg(cofs, cofs + 128) + wqkv.reg(wc0, wc0 + 512),
                          writes=[breg], tag=f"qkv{it}_{oc}_{dt}")
                if oc < 2:
                    if sub >= 4:
                        emit_rotary(it, oc, bank, breg, is_q=True)
                elif oc < 4:
                    if sub >= 4:
                        emit_rotary(it, oc, bank, breg, is_q=False)
                elif sub >= 3:
                    h0 = 8 * (oc - 4)
                    vc0 = sl * 1040 + h0 * 65
                    vb.op("dve", (lambda e, bank=bank, vc0=vc0: e.tensor_copy(
                        out=vbuf.ap(vc0, vc0 + 8 * 65).rearrange("p (h d) -> p h d", d=65)[:, :, 0:64],
                        in_=bank[:, 0:512].rearrange("p (h d) -> p h d", d=64))),
                        reads=[breg], writes=vbuf.reg(vc0, vc0 + 8 * 65),
                        tag=f"vev{it}_{oc}")
            if sub >= 3:
                vb.op("dve", (lambda e, sl=sl: e.memset(
                    vbuf.ap(sl * 1040, sl * 1040 + 1040).rearrange("p (h d) -> p h d", d=65)[:, :, 64:65], 1.0)),
                    reads=[], writes=vbuf.reg(sl * 1040, sl * 1040 + 1040), tag=f"vone{it}")

            for hp in range(DT if sub >= 5 else 0):
                bank, breg = tpbank()
                vb.op("pe", (lambda e, bank=bank, hp=hp: e.transpose(
                    out=bank[:, 0:128], in_=knst.ap(hp * 128, hp * 128 + 128),
                    identity=idn.ap())),
                    reads=knst.reg(hp * 128, hp * 128 + 128) + idn.reg(),
                    writes=[breg], tag=f"ktp{it}_{hp}")
                kc0 = sl * D + hp * 128
                vb.op("dve", (lambda e, bank=bank, kc0=kc0: e.tensor_copy(
                    out=knf.ap(kc0, kc0 + 128), in_=bank[:, 0:128])),
                    reads=[breg], writes=knf.reg(kc0, kc0 + 128), tag=f"kte{it}_{hp}")

            if it < QT0 or stage < 2:
                continue

            ms = mskb[it % 2]
            vb.dma("pool", (lambda e, ms=ms, it=it: e.dma_start(
                out=ms.ap(), in_=msk_d[it])),
                reads=[("dram", "masks")], writes=ms.reg(), tag=f"ldm{it}")
            for hp in range(DT):
                bank, breg = tpbank()
                vb.op("pe", (lambda e, bank=bank, hp=hp: e.transpose(
                    out=bank[:, 0:128], in_=qnst.ap(hp * 128, hp * 128 + 128),
                    identity=idn.ap())),
                    reads=qnst.reg(hp * 128, hp * 128 + 128) + idn.reg(),
                    writes=[breg], tag=f"qtp{it}_{hp}")
                vb.op("dve", (lambda e, bank=bank, hp=hp: e.tensor_copy(
                    out=qnf.ap(hp * 128, hp * 128 + 128), in_=bank[:, 0:128])),
                    reads=[breg], writes=qnf.reg(hp * 128, hp * 128 + 128),
                    tag=f"qte{it}_{hp}")

            for h in range(16):
                hp, r0 = h // 2, (h % 2) * 64
                sbank, sbreg = simbank()
                for kt in range(3):
                    ksl = (it - 2 + kt) % 4
                    kc0 = ksl * D + hp * 128
                    vb.op("pe", (lambda e, sbank=sbank, hp=hp, r0=r0, kt=kt, kc0=kc0:
                          e.matmul(out=sbank[:, 128 * kt: 128 * kt + 128],
                                   lhsT=qnf.prt(r0, r0 + 64, hp * 128, hp * 128 + 128),
                                   rhs=knf.prt(r0, r0 + 64, kc0, kc0 + 128),
                                   start=True, stop=True)),
                          reads=qnf.reg(hp * 128, hp * 128 + 128) + knf.reg(kc0, kc0 + 128),
                          writes=[sbreg], tag=f"sim{it}_{h}_{kt}")
                nM = fap(nrm, 32, 33)
                nMr = freg(nrm, 32, 33)
                vb.op("dve", (lambda e, sbank=sbank, nM=nM: e.reduce_max(
                    out=nM, in_=sbank[:, 0:WIN], axis=AX.X, negate=True)),
                    reads=[sbreg], writes=nMr, tag=f"mx{it}_{h}")
                nM8 = fap(nrm, 33, 34)
                nM8r = freg(nrm, 33, 34)
                vb.op("dve", (lambda e, nM=nM, nM8=nM8: e.tensor_scalar(
                    out=nM8, in0=nM, scalar1=QK_SCALE, scalar2=None, op0=ALU.mult)),
                    reads=nMr, writes=nM8r, tag=f"mx8{it}_{h}")
                vb.op("act", (lambda e, sbank=sbank, nM8=nM8: e.activation(
                    out=prb.ap(), in_=sbank[:, 0:WIN], func=AF.Exp, bias=nM8,
                    scale=sc8)),
                    reads=[sbreg] + nM8r + sc8_rg, writes=prb.reg(), tag=f"ex{it}_{h}")
                vb.op("dve", (lambda e, ms=ms: e.tensor_tensor(
                    out=prm.ap(), in0=prb.ap(), in1=ms.ap(), op=ALU.mult)),
                    reads=prb.reg() + ms.reg(), writes=prm.reg(), tag=f"mk{it}_{h}")
                if dbg and it == it_dbg and h == h_dbg:
                    vb.dma("sp", (lambda e: e.dma_start(out=dbg_prm, in_=prm.ap())),
                           reads=prm.reg(), writes=[("dram", "dprm")], tag="dbgprm")
                abank, abreg = avbank()
                for kt in range(3):
                    tbank, tbreg = tpbank()
                    vb.op("pe", (lambda e, tbank=tbank, kt=kt: e.transpose(
                        out=tbank[:, 0:128], in_=prm.ap(128 * kt, 128 * kt + 128),
                        identity=idn.ap())),
                        reads=prm.reg() + idn.reg(), writes=[tbreg], tag=f"ptp{it}_{h}_{kt}")
                    vb.op("dve", (lambda e, tbank=tbank, kt=kt: e.tensor_copy(
                        out=ptb.ap(128 * kt, 128 * kt + 128), in_=tbank[:, 0:128])),
                        reads=[tbreg], writes=ptb.reg(128 * kt, 128 * kt + 128),
                        tag=f"pte{it}_{h}_{kt}")
                for kt in range(3):
                    ksl = (it - 2 + kt) % 4
                    vc0 = ksl * 1040 + h * 65
                    vb.op("pe", (lambda e, abank=abank, kt=kt, vc0=vc0:
                          e.matmul(out=abank[:, 0:65],
                                   lhsT=ptb.ap(128 * kt, 128 * kt + 128),
                                   rhs=vbuf.ap(vc0, vc0 + 65),
                                   start=(kt == 0), stop=(kt == 2))),
                          reads=ptb.reg(128 * kt, 128 * kt + 128) + vbuf.reg(vc0, vc0 + 65),
                          writes=[abreg], tag=f"av{it}_{h}_{kt}")
                rc = fap(rcp, 0, 1)
                rcr = freg(rcp, 0, 1)
                vb.op("dve", (lambda e, abank=abank, rc=rc: e.reciprocal(
                    out=rc, in_=abank[:, 64:65])),
                    reads=[abreg], writes=rcr, tag=f"rcd{it}_{h}")
                vb.op("dve", (lambda e, abank=abank, rc=rc, h=h: e.tensor_scalar(
                    out=aot.ap(h * 64, h * 64 + 64), in0=abank[:, 0:64],
                    scalar1=rc, scalar2=None, op0=ALU.mult)),
                    reads=[abreg] + rcr, writes=aot.reg(h * 64, h * 64 + 64),
                    tag=f"sca{it}_{h}")

            if stage < 3:
                continue
            if dbg and it == it_dbg:
                vb.dma("sp", (lambda e: e.dma_start(out=dbg_qn, in_=qnst.ap())),
                       reads=qnst.reg(), writes=[("dram", "dqn")], tag="dbgqn")
                vb.dma("sp", (lambda e: e.dma_start(out=dbg_kn, in_=knst.ap())),
                       reads=knst.reg(), writes=[("dram", "dkn")], tag="dbgkn")
                vb.dma("sp", (lambda e: e.dma_start(out=dbg_aot, in_=aot.ap())),
                       reads=aot.reg(), writes=[("dram", "daot")], tag="dbgaot")
                sl_ = it % 4
                vb.dma("sp", (lambda e, sl_=sl_: e.dma_start(
                    out=dbg_v, in_=vbuf.ap(sl_ * 1040, sl_ * 1040 + 1040))),
                       reads=vbuf.reg(sl_ * 1040, sl_ * 1040 + 1040),
                       writes=[("dram", "dv")], tag="dbgv")
            for dt in range(DT):
                bank, breg = tpbank()
                vb.op("pe", (lambda e, bank=bank, dt=dt: e.transpose(
                    out=bank[:, 0:128], in_=aot.ap(dt * 128, dt * 128 + 128),
                    identity=idn.ap())),
                    reads=aot.reg(dt * 128, dt * 128 + 128) + idn.reg(),
                    writes=[breg], tag=f"atp{it}_{dt}")
                vb.op("dve", (lambda e, bank=bank, dt=dt: e.tensor_copy(
                    out=aof.ap(dt * 128, dt * 128 + 128), in_=bank[:, 0:128])),
                    reads=[breg], writes=aof.reg(dt * 128, dt * 128 + 128),
                    tag=f"ate{it}_{dt}")
            xmc0 = (it - 2) * 128
            for ot in range(DT):
                bank, breg = mmbank()
                for kt in range(DT):
                    wc0 = kt * D + ot * 128
                    vb.op("pe", (lambda e, bank=bank, kt=kt, wc0=wc0: e.matmul(
                        out=bank[:, 0:128],
                        lhsT=wo.ap(wc0, wc0 + 128),
                        rhs=aof.ap(kt * 128, kt * 128 + 128),
                        start=(kt == 0), stop=(kt == DT - 1))),
                        reads=wo.reg(wc0, wc0 + 128) + aof.reg(kt * 128, kt * 128 + 128),
                        writes=[breg], tag=f"wog{it}_{ot}_{kt}")
                vb.op("dve", (lambda e, bank=bank, ot=ot, sx=sx, xmc0=xmc0:
                      e.scalar_tensor_tensor(
                          out=fap(xmid, ot * XMC + xmc0, ot * XMC + xmc0 + 128),
                          in0=bank[:, 0:128], scalar=1.0,
                          in1=fap(xf[sx], ot * 129 + 1, ot * 129 + 129),
                          op0=ALU.mult, op1=ALU.add)),
                      reads=[breg] + freg(xf[sx], ot * 129 + 1, ot * 129 + 129),
                      writes=freg(xmid, ot * XMC + xmc0, ot * XMC + xmc0 + 128),
                      tag=f"xm{it}_{ot}")

        if dbg and stage >= 3:
            vb.dma("sp", (lambda e: e.dma_start(out=dbg_xmid, in_=fap(xmid))),
                   reads=freg(xmid), writes=[("dram", "dxmid")], tag="dbgxmid")
        A.pop()
        if stage < 1:
            zstg0 = falloc(512, "zstg0")
            vb.op("dve", (lambda e: e.memset(fap(zstg0), 0.0)), writes=freg(zstg0), tag="zz0")
            for ot in range(DT):
                for tcq in range(4):
                    vb.dma("sp", (lambda e, ot=ot, tcq=tcq: e.dma_start(
                        out=xo_d[ot * P: ot * P + P, tcq * 512: tcq * 512 + 512],
                        in_=fap(zstg0))),
                        reads=freg(zstg0), writes=[("dram", "xo")], tag=f"zz{ot}_{tcq}")
            stats = vb.finalize_and_emit(es, final_queue="pool")
            return nc, stats
        if stage < 4:
            zstg = falloc(512, "zstg")
            vb.op("dve", (lambda e: e.memset(fap(zstg), 0.0)), writes=freg(zstg), tag="z0")
            for ot in range(DT):
                for tcq in range(4):
                    vb.dma("sp", (lambda e, ot=ot, tcq=tcq: e.dma_start(
                        out=xo_d[ot * P: ot * P + P, tcq * 512: tcq * 512 + 512],
                        in_=fap(zstg))),
                        reads=freg(zstg), writes=[("dram", "xo")], tag=f"z{ot}_{tcq}")
            stats = vb.finalize_and_emit(es, final_queue="pool")
            return nc, stats

        xln = A.alloc(DT * T, "xln")
        A.push()
        abp = falloc(T, "Ab")
        bbp = falloc(T, "Bb")
        rws = falloc(4 * XMC, "rws")
        rws2 = falloc(2 * T, "rws2")
        sqf = falloc(512, "sqf")
        eps5, eps5_rg = cst(1e-5)

        chunks = [(0, 512), (512, 1024), (1024, 1536), (1536, 2048), (2048, 2176)]
        for ci, (c0, c1) in enumerate(chunks):
            w = c1 - c0
            for half in range(2):
                bank, breg = pbf[half], PSF[half]
                qbank, qbreg = pbf[2 + half], PSF[2 + half]
                for dd in range(4):
                    dt = half * 4 + dd
                    xc0 = dt * XMC + c0
                    vb.op("pe", (lambda e, bank=bank, xc0=xc0, w=w, dd=dd: e.matmul(
                        out=bank[0:1, 0:w], lhsT=fprt(ones, 0, P, 0, 1),
                        rhs=fap(xmid, xc0, xc0 + w),
                        start=(dd == 0), stop=(dd == 3))),
                        reads=freg(xmid, xc0, xc0 + w) + freg(ones, 0, 1),
                        writes=[breg], tag=f"ls{half}_{ci}_{dd}")
                    vb.op("act", (lambda e, xc0=xc0, w=w: e.activation(
                        out=fap(sqf, 0, w), in_=fap(xmid, xc0, xc0 + w), func=AF.Square)),
                        reads=freg(xmid, xc0, xc0 + w), writes=freg(sqf), tag=f"lq{half}_{ci}_{dd}")
                    vb.op("pe", (lambda e, qbank=qbank, w=w, dd=dd: e.matmul(
                        out=qbank[0:1, 0:w], lhsT=fprt(ones, 0, P, 0, 1),
                        rhs=fap(sqf, 0, w),
                        start=(dd == 0), stop=(dd == 3))),
                        reads=freg(sqf) + freg(ones, 0, 1),
                        writes=[qbreg], tag=f"lqs{half}_{ci}_{dd}")
                po = half * XMC + c0
                qo = (2 + half) * XMC + c0
                vb.op("dve", (lambda e, bank=bank, po=po, w=w: e.tensor_copy(
                    out=fprt(rws, 0, 1, po, po + w), in_=bank[0:1, 0:w])),
                    reads=[breg], writes=freg(rws, po, po + w), tag=f"lse{half}_{ci}")
                vb.op("dve", (lambda e, qbank=qbank, qo=qo, w=w: e.tensor_copy(
                    out=fprt(rws, 0, 1, qo, qo + w), in_=qbank[0:1, 0:w])),
                    reads=[qbreg], writes=freg(rws, qo, qo + w), tag=f"lqe{half}_{ci}")

        for cc in range(4):
            c0 = cc * 512
            s1 = fprt(rws, 0, 1, 0 * XMC + 128 + c0, 0 * XMC + 128 + c0 + 512)
            s2 = fprt(rws, 0, 1, 1 * XMC + 127 + c0, 1 * XMC + 127 + c0 + 512)
            q1 = fprt(rws, 0, 1, 2 * XMC + 128 + c0, 2 * XMC + 128 + c0 + 512)
            q2 = fprt(rws, 0, 1, 3 * XMC + 127 + c0, 3 * XMC + 127 + c0 + 512)
            mu = fprt(rws2, 0, 1, c0, c0 + 512)
            va = fprt(rws2, 0, 1, T + c0, T + c0 + 512)
            mur = freg(rws2)
            rall = freg(rws)
            vb.op("dve", (lambda e, s1=s1, s2=s2, mu=mu: e.tensor_tensor(
                out=mu, in0=s1, in1=s2, op=ALU.add)),
                reads=rall, writes=mur, tag=f"cmu{cc}")
            vb.op("dve", (lambda e, mu=mu: e.tensor_scalar(
                out=mu, in0=mu, scalar1=1.0 / D, scalar2=None, op0=ALU.mult)),
                reads=mur, writes=mur, tag=f"cmu2{cc}")
            vb.op("dve", (lambda e, q1=q1, q2=q2, va=va: e.tensor_tensor(
                out=va, in0=q1, in1=q2, op=ALU.add)),
                reads=rall, writes=mur, tag=f"cva{cc}")
            vb.op("dve", (lambda e, mu=mu: e.scalar_tensor_tensor(
                out=fap(sqf, 0, 512)[0:1, :], in0=mu, scalar=1.0, in1=mu,
                op0=ALU.mult, op1=ALU.mult)),
                reads=mur, writes=freg(sqf), tag=f"cm2{cc}")
            vb.op("dve", (lambda e, va=va: e.scalar_tensor_tensor(
                out=va, in0=va, scalar=1.0 / D, in1=fap(sqf, 0, 512)[0:1, :],
                op0=ALU.mult, op1=ALU.subtract)),
                reads=mur + freg(sqf), writes=mur, tag=f"cv2{cc}")
            vb.op("act", (lambda e, va=va: e.activation(
                out=va, in_=va, func=AF.Sqrt, bias=eps5[0:1, :])),
                reads=mur + eps5_rg, writes=mur, tag=f"cstd{cc}")
            vb.op("dve", (lambda e, va=va: e.reciprocal(out=va, in_=va)),
                reads=mur, writes=mur, tag=f"crs{cc}")
            vb.op("dve", (lambda e, mu=mu, va=va: e.scalar_tensor_tensor(
                out=mu, in0=mu, scalar=-1.0, in1=va, op0=ALU.mult, op1=ALU.mult)),
                reads=mur, writes=mur, tag=f"cnm{cc}")
            bka, brga = pbf[4], PSF[4]
            bkb, brgb = pbf[5], PSF[5]
            vb.op("pe", (lambda e, bka=bka, va=va: e.matmul(
                out=bka[:, 0:512], lhsT=fprt(ones, 0, 1), rhs=va, start=True, stop=True)),
                reads=mur + freg(ones), writes=[brga], tag=f"cba{cc}")
            vb.op("dve", (lambda e, bka=bka, c0=c0: e.tensor_copy(
                out=fap(abp, c0, c0 + 512), in_=bka[:, 0:512])),
                reads=[brga], writes=freg(abp, c0, c0 + 512), tag=f"cbae{cc}")
            vb.op("pe", (lambda e, bkb=bkb, mu=mu: e.matmul(
                out=bkb[:, 0:512], lhsT=fprt(ones, 0, 1), rhs=mu, start=True, stop=True)),
                reads=mur + freg(ones), writes=[brgb], tag=f"cbb{cc}")
            vb.op("dve", (lambda e, bkb=bkb, c0=c0: e.tensor_copy(
                out=fap(bbp, c0, c0 + 512), in_=bkb[:, 0:512])),
                reads=[brgb], writes=freg(bbp, c0, c0 + 512), tag=f"cbbe{cc}")

        for dt in range(DT):
            srcc0 = dt * XMC + (128 if dt < 4 else 127)
            yr = xln.reg(dt * T, (dt + 1) * T)
            vb.op("dve", (lambda e, dt=dt, srcc0=srcc0: e.tensor_tensor(
                out=xln.ap(dt * T, (dt + 1) * T),
                in0=fap(xmid, srcc0, srcc0 + T), in1=fap(abp), op=ALU.mult)),
                reads=freg(xmid, srcc0, srcc0 + T) + freg(abp), writes=yr,
                tag=f"la1_{dt}")
            vb.op("dve", (lambda e, dt=dt: e.tensor_tensor(
                out=xln.ap(dt * T, (dt + 1) * T), in0=xln.ap(dt * T, (dt + 1) * T),
                in1=fap(bbp), op=ALU.add)),
                reads=yr + freg(bbp), writes=yr, tag=f"la2_{dt}")
            vb.op("dve", (lambda e, dt=dt: e.tensor_scalar(
                out=xln.ap(dt * T, (dt + 1) * T), in0=xln.ap(dt * T, (dt + 1) * T),
                scalar1=fap(lng, dt, dt + 1), scalar2=fap(lnb, dt, dt + 1),
                op0=ALU.mult, op1=ALU.add)),
                reads=yr + freg(lng) + freg(lnb), writes=yr, tag=f"la3_{dt}")
        if dbg:
            vb.dma("sp", (lambda e: e.dma_start(out=dbg_xln, in_=xln.ap())),
                   reads=xln.reg(), writes=[("dram", "dxln")], tag="dbgxln")
        A.pop()
        if stage < 5:
            zstg = falloc(512, "zstg")
            vb.op("dve", (lambda e: e.memset(fap(zstg), 0.0)), writes=freg(zstg), tag="z0")
            for ot in range(DT):
                for tcq in range(4):
                    vb.dma("sp", (lambda e, ot=ot, tcq=tcq: e.dma_start(
                        out=xo_d[ot * P: ot * P + P, tcq * 512: tcq * 512 + 512],
                        in_=fap(zstg))),
                        reads=freg(zstg), writes=[("dram", "xo")], tag=f"z{ot}_{tcq}")
            stats = vb.finalize_and_emit(es, final_queue="pool")
            return nc, stats
        hbuf = A.alloc(NPT * 1024, "hbuf")
        glt = A.alloc(512, "glt")
        w1b = [A.alloc(DT * P, f"w1b{s}") for s in range(2)]
        w2b = [A.alloc(NPT * P, f"w2b{s}") for s in range(2)]
        stg = [falloc(512, f"stg{s}") for s in range(2)]

        for half in range(2):
            th0 = half * 1024
            for pt in range(NPT):
                for m in range(2):
                    ot = 2 * pt + m
                    wb = w1b[(2 * pt + m) % 2]
                    vb.dma("pool", (lambda e, wb=wb, ot=ot: e.dma_start(
                        out=wb.ap().rearrange("p (dt o) -> p dt o", o=P),
                        in_=w1_d[:, ot * P: ot * P + P].rearrange("(dt p) o -> p dt o", p=P))),
                        reads=[("dram", "w1")], writes=wb.reg(), tag=f"lw1_{half}_{ot}")
                for tc in range(2):
                    c0 = th0 + tc * 512
                    banka, brega = pbf[2 * (tc % 2)], PSF[2 * (tc % 2)]
                    bankg, bregg = pbf[2 * (tc % 2) + 1], PSF[2 * (tc % 2) + 1]
                    for m, (bank, breg) in ((0, (banka, brega)), (1, (bankg, bregg))):
                        wb = w1b[(2 * pt + m) % 2]
                        for dt in range(DT):
                            vb.op("pe", (lambda e, bank=bank, wb=wb, dt=dt, c0=c0:
                                  e.matmul(out=bank[:, 0:512],
                                           lhsT=wb.ap(dt * P, dt * P + P),
                                           rhs=xln.ap(dt * T + c0, dt * T + c0 + 512),
                                           start=(dt == 0), stop=(dt == DT - 1))),
                                  reads=wb.reg(dt * P, dt * P + P) + xln.reg(dt * T + c0, dt * T + c0 + 512),
                                  writes=[breg], tag=f"w1m{half}_{pt}_{m}_{tc}")
                    hc0 = pt * 1024 + tc * 512
                    vb.op("act", (lambda e, bankg=bankg, pt=pt: e.activation(
                        out=glt.ap(), in_=bankg[:, 0:512], func=AF.Gelu,
                        bias=fap(b1c, 2 * pt + 1, 2 * pt + 2))),
                        reads=[bregg] + freg(b1c), writes=glt.reg(), tag=f"gel{half}_{pt}_{tc}")
                    vb.op("dve", (lambda e, banka=banka, pt=pt, hc0=hc0:
                          e.scalar_tensor_tensor(
                              out=hbuf.ap(hc0, hc0 + 512), in0=banka[:, 0:512],
                              scalar=fap(b1c, 2 * pt, 2 * pt + 1), in1=glt.ap(),
                              op0=ALU.add, op1=ALU.mult)),
                          reads=[brega] + glt.reg() + freg(b1c),
                          writes=hbuf.reg(hc0, hc0 + 512), tag=f"glu{half}_{pt}_{tc}")

            if dbg and half == 0:
                vb.dma("sp", (lambda e: e.dma_start(out=dbg_h, in_=hbuf.ap())),
                       reads=hbuf.reg(), writes=[("dram", "dh")], tag="dbgh")
            for ot in range(DT):
                wb = w2b[ot % 2]
                vb.dma("pool", (lambda e, wb=wb, ot=ot: e.dma_start(
                    out=wb.ap().rearrange("p (kt o) -> p kt o", o=P),
                    in_=w2_d[:, ot * P: ot * P + P].rearrange("(kt p) o -> p kt o", p=P))),
                    reads=[("dram", "w2")], writes=wb.reg(), tag=f"lw2_{half}_{ot}")
                for tc in range(2):
                    c0 = th0 + tc * 512
                    bank, breg = pbf[4 + (rot["mm"] % 2)], PSF[4 + (rot["mm"] % 2)]
                    rot["mm"] += 1
                    for kt in range(NPT):
                        vb.op("pe", (lambda e, bank=bank, wb=wb, kt=kt, c0=c0, th0=th0:
                              e.matmul(out=bank[:, 0:512],
                                       lhsT=wb.ap(kt * P, kt * P + P),
                                       rhs=hbuf.ap(kt * 1024 + (c0 - th0), kt * 1024 + (c0 - th0) + 512),
                                       start=(kt == 0), stop=(kt == NPT - 1))),
                              reads=wb.reg(kt * P, kt * P + P) + hbuf.reg(kt * 1024 + (c0 - th0), kt * 1024 + (c0 - th0) + 512),
                              writes=[breg], tag=f"w2m{half}_{ot}_{tc}_{kt}")
                    if dbg and half == 0 and ot == 0 and tc == 0:
                        vb.dma("sp", (lambda e, wb=wb: e.dma_start(out=dbg_w2, in_=wb.ap())),
                               reads=wb.reg(), writes=[("dram", "dw2")], tag="dbgw2")
                        vb.op("dve", (lambda e, bank=bank: e.tensor_copy(
                            out=fap(stg[1]), in_=bank[:, 0:512])),
                            reads=[breg], writes=freg(stg[1]), tag="dbgpscp")
                        vb.dma("sp", (lambda e: e.dma_start(out=dbg_ps, in_=fap(stg[1]))),
                               reads=freg(stg[1]), writes=[("dram", "dps")], tag="dbgps")
                    st = stg[tc % 2]
                    xmc0 = ot * XMC + 128 + c0
                    vb.op("dve", (lambda e, bank=bank, st=st, ot=ot, xmc0=xmc0:
                          e.scalar_tensor_tensor(
                              out=fap(st), in0=bank[:, 0:512],
                              scalar=fap(b2c, ot, ot + 1), in1=fap(xmid, xmc0, xmc0 + 512),
                              op0=ALU.add, op1=ALU.add)),
                          reads=[breg] + freg(b2c) + freg(xmid, xmc0, xmc0 + 512),
                          writes=freg(st), tag=f"w2e{half}_{ot}_{tc}")
                    vb.dma("sp", (lambda e, st=st, ot=ot, c0=c0: e.dma_start(
                        out=xo_d[ot * P: ot * P + P, c0: c0 + 512], in_=fap(st))),
                        reads=freg(st), writes=[("dram", "xo")], tag=f"sto{half}_{ot}_{tc}")

        stats = vb.finalize_and_emit(es, final_queue="pool")
    return nc, stats



def host_eye():
    return np.eye(P, dtype=np.float32)


def host_l0_masks(h):
    m = np.zeros((NTT, P, WIN), np.float32)
    for it in range(QT0, NTT):
        k = np.arange(P)[:, None]
        c = np.arange(WIN)[None, :]
        band = (c >= k) & (c <= k + 256)
        if h == 0:
            keyrow = it * P - 256 + c
            valid = band & ((keyrow >= HALO) | (c == k + 256))
        else:
            valid = band
        m[it] = valid.astype(np.float32)
    return m


def host_l0_tables(h):
    r = np.arange(TE)
    pos = np.clip(h * T + r - HALO, 0, SEQ - 1).astype(np.float64)
    half = np.arange(0, 64, 2, dtype=np.float64)
    inv_freq = 1.0 / 10000.0 ** (half / 64.0)
    f32_ = lambda a: a.astype(np.float32)
    fr = pos[:, None] * inv_freq[None, :]
    fr = np.concatenate([fr, fr], -1)
    base = (half + 0.4 * 64) / (1.4 * 64)
    power = (pos - SEQ // 2) / (256 // 2)
    scale = base[None, :] ** power[:, None]
    scale = np.concatenate([scale, scale], -1)
    sign = np.where(np.arange(64) < 32, -1.0, 1.0)[None, :]
    cosf, sinf = np.cos(fr), np.sin(fr)
    aq = f32_(cosf * scale)
    bq = f32_(sign * sinf * scale)
    ak = f32_(cosf / scale)
    bk = f32_(sign * sinf / scale)
    return np.stack([aq, bq, ak, bk], 0)


def host_l0_w1p(W1):
    a = np.zeros((D, DIP), np.float32)
    g = np.zeros((D, DIP), np.float32)
    a[:, :2730] = W1[:, :2730]
    g[:, :2730] = W1[:, 2730:]
    out = np.zeros((D, W1C), np.float32)
    for pt in range(NPT):
        out[:, (2 * pt) * P:(2 * pt + 1) * P] = a[:, pt * P:(pt + 1) * P]
        out[:, (2 * pt + 1) * P:(2 * pt + 2) * P] = g[:, pt * P:(pt + 1) * P]
    return out


def host_l0_b1p(b1):
    a = np.zeros(DIP, np.float32); a[:2730] = b1[:2730]
    g = np.zeros(DIP, np.float32); g[:2730] = b1[2730:]
    out = np.zeros((P, 2 * NPT), np.float32)
    for pt in range(NPT):
        out[:, 2 * pt] = a[pt * P:(pt + 1) * P]
        out[:, 2 * pt + 1] = g[pt * P:(pt + 1) * P]
    return out


def host_l0_w2p(W2):
    return np.pad(W2, ((0, DIP - 2730), (0, 0))).astype(np.float32)





NE = 8
EH = 2048


def build_l1ffn_program(dbg=False):
    nc = bass.Bass("TRN2", target_bir_lowering=False, debug=False)
    x_d = nc.dram_tensor("x_fm", [D, T], F32, kind="ExternalInput").ap()
    gw_d = nc.dram_tensor("gate_w", [D, NE], F32, kind="ExternalInput").ap()
    w1_d = nc.dram_tensor("e_w1", [NE, D, EH], F32, kind="ExternalInput").ap()
    b1_d = nc.dram_tensor("e_b1", [NE, P, EH // P], F32, kind="ExternalInput").ap()
    w2_d = nc.dram_tensor("e_w2", [NE, EH, D], F32, kind="ExternalInput").ap()
    b2_d = nc.dram_tensor("e_b2", [NE, P, DT], F32, kind="ExternalInput").ap()
    lg_d = nc.dram_tensor("e_lng", [NE, P, DT], F32, kind="ExternalInput").ap()
    lb_d = nc.dram_tensor("e_lnb", [NE, P, DT], F32, kind="ExternalInput").ap()
    ones_d = nc.dram_tensor("cst_ones", [P, P], F32, kind="ExternalInput").ap()
    eye_d = nc.dram_tensor("cst_eye", [P, P], F32, kind="ExternalInput").ap()
    xo_d = nc.dram_tensor("xo_fm", [D, T], F32, kind="ExternalOutput").ap()
    if dbg:
        dbg_xln = nc.dram_tensor("dbg_xln", [P, DT * T], BF16, kind="ExternalOutput").ap()
        dbg_sc = nc.dram_tensor("dbg_sc", [P, 16 * NE], F32, kind="ExternalOutput").ap()
        dbg_wE = nc.dram_tensor("dbg_wE", [P, 16 * NE], F32, kind="ExternalOutput").ap()
        dbg_web = nc.dram_tensor("dbg_web", [P, 1024], F32, kind="ExternalOutput").ap()
        dbg_z = nc.dram_tensor("dbg_z", [P, DT * 1024], F32, kind="ExternalOutput").ap()
        dbg_h = nc.dram_tensor("dbg_h", [P, 16 * 1024], BF16, kind="ExternalOutput").ap()
        dbg_acc = nc.dram_tensor("dbg_acc", [P, DT * 1024], F32, kind="ExternalOutput").ap()

    es = ExitStack()
    with es:
        A = Arena(nc, es, "arn", 103 * 1024, BF16)
        vb = VB(nc)

        def falloc(cols, tag):
            return A.alloc(2 * cols, tag)

        def fap(buf, c0=0, c1=None):
            c1 = buf.cols // 2 if c1 is None else c1
            return buf.arena[:, buf.off + 2 * c0: buf.off + 2 * c1].bitcast(F32)

        def fprt(buf, p0, p1, c0=0, c1=None):
            c1 = buf.cols // 2 if c1 is None else c1
            return buf.arena[p0:p1, buf.off + 2 * c0: buf.off + 2 * c1].bitcast(F32)

        def freg(buf, c0=0, c1=None):
            c1 = buf.cols // 2 if c1 is None else c1
            return buf.reg(2 * c0, 2 * c1)

        cstb = falloc(16, "consts")
        ones = falloc(P, "ones")
        eyef = falloc(P, "eyef")
        gwb = A.alloc(DT * NE, "gw")
        b1c = [falloc(EH // P, f"b1_{e}") for e in range(NE)]
        b2c = [falloc(DT, f"b2_{e}") for e in range(NE)]
        lgc = [falloc(DT, f"lg_{e}") for e in range(NE)]
        lbc = [falloc(DT, f"lb_{e}") for e in range(NE)]
        xlnb = A.alloc(DT * T, "xlnb")
        wE = falloc(16 * NE, "wE")
        scb = falloc(16 * NE, "scb")
        g1 = falloc(16, "g1")
        g2 = falloc(16, "g2")
        g3 = falloc(16, "g3")

        cst_map = {}

        def cst(value):
            value = float(value)
            if value not in cst_map:
                i = len(cst_map)
                assert i < 16
                ap = fap(cstb, i, i + 1)
                rg = freg(cstb, i, i + 1)
                vb.op("dve", (lambda e, ap=ap, value=value: e.memset(ap, value)),
                      reads=[], writes=rg, tag=f"cst{value}")
                cst_map[value] = (ap, rg)
            return cst_map[value]

        pbf = [es.enter_context(nc.psum_tensor(f"pf{i}", [P, 512], F32)) for i in range(8)]
        PSF = [("psf", i) for i in range(8)]
        rot = {"mm": 0}

        def mmbank(lo=0, n=4):
            i = lo + rot["mm"] % n
            rot["mm"] += 1
            return pbf[i], PSF[i]

        vb.dma("sp", lambda e: e.dma_start(out=fap(ones), in_=ones_d),
               writes=freg(ones), tag="ld_ones")
        vb.dma("sp", lambda e: e.dma_start(out=fap(eyef), in_=eye_d),
               writes=freg(eyef), tag="ld_eye")
        vb.dma("pool", (lambda e: e.dma_start(
            out=gwb.ap().rearrange("p (dt e) -> p dt e", e=NE),
            in_=gw_d.rearrange("(dt p) e -> p dt e", p=P))),
            writes=gwb.reg(), tag="ld_gw")
        for e_ in range(NE):
            for dst, srcd in ((b1c[e_], b1_d[e_]), (b2c[e_], b2_d[e_]),
                              (lgc[e_], lg_d[e_]), (lbc[e_], lb_d[e_])):
                vb.dma("sp", (lambda en, dst=dst, srcd=srcd: en.dma_start(
                    out=fap(dst), in_=srcd)),
                    writes=freg(dst), tag=f"ldv{e_}")

        A.push()
        xres = falloc(DT * T, "xres")
        abp = falloc(T, "Ab")
        bbp = falloc(T, "Bb")
        rows = falloc(2 * T, "rows")
        sqs = falloc(512, "sqs")
        vb.dma("sp", (lambda e: e.dma_start(
            out=fap(xres).rearrange("p (dt c) -> p dt c", c=T),
            in_=x_d.rearrange("(dt p) c -> p dt c", p=P))),
            reads=[("dram", "x")], writes=freg(xres), tag="ld_x")

        class _BufView:
            def __init__(self, buf, fap, freg):
                self.b, self._fap, self._freg = buf, fap, freg
                self.cols = buf.cols // 2

            def ap(self, c0=0, c1=None):
                return self._fap(self.b, c0, c1)

            def prt(self, p0, p1, c0=0, c1=None):
                return fprt(self.b, p0, p1, c0, c1)

            def reg(self, c0=0, c1=None):
                return self._freg(self.b, c0, c1)

        xv = _BufView(xres, fap, freg)
        av = _BufView(abp, fap, freg)
        bv = _BufView(bbp, fap, freg)
        rv = _BufView(rows, fap, freg)
        sv = _BufView(sqs, fap, freg)

        class _CstAdapter:
            def col(self, v):
                return cst(v)

        emit_ln_fm(vb, pbf[:4], PSF[:4], xv, xlnb, rv, av, bv,
                   ones_col=fprt(ones, 0, P, 0, 1), ones_col_reg=freg(ones, 0, 1),
                   ones_row=fprt(ones, 0, 1, 0, P), ones_row_reg=freg(ones, 0, P),
                   TT=T, eps=1e-5, consts=_CstAdapter(), sq_tmp=sv, prefix="mln")

        half_c, half_rg = cst(0.5)
        for tt in range(16):
            bank, breg = mmbank()
            for dt in range(DT):
                vb.op("pe", (lambda e, bank=bank, dt=dt, tt=tt: e.matmul(
                    out=bank[:, 0:NE],
                    lhsT=xlnb.ap(dt * T + tt * P, dt * T + tt * P + P),
                    rhs=gwb.ap(dt * NE, dt * NE + NE),
                    start=(dt == 0), stop=(dt == DT - 1))),
                    reads=xlnb.reg(dt * T + tt * P, dt * T + tt * P + P) + gwb.reg(),
                    writes=[breg], tag=f"gt{tt}_{dt}")
            sc = fap(scb, tt * NE, tt * NE + NE)
            scr = freg(scb, tt * NE, tt * NE + NE)
            vb.op("dve", (lambda e, bank=bank, sc=sc: e.tensor_copy(
                out=sc, in_=bank[:, 0:NE])),
                reads=[breg], writes=scr, tag=f"gte{tt}")
            m1 = fap(g1, 0, 1); m1r = freg(g1, 0, 1)
            m2 = fap(g1, 1, 2); m2r = freg(g1, 1, 2)
            w1g = fap(g1, 2, 3); w1r = freg(g1, 2, 3)
            w2g = fap(g1, 3, 4); w2r = freg(g1, 3, 4)
            eq1 = fap(g2, 0, NE); eq1r = freg(g2, 0, NE)
            eq2 = fap(g3, 0, NE); eq2r = freg(g3, 0, NE)
            vb.op("dve", (lambda e, sc=sc, m1=m1: e.reduce_max(
                out=m1, in_=sc, axis=AX.X)),
                reads=scr, writes=m1r, tag=f"g_m1_{tt}")
            vb.op("dve", (lambda e, sc=sc, m1=m1, eq1=eq1: e.tensor_scalar(
                out=eq1, in0=sc, scalar1=m1, scalar2=None, op0=ALU.is_equal)),
                reads=scr + m1r, writes=eq1r, tag=f"g_e1_{tt}")
            vb.op("dve", (lambda e, sc=sc, eq1=eq1, m2=m2: e.scalar_tensor_tensor(
                out=fap(g2, NE, 2 * NE), in0=eq1, scalar=-1e30, in1=sc,
                op0=ALU.mult, op1=ALU.add)),
                reads=eq1r + scr, writes=freg(g2, NE, 2 * NE), tag=f"g_msk_{tt}")
            vb.op("dve", (lambda e, m2=m2: e.reduce_max(
                out=m2, in_=fap(g2, NE, 2 * NE), axis=AX.X)),
                reads=freg(g2, NE, 2 * NE), writes=m2r, tag=f"g_m2_{tt}")
            vb.op("dve", (lambda e, m2=m2, eq2=eq2: e.tensor_scalar(
                out=eq2, in0=fap(g2, NE, 2 * NE), scalar1=m2, scalar2=None,
                op0=ALU.is_equal)),
                reads=freg(g2, NE, 2 * NE) + m2r, writes=eq2r, tag=f"g_e2_{tt}")
            vb.op("dve", (lambda e, m1=m1, m2=m2, w1g=w1g: e.tensor_tensor(
                out=w1g, in0=m1, in1=m2, op=ALU.subtract)),
                reads=m1r + m2r, writes=w1r, tag=f"g_d_{tt}")
            vb.op("act", (lambda e, w1g=w1g: e.activation(
                out=w1g, in_=w1g, func=AF.Sigmoid)),
                reads=w1r, writes=w1r, tag=f"g_s_{tt}")
            vb.op("dve", (lambda e, w1g=w1g, w2g=w2g: e.tensor_scalar(
                out=w2g, in0=w1g, scalar1=-1.0, scalar2=1.0, op0=ALU.mult,
                op1=ALU.add)),
                reads=w1r, writes=w2r, tag=f"g_w2_{tt}")
            wEsl = fap(wE, tt * NE, tt * NE + NE)
            wEr = freg(wE, tt * NE, tt * NE + NE)
            vb.op("dve", (lambda e, eq2=eq2, w2g=w2g, wEsl=wEsl: e.tensor_scalar(
                out=wEsl, in0=eq2, scalar1=w2g, scalar2=None, op0=ALU.mult)),
                reads=eq2r + w2r, writes=wEr, tag=f"g_wa_{tt}")
            vb.op("dve", (lambda e, eq1=eq1, w1g=w1g, wEsl=wEsl: e.scalar_tensor_tensor(
                out=wEsl, in0=eq1, scalar=w1g, in1=wEsl, op0=ALU.mult, op1=ALU.add)),
                reads=eq1r + w1r + wEr, writes=wEr, tag=f"g_wb_{tt}")

        if dbg:
            vb.dma("sp", (lambda e: e.dma_start(out=dbg_xln, in_=xlnb.ap())),
                   reads=xlnb.reg(), writes=[("dram", "dxln")], tag="dxln")
            vb.dma("sp", (lambda e: e.dma_start(out=dbg_sc, in_=fap(scb))),
                   reads=freg(scb), writes=[("dram", "dsc")], tag="dsc")
            vb.dma("sp", (lambda e: e.dma_start(out=dbg_wE, in_=fap(wE))),
                   reads=freg(wE), writes=[("dram", "dwE")], tag="dwE")
        A.pop()

        hbe = A.alloc(16 * 1024, "hbe")
        zbe = falloc(DT * 1024, "zbe")
        acc = falloc(DT * 1024, "acc")
        ab2 = falloc(1024, "ab2")
        bb2 = falloc(1024, "bb2")
        rws = falloc(2 * 1024, "rws")
        sqf = falloc(512, "sqf")
        w1b = [A.alloc(DT * P, f"w1b{s}") for s in range(2)]
        w2b = [A.alloc(16 * P, f"w2b{s}") for s in range(2)]
        web = falloc(1024, "web")
        stg = [falloc(512, f"stg{s}") for s in range(2)]
        eps6, eps6_rg = cst(1e-6)

        for half in range(2):
            th0 = half * 1024
            for dt in range(DT):
                vb.op("dve", (lambda e, dt=dt: e.memset(
                    fap(acc, dt * 1024, (dt + 1) * 1024), 0.0)),
                    writes=freg(acc, dt * 1024, (dt + 1) * 1024), tag=f"z{half}_{dt}")
            for e_ in range(NE):
                for ot in range(16):
                    wb = w1b[ot % 2]
                    vb.dma("pool", (lambda en, wb=wb, e_=e_, ot=ot: en.dma_start(
                        out=wb.ap().rearrange("p (dt o) -> p dt o", o=P),
                        in_=w1_d[e_][:, ot * P: ot * P + P].rearrange("(dt p) o -> p dt o", p=P))),
                        reads=[("dram", "w1")], writes=wb.reg(), tag=f"lw1_{e_}_{ot}")
                    for tc in range(2):
                        c0 = th0 + tc * 512
                        bank, breg = mmbank(0, 4)
                        for dt in range(DT):
                            vb.op("pe", (lambda en, bank=bank, wb=wb, dt=dt, c0=c0:
                                  en.matmul(out=bank[:, 0:512],
                                            lhsT=wb.ap(dt * P, dt * P + P),
                                            rhs=xlnb.ap(dt * T + c0, dt * T + c0 + 512),
                                            start=(dt == 0), stop=(dt == DT - 1))),
                                  reads=wb.reg(dt * P, dt * P + P) + xlnb.reg(dt * T + c0, dt * T + c0 + 512),
                                  writes=[breg], tag=f"w1m{e_}_{ot}_{tc}")
                        hc0 = ot * 1024 + tc * 512
                        vb.op("act", (lambda en, bank=bank, e_=e_, ot=ot, hc0=hc0:
                              en.activation(out=hbe.ap(hc0, hc0 + 512), in_=bank[:, 0:512],
                                            func=AF.Gelu, bias=fap(b1c[e_], ot, ot + 1))),
                              reads=[breg] + freg(b1c[e_]), writes=hbe.reg(hc0, hc0 + 512),
                              tag=f"gel{e_}_{ot}_{tc}")
                for ot in range(DT):
                    wb = w2b[ot % 2]
                    vb.dma("pool", (lambda en, wb=wb, e_=e_, ot=ot: en.dma_start(
                        out=wb.ap().rearrange("p (kt o) -> p kt o", o=P),
                        in_=w2_d[e_][:, ot * P: ot * P + P].rearrange("(kt p) o -> p kt o", p=P))),
                        reads=[("dram", "w2")], writes=wb.reg(), tag=f"lw2_{e_}_{ot}")
                    for tc in range(2):
                        c0 = th0 + tc * 512
                        bank, breg = mmbank(4, 2)
                        for kt in range(16):
                            vb.op("pe", (lambda en, bank=bank, wb=wb, kt=kt, tc=tc:
                                  en.matmul(out=bank[:, 0:512],
                                            lhsT=wb.ap(kt * P, kt * P + P),
                                            rhs=hbe.ap(kt * 1024 + tc * 512, kt * 1024 + tc * 512 + 512),
                                            start=(kt == 0), stop=(kt == 15))),
                                  reads=wb.reg(kt * P, kt * P + P) + hbe.reg(kt * 1024 + tc * 512, kt * 1024 + tc * 512 + 512),
                                  writes=[breg], tag=f"w2m{e_}_{ot}_{tc}")
                        zc0 = ot * 1024 + tc * 512
                        vb.op("dve", (lambda en, bank=bank, e_=e_, ot=ot, zc0=zc0, c0=c0:
                              en.scalar_tensor_tensor(
                                  out=fap(zbe, zc0, zc0 + 512), in0=bank[:, 0:512],
                                  scalar=fap(b2c[e_], ot, ot + 1),
                                  in1=xlnb.ap(ot * T + c0, ot * T + c0 + 512),
                                  op0=ALU.add, op1=ALU.add)),
                              reads=[breg] + freg(b2c[e_]) + xlnb.reg(ot * T + c0, ot * T + c0 + 512),
                              writes=freg(zbe, zc0, zc0 + 512), tag=f"w2e{e_}_{ot}_{tc}")
                for ci in range(2):
                    c0 = ci * 512
                    bank, breg = pbf[6], PSF[6]
                    qbank, qbreg = pbf[7], PSF[7]
                    for dt in range(DT):
                        zc0 = dt * 1024 + c0
                        vb.op("pe", (lambda en, bank=bank, zc0=zc0, dt=dt: en.matmul(
                            out=bank[0:1, 0:512], lhsT=fprt(ones, 0, P, 0, 1),
                            rhs=fap(zbe, zc0, zc0 + 512),
                            start=(dt == 0), stop=(dt == DT - 1))),
                            reads=freg(zbe, zc0, zc0 + 512) + freg(ones, 0, 1),
                            writes=[breg], tag=f"es{e_}_{ci}_{dt}")
                        vb.op("act", (lambda en, zc0=zc0: en.activation(
                            out=fap(sqf), in_=fap(zbe, zc0, zc0 + 512), func=AF.Square)),
                            reads=freg(zbe, zc0, zc0 + 512), writes=freg(sqf),
                            tag=f"eq{e_}_{ci}_{dt}")
                        vb.op("pe", (lambda en, qbank=qbank, dt=dt: en.matmul(
                            out=qbank[0:1, 0:512], lhsT=fprt(ones, 0, P, 0, 1),
                            rhs=fap(sqf),
                            start=(dt == 0), stop=(dt == DT - 1))),
                            reads=freg(sqf) + freg(ones, 0, 1),
                            writes=[qbreg], tag=f"eqs{e_}_{ci}_{dt}")
                    mu = fprt(rws, 0, 1, c0, c0 + 512)
                    va = fprt(rws, 0, 1, 1024 + c0, 1024 + c0 + 512)
                    mur = freg(rws)
                    vb.op("dve", (lambda en, bank=bank, mu=mu: en.tensor_scalar(
                        out=mu, in0=bank[0:1, 0:512], scalar1=1.0 / D, scalar2=None,
                        op0=ALU.mult)),
                        reads=[breg], writes=mur, tag=f"emu{e_}_{ci}")
                    vb.op("dve", (lambda en, mu=mu, va=va: en.tensor_tensor(
                        out=va, in0=mu, in1=mu, op=ALU.mult)),
                        reads=mur, writes=mur, tag=f"em2{e_}_{ci}")
                    vb.op("dve", (lambda en, qbank=qbank, va=va: en.scalar_tensor_tensor(
                        out=va, in0=qbank[0:1, 0:512], scalar=1.0 / D, in1=va,
                        op0=ALU.mult, op1=ALU.subtract)),
                        reads=[qbreg] + mur, writes=mur, tag=f"ev{e_}_{ci}")
                    vb.op("act", (lambda en, va=va: en.activation(
                        out=va, in_=va, func=AF.Sqrt, bias=eps6[0:1, :])),
                        reads=mur + eps6_rg, writes=mur, tag=f"est{e_}_{ci}")
                    vb.op("dve", (lambda en, va=va: en.reciprocal(out=va, in_=va)),
                        reads=mur, writes=mur, tag=f"ers{e_}_{ci}")
                    vb.op("dve", (lambda en, mu=mu, va=va: en.scalar_tensor_tensor(
                        out=mu, in0=mu, scalar=-1.0, in1=va, op0=ALU.mult, op1=ALU.mult)),
                        reads=mur, writes=mur, tag=f"enm{e_}_{ci}")
                    bka, brga = pbf[6], PSF[6]
                    bkb, brgb = pbf[7], PSF[7]
                    vb.op("pe", (lambda en, bka=bka, va=va: en.matmul(
                        out=bka[:, 0:512], lhsT=fprt(ones, 0, 1), rhs=va,
                        start=True, stop=True)),
                        reads=mur + freg(ones), writes=[brga], tag=f"eba{e_}_{ci}")
                    vb.op("dve", (lambda en, bka=bka, c0=c0: en.tensor_copy(
                        out=fap(ab2, c0, c0 + 512), in_=bka[:, 0:512])),
                        reads=[brga], writes=freg(ab2, c0, c0 + 512), tag=f"ebae{e_}_{ci}")
                    vb.op("pe", (lambda en, bkb=bkb, mu=mu: en.matmul(
                        out=bkb[:, 0:512], lhsT=fprt(ones, 0, 1), rhs=mu,
                        start=True, stop=True)),
                        reads=mur + freg(ones), writes=[brgb], tag=f"ebb{e_}_{ci}")
                    vb.op("dve", (lambda en, bkb=bkb, c0=c0: en.tensor_copy(
                        out=fap(bb2, c0, c0 + 512), in_=bkb[:, 0:512])),
                        reads=[brgb], writes=freg(bb2, c0, c0 + 512), tag=f"ebbe{e_}_{ci}")
                for sc_ in range(2):
                    bank2, breg2 = pbf[6], PSF[6]
                    for q4 in range(4):
                        tt = half * 8 + sc_ * 4 + q4
                        vb.op("pe", (lambda en, bank2=bank2, tt=tt, q4=q4, e_=e_: en.matmul(
                            out=bank2[0:1, q4 * P: q4 * P + P],
                            lhsT=fap(wE, tt * NE + e_, tt * NE + e_ + 1),
                            rhs=fap(eyef),
                            start=True, stop=True)),
                            reads=freg(wE) + freg(eyef), writes=[breg2], tag=f"wt{e_}_{sc_}_{q4}")
                    vb.op("dve", (lambda en, bank2=bank2, sc_=sc_: en.tensor_copy(
                        out=fprt(web, 0, 1, sc_ * 512, sc_ * 512 + 512),
                        in_=bank2[0:1, 0:512])),
                        reads=[breg2], writes=freg(web, sc_ * 512, sc_ * 512 + 512),
                        tag=f"wte{e_}_{sc_}")
                for sc_ in range(2):
                    bank3, breg3 = pbf[7], PSF[7]
                    vb.op("pe", (lambda en, bank3=bank3, sc_=sc_: en.matmul(
                        out=bank3[:, 0:512], lhsT=fprt(ones, 0, 1),
                        rhs=fprt(web, 0, 1, sc_ * 512, sc_ * 512 + 512),
                        start=True, stop=True)),
                        reads=freg(web, sc_ * 512, sc_ * 512 + 512) + freg(ones),
                        writes=[breg3], tag=f"wb{e_}_{sc_}")
                    vb.op("dve", (lambda en, bank3=bank3, sc_=sc_: en.tensor_copy(
                        out=fap(web, sc_ * 512, sc_ * 512 + 512), in_=bank3[:, 0:512])),
                        reads=[breg3], writes=freg(web, sc_ * 512, sc_ * 512 + 512),
                        tag=f"wbe{e_}_{sc_}")
                if dbg and half == 0 and e_ == 0:
                    vb.dma("sp", (lambda e: e.dma_start(out=dbg_web, in_=fap(web))),
                           reads=freg(web), writes=[("dram", "dweb")], tag="dweb")
                    vb.dma("sp", (lambda e: e.dma_start(out=dbg_z, in_=fap(zbe))),
                           reads=freg(zbe), writes=[("dram", "dz")], tag="dz")
                    vb.dma("sp", (lambda e: e.dma_start(out=dbg_h, in_=hbe.ap())),
                           reads=hbe.reg(), writes=[("dram", "dh")], tag="dhh")
                for dt in range(DT):
                    zc0 = dt * 1024
                    t1 = fap(zbe, zc0, zc0 + 1024)
                    t1r = freg(zbe, zc0, zc0 + 1024)
                    vb.op("dve", (lambda en, t1=t1: en.tensor_tensor(
                        out=t1, in0=t1, in1=fap(ab2), op=ALU.mult)),
                        reads=t1r + freg(ab2), writes=t1r, tag=f"ea1{e_}_{dt}")
                    vb.op("dve", (lambda en, t1=t1: en.tensor_tensor(
                        out=t1, in0=t1, in1=fap(bb2), op=ALU.add)),
                        reads=t1r + freg(bb2), writes=t1r, tag=f"ea2{e_}_{dt}")
                    vb.op("dve", (lambda en, t1=t1, e_=e_, dt=dt: en.tensor_scalar(
                        out=t1, in0=t1, scalar1=fap(lgc[e_], dt, dt + 1),
                        scalar2=fap(lbc[e_], dt, dt + 1), op0=ALU.mult, op1=ALU.add)),
                        reads=t1r + freg(lgc[e_]) + freg(lbc[e_]), writes=t1r,
                        tag=f"ea3{e_}_{dt}")
                    vb.op("dve", (lambda en, t1=t1: en.tensor_tensor(
                        out=t1, in0=t1, in1=fap(web), op=ALU.mult)),
                        reads=t1r + freg(web), writes=t1r, tag=f"ea4{e_}_{dt}")
                    ac = fap(acc, dt * 1024, (dt + 1) * 1024)
                    acr = freg(acc, dt * 1024, (dt + 1) * 1024)
                    vb.op("dve", (lambda en, t1=t1, ac=ac: en.tensor_tensor(
                        out=ac, in0=ac, in1=t1, op=ALU.add)),
                        reads=acr + t1r, writes=acr, tag=f"ea5{e_}_{dt}")
            if dbg and half == 0:
                vb.dma("sp", (lambda e: e.dma_start(out=dbg_acc, in_=fap(acc))),
                       reads=freg(acc), writes=[("dram", "dacc")], tag="dacc")
            for dt in range(DT):
                for tc in range(2):
                    c0 = th0 + tc * 512
                    st = stg[tc % 2]
                    vb.dma("sp", (lambda en, st=st, dt=dt, c0=c0: en.dma_start(
                        out=fap(st), in_=x_d[dt * P: dt * P + P, c0: c0 + 512])),
                        reads=[("dram", "x")], writes=freg(st), tag=f"lxr{half}_{dt}_{tc}")
                    vb.op("dve", (lambda en, st=st, dt=dt, tc=tc, c0=c0, th0=th0:
                          en.scalar_tensor_tensor(
                              out=fap(st), in0=fap(acc, dt * 1024 + tc * 512, dt * 1024 + tc * 512 + 512),
                              scalar=0.5, in1=fap(st), op0=ALU.mult, op1=ALU.add)),
                          reads=freg(acc, dt * 1024 + tc * 512, dt * 1024 + tc * 512 + 512) + freg(st),
                          writes=freg(st), tag=f"xos{half}_{dt}_{tc}")
                    vb.dma("sp", (lambda en, st=st, dt=dt, c0=c0: en.dma_start(
                        out=xo_d[dt * P: dt * P + P, c0: c0 + 512], in_=fap(st))),
                        reads=freg(st), writes=[("dram", "xo")], tag=f"sxo{half}_{dt}_{tc}")

        stats = vb.finalize_and_emit(es, final_queue="pool")
    return nc, stats



def build_axpy_program():
    nc = bass.Bass("TRN2", target_bir_lowering=False, debug=False)
    x_d = nc.dram_tensor("x_fm", [D, T], F32, kind="ExternalInput").ap()
    y_d = nc.dram_tensor("y_fm", [D, T], F32, kind="ExternalInput").ap()
    xo_d = nc.dram_tensor("xo_fm", [D, T], F32, kind="ExternalOutput").ap()
    es = ExitStack()
    with es:
        A = Arena(nc, es, "arn", 16 * 1024, BF16)
        vb = VB(nc)

        def falloc(cols, tag):
            return A.alloc(2 * cols, tag)

        def fap(buf, c0=0, c1=None):
            c1 = buf.cols // 2 if c1 is None else c1
            return buf.arena[:, buf.off + 2 * c0: buf.off + 2 * c1].bitcast(F32)

        def freg(buf, c0=0, c1=None):
            c1 = buf.cols // 2 if c1 is None else c1
            return buf.reg(2 * c0, 2 * c1)

        xb = [falloc(512, f"x{s}") for s in range(3)]
        yb = [falloc(512, f"y{s}") for s in range(3)]
        for dt in range(DT):
            for tc in range(4):
                s = (dt * 4 + tc) % 3
                c0 = tc * 512
                vb.dma("sp", (lambda e, s=s, dt=dt, c0=c0: e.dma_start(
                    out=fap(xb[s]), in_=x_d[dt * P: dt * P + P, c0: c0 + 512])),
                    reads=[("dram", "x")], writes=freg(xb[s]), tag=f"lx{dt}_{tc}")
                vb.dma("pool", (lambda e, s=s, dt=dt, c0=c0: e.dma_start(
                    out=fap(yb[s]), in_=y_d[dt * P: dt * P + P, c0: c0 + 512])),
                    reads=[("dram", "y")], writes=freg(yb[s]), tag=f"ly{dt}_{tc}")
                vb.op("dve", (lambda e, s=s: e.scalar_tensor_tensor(
                    out=fap(xb[s]), in0=fap(yb[s]), scalar=0.5, in1=fap(xb[s]),
                    op0=ALU.mult, op1=ALU.add)),
                    reads=freg(xb[s]) + freg(yb[s]), writes=freg(xb[s]),
                    tag=f"ax{dt}_{tc}")
                vb.dma("sp", (lambda e, s=s, dt=dt, c0=c0: e.dma_start(
                    out=xo_d[dt * P: dt * P + P, c0: c0 + 512], in_=fap(xb[s]))),
                    reads=freg(xb[s]), writes=[("dram", "xo")], tag=f"sx{dt}_{tc}")
        stats = vb.finalize_and_emit(es, final_queue="pool")
    return nc, stats



def build_linattn_a_program():
    nc = bass.Bass("TRN2", target_bir_lowering=False, debug=False)
    x_d = nc.dram_tensor("x_fm", [D, T], F32, kind="ExternalInput").ap()
    wq_d = nc.dram_tensor("w_qkv", [D, 3 * D], F32, kind="ExternalInput").ap()
    ones_d = nc.dram_tensor("cst_ones", [P, P], F32, kind="ExternalInput").ap()
    eye_d = nc.dram_tensor("cst_eye", [P, P], F32, kind="ExternalInput").ap()
    qs_d = nc.dram_tensor("qs_tm", [T, D], F32, kind="ExternalOutput").ap()
    k_d = nc.dram_tensor("k_tm", [T, D], F32, kind="ExternalOutput").ap()
    v_d = nc.dram_tensor("v_tm", [T, D], F32, kind="ExternalOutput").ap()
    km_d = nc.dram_tensor("kmax_fm", [P, DT], F32, kind="ExternalOutput").ap()
    es = ExitStack()
    with es:
        A = Arena(nc, es, "arn", 100 * 1024, BF16)
        vb = VB(nc)

        def falloc(cols, tag):
            return A.alloc(2 * cols, tag)

        def fap(buf, c0=0, c1=None):
            c1 = buf.cols // 2 if c1 is None else c1
            return buf.arena[:, buf.off + 2 * c0: buf.off + 2 * c1].bitcast(F32)

        def fprt(buf, p0, p1, c0=0, c1=None):
            c1 = buf.cols // 2 if c1 is None else c1
            return buf.arena[p0:p1, buf.off + 2 * c0: buf.off + 2 * c1].bitcast(F32)

        def freg(buf, c0=0, c1=None):
            c1 = buf.cols // 2 if c1 is None else c1
            return buf.reg(2 * c0, 2 * c1)

        cstb = falloc(16, "consts")
        ones = falloc(P, "ones")
        idn = A.alloc(P, "idn")
        xlnb = A.alloc(DT * T, "xlnb")
        wqkv = A.alloc(DT * 3 * D, "wqkv")
        mx = falloc(D, "mx")
        tmb = falloc(512, "tm0")
        nrm = falloc(3 * 16, "nrm")
        cst_map = {}

        def cst(value):
            value = float(value)
            if value not in cst_map:
                i = len(cst_map)
                ap = fap(cstb, i, i + 1)
                rg = freg(cstb, i, i + 1)
                vb.op("dve", (lambda e, ap=ap, value=value: e.memset(ap, value)),
                      reads=[], writes=rg, tag=f"cst{value}")
                cst_map[value] = (ap, rg)
            return cst_map[value]

        pbf = [es.enter_context(nc.psum_tensor(f"pf{i}", [P, 512], F32)) for i in range(6)]
        PSF = [("psf", i) for i in range(6)]
        pbb = [es.enter_context(nc.psum_tensor(f"pb{i}", [P, 1024], BF16)) for i in range(2)]
        PSB = [("psb", i) for i in range(2)]
        rot = {"mm": 0, "tp": 0}

        def mmbank():
            i = rot["mm"] % 4
            rot["mm"] += 1
            return pbf[i], PSF[i]

        def tpbank():
            i = rot["tp"] % 2
            rot["tp"] += 1
            return pbb[i], PSB[i]

        vb.dma("sp", lambda e: e.dma_start(out=fap(ones), in_=ones_d),
               writes=freg(ones), tag="ld_ones")
        vb.dma("pool", lambda e: e.dma_start(out=idn.ap(), in_=eye_d),
               writes=idn.reg(), tag="ld_idn")
        for oc in range(6):
            vb.dma("pool", (lambda e, oc=oc: e.dma_start(
                out=wqkv.ap().rearrange("p (d o) -> p d o", o=3 * D)[:, :, 512 * oc: 512 * oc + 512],
                in_=wq_d[:, 512 * oc: 512 * oc + 512].rearrange("(dt p) o -> p dt o", p=P))),
                writes=wqkv.reg(), tag=f"ld_w{oc}")

        A.push()
        xres = falloc(DT * T, "xres")
        abp = falloc(T, "Ab")
        bbp = falloc(T, "Bb")
        rows = falloc(2 * T, "rows")
        sqs = falloc(512, "sqs")
        vb.dma("sp", (lambda e: e.dma_start(
            out=fap(xres).rearrange("p (dt c) -> p dt c", c=T),
            in_=x_d.rearrange("(dt p) c -> p dt c", p=P))),
            reads=[("dram", "x")], writes=freg(xres), tag="ld_x")

        class _BV:
            def __init__(self, b):
                self.b = b
                self.cols = b.cols // 2

            def ap(self, c0=0, c1=None):
                return fap(self.b, c0, c1)

            def prt(self, p0, p1, c0=0, c1=None):
                return fprt(self.b, p0, p1, c0, c1)

            def reg(self, c0=0, c1=None):
                return freg(self.b, c0, c1)

        class _CA:
            def col(self, v):
                return cst(v)

        emit_ln_fm(vb, pbf[:4], PSF[:4], _BV(xres), xlnb, _BV(rows), _BV(abp), _BV(bbp),
                   ones_col=fprt(ones, 0, P, 0, 1), ones_col_reg=freg(ones, 0, 1),
                   ones_row=fprt(ones, 0, 1, 0, P), ones_row_reg=freg(ones, 0, P),
                   TT=T, eps=1e-5, consts=_CA(), sq_tmp=_BV(sqs), prefix="aln")
        A.pop()

        vb.op("dve", (lambda e: e.memset(fap(mx), -3.0e38)),
              writes=freg(mx), tag="mxinit")

        for tt in range(16):
            for oc in range(6):
                bank, breg = mmbank()
                for dt in range(DT):
                    wc0 = dt * 3 * D + 512 * oc
                    vb.op("pe", (lambda e, bank=bank, tt=tt, dt=dt, wc0=wc0: e.matmul(
                        out=bank[:, 0:512],
                        lhsT=xlnb.ap(dt * T + tt * P, dt * T + tt * P + P),
                        rhs=wqkv.ap(wc0, wc0 + 512),
                        start=(dt == 0), stop=(dt == DT - 1))),
                        reads=xlnb.reg(dt * T + tt * P, dt * T + tt * P + P) + wqkv.reg(wc0, wc0 + 512),
                        writes=[breg], tag=f"qkv{tt}_{oc}")
                if oc < 2:
                    nA = fap(nrm, 0, 8)
                    nAr = freg(nrm, 0, 8)
                    vb.op("dve", (lambda e, bank=bank, nA=nA: e.reduce_max(
                        out=nA, in_=bank[:, 0:512].rearrange("p (h d) -> p h d", d=64),
                        axis=AX.X, negate=True)),
                        reads=[breg], writes=nAr, tag=f"qm{tt}_{oc}")
                    vb.op("dve", (lambda e, bank=bank, nA=nA: e.tensor_tensor(
                        out=fap(tmb), in0=bank[:, 0:512],
                        in1=nA.unsqueeze(2).broadcast_to([P, 8, 64]).rearrange("p h d -> p (h d)") if False else nA.unsqueeze(2).broadcast_to([P, 8, 64]),
                        op=ALU.add)) if False else
                        (lambda e, bank=bank, nA=nA: e.tensor_tensor(
                            out=fap(tmb).rearrange("p (h d) -> p h d", d=64),
                            in0=bank[:, 0:512].rearrange("p (h d) -> p h d", d=64),
                            in1=nA.unsqueeze(2).broadcast_to([P, 8, 64]),
                            op=ALU.add)),
                        reads=[breg] + nAr, writes=freg(tmb), tag=f"qs{tt}_{oc}")
                    vb.op("act", (lambda e: e.activation(
                        out=fap(tmb), in_=fap(tmb), func=AF.Exp)),
                        reads=freg(tmb), writes=freg(tmb), tag=f"qe{tt}_{oc}")
                    sA = fap(nrm, 16, 24)
                    sAr = freg(nrm, 16, 24)
                    vb.op("dve", (lambda e, sA=sA: e.reduce_sum(
                        out=sA, in_=fap(tmb).rearrange("p (h d) -> p h d", d=64),
                        axis=AX.X)),
                        reads=freg(tmb), writes=sAr, tag=f"qss{tt}_{oc}")
                    vb.op("dve", (lambda e, sA=sA: e.reciprocal(out=sA, in_=sA)),
                        reads=sAr, writes=sAr, tag=f"qrc{tt}_{oc}")
                    vb.op("dve", (lambda e, sA=sA: e.tensor_scalar(
                        out=sA, in0=sA, scalar1=0.125, scalar2=None, op0=ALU.mult)),
                        reads=sAr, writes=sAr, tag=f"qsc{tt}_{oc}")
                    vb.op("dve", (lambda e, sA=sA: e.tensor_tensor(
                        out=fap(tmb).rearrange("p (h d) -> p h d", d=64),
                        in0=fap(tmb).rearrange("p (h d) -> p h d", d=64),
                        in1=sA.unsqueeze(2).broadcast_to([P, 8, 64]),
                        op=ALU.mult)),
                        reads=freg(tmb) + sAr, writes=freg(tmb), tag=f"qn{tt}_{oc}")
                    vb.dma("sp", (lambda e, tt=tt, oc=oc: e.dma_start(
                        out=qs_d[tt * P: tt * P + P, 512 * oc: 512 * oc + 512],
                        in_=fap(tmb))),
                        reads=freg(tmb), writes=[("dram", "qs")], tag=f"qst{tt}_{oc}")
                elif oc < 4:
                    vb.op("dve", (lambda e, bank=bank: e.tensor_copy(
                        out=fap(tmb), in_=bank[:, 0:512])),
                        reads=[breg], writes=freg(tmb), tag=f"ke{tt}_{oc}")
                    mc0 = (oc - 2) * 512
                    vb.op("dve", (lambda e, mc0=mc0: e.tensor_tensor(
                        out=fap(mx, mc0, mc0 + 512), in0=fap(mx, mc0, mc0 + 512),
                        in1=fap(tmb), op=ALU.max)),
                        reads=freg(mx, mc0, mc0 + 512) + freg(tmb),
                        writes=freg(mx, mc0, mc0 + 512), tag=f"km{tt}_{oc}")
                    vb.dma("sp", (lambda e, tt=tt, oc=oc: e.dma_start(
                        out=k_d[tt * P: tt * P + P, 512 * (oc - 2): 512 * (oc - 2) + 512],
                        in_=fap(tmb))),
                        reads=freg(tmb), writes=[("dram", "k")], tag=f"kst{tt}_{oc}")
                else:
                    vb.op("dve", (lambda e, bank=bank: e.tensor_copy(
                        out=fap(tmb), in_=bank[:, 0:512])),
                        reads=[breg], writes=freg(tmb), tag=f"ve{tt}_{oc}")
                    vb.dma("sp", (lambda e, tt=tt, oc=oc: e.dma_start(
                        out=v_d[tt * P: tt * P + P, 512 * (oc - 4): 512 * (oc - 4) + 512],
                        in_=fap(tmb))),
                        reads=freg(tmb), writes=[("dram", "v")], tag=f"vst{tt}_{oc}")

        kmf = falloc(DT, "kmf")
        mxb = A.alloc(D, "mxb")
        vb.op("dve", (lambda e: e.tensor_copy(out=mxb.ap(), in_=fap(mx))),
              reads=freg(mx), writes=mxb.reg(), tag="mxc")
        for c in range(DT):
            bank, breg = tpbank()
            vb.op("pe", (lambda e, bank=bank, c=c: e.transpose(
                out=bank[:, 0:128], in_=mxb.ap(c * P, c * P + P), identity=idn.ap())),
                reads=mxb.reg(c * P, c * P + P) + idn.reg(), writes=[breg],
                tag=f"mt{c}")
            vb.op("dve", (lambda e, bank=bank, c=c: e.reduce_max(
                out=fap(kmf, c, c + 1), in_=bank[:, 0:128], axis=AX.X)),
                reads=[breg], writes=freg(kmf, c, c + 1), tag=f"mr{c}")
        vb.dma("sp", (lambda e: e.dma_start(out=km_d, in_=fap(kmf))),
               reads=freg(kmf), writes=[("dram", "km")], tag="kms")
        stats = vb.finalize_and_emit(es, final_queue="pool")
    return nc, stats



def build_linattn_b_program():
    nc = bass.Bass("TRN2", target_bir_lowering=False, debug=False)
    k_dd = nc.dram_tensor("k_tm", [T, D], F32, kind="ExternalInput").ap()
    v_dd = nc.dram_tensor("v_tm", [T, D], F32, kind="ExternalInput").ap()
    km_d = nc.dram_tensor("kmax2", [P, 2 * DT], F32, kind="ExternalInput").ap()
    ones_d = nc.dram_tensor("cst_ones", [P, P], F32, kind="ExternalInput").ap()
    cn_d = nc.dram_tensor("cnum", [D, 65], F32, kind="ExternalOutput").ap()
    mrow_d = nc.dram_tensor("mrow_scr", [D], F32, kind="Internal")
    es = ExitStack()
    with es:
        A = Arena(nc, es, "arn", 100 * 1024, BF16)
        vb = VB(nc)

        def falloc(cols, tag):
            return A.alloc(2 * cols, tag)

        def fap(buf, c0=0, c1=None):
            c1 = buf.cols // 2 if c1 is None else c1
            return buf.arena[:, buf.off + 2 * c0: buf.off + 2 * c1].bitcast(F32)

        def fprt(buf, p0, p1, c0=0, c1=None):
            c1 = buf.cols // 2 if c1 is None else c1
            return buf.arena[p0:p1, buf.off + 2 * c0: buf.off + 2 * c1].bitcast(F32)

        def freg(buf, c0=0, c1=None):
            c1 = buf.cols // 2 if c1 is None else c1
            return buf.reg(2 * c0, 2 * c1)

        ones = falloc(P, "ones")
        kmb = falloc(2 * DT, "kmb")
        mfm = falloc(DT, "mfm")
        mbc = falloc(D, "mbc")
        expk = A.alloc(16 * D, "expk")
        vaug = A.alloc(16 * 16 * 65, "vaug")
        ktb = [falloc(D, f"kt{s}") for s in range(2)]
        vtb = [A.alloc(D, f"vt{s}") for s in range(2)]
        stg = falloc(P, "stg")

        pbf = [es.enter_context(nc.psum_tensor(f"pf{i}", [P, 512], F32)) for i in range(8)]
        PSF = [("psf", i) for i in range(8)]

        vb.dma("sp", lambda e: e.dma_start(out=fap(ones), in_=ones_d),
               writes=freg(ones), tag="ld_ones")
        vb.dma("sp", lambda e: e.dma_start(out=fap(kmb), in_=km_d),
               writes=freg(kmb), tag="ld_km")
        vb.op("dve", (lambda e: e.tensor_tensor(
            out=fap(mfm), in0=fap(kmb, 0, DT), in1=fap(kmb, DT, 2 * DT), op=ALU.max)),
            reads=freg(kmb), writes=freg(mfm), tag="mglob")
        vb.dma("sp", (lambda e: e.dma_start(
            out=mrow_d.ap().rearrange("(c p) -> p c", p=P), in_=fap(mfm),
            allow_slow_non_contiguous=True)),
            reads=freg(mfm), writes=[("dram", "mrow")], tag="mb_out")
        mr = falloc(D, "mr")
        vb.dma("sp", (lambda e: e.dma_start(
            out=fprt(mr, 0, 1), in_=mrow_d.ap().unsqueeze(0))),
            reads=[("dram", "mrow")], writes=freg(mr), tag="mb_in")
        for c in range(2):
            bank, breg = pbf[6], PSF[6]
            vb.op("pe", (lambda e, bank=bank, c=c: e.matmul(
                out=bank[:, 0:512], lhsT=fprt(ones, 0, 1),
                rhs=fprt(mr, 0, 1, c * 512, c * 512 + 512), start=True, stop=True)),
                reads=freg(mr) + freg(ones), writes=[breg], tag=f"mbc{c}")
            vb.op("dve", (lambda e, bank=bank, c=c: e.tensor_copy(
                out=fap(mbc, c * 512, c * 512 + 512), in_=bank[:, 0:512])),
                reads=[breg], writes=freg(mbc, c * 512, c * 512 + 512), tag=f"mbce{c}")

        for tt in range(16):
            s = tt % 2
            vb.dma("sp", (lambda e, s=s, tt=tt: e.dma_start(
                out=fap(ktb[s]), in_=k_dd[tt * P: tt * P + P, :])),
                reads=[("dram", "k")], writes=freg(ktb[s]), tag=f"lk{tt}")
            vb.dma("pool", (lambda e, s=s, tt=tt: e.dma_start(
                out=vtb[s].ap(), in_=v_dd[tt * P: tt * P + P, :])),
                reads=[("dram", "v")], writes=vtb[s].reg(), tag=f"lv{tt}")
            vb.op("dve", (lambda e, s=s: e.tensor_tensor(
                out=fap(ktb[s]), in0=fap(ktb[s]), in1=fap(mbc), op=ALU.subtract)),
                reads=freg(ktb[s]) + freg(mbc), writes=freg(ktb[s]), tag=f"ks{tt}")
            vb.op("act", (lambda e, s=s, tt=tt: e.activation(
                out=expk.ap(tt * D, (tt + 1) * D), in_=fap(ktb[s]), func=AF.Exp)),
                reads=freg(ktb[s]), writes=expk.reg(tt * D, (tt + 1) * D), tag=f"ke{tt}")
            vc0 = tt * 16 * 65
            vb.op("dve", (lambda e, s=s, vc0=vc0: e.tensor_copy(
                out=vaug.ap(vc0, vc0 + 16 * 65).rearrange("p (h d) -> p h d", d=65)[:, :, 0:64],
                in_=vtb[s].ap().rearrange("p (h d) -> p h d", d=64))),
                reads=vtb[s].reg(), writes=vaug.reg(vc0, vc0 + 16 * 65), tag=f"vc{tt}")
            vb.op("dve", (lambda e, vc0=vc0: e.memset(
                vaug.ap(vc0, vc0 + 16 * 65).rearrange("p (h d) -> p h d", d=65)[:, :, 64:65], 1.0)),
                reads=[], writes=vaug.reg(vc0, vc0 + 16 * 65), tag=f"vo{tt}")

        for h in range(16):
            bank, breg = pbf[h % 6], PSF[h % 6]
            for tt in range(16):
                vb.op("pe", (lambda e, bank=bank, h=h, tt=tt: e.matmul(
                    out=bank[0:64, 0:65],
                    lhsT=expk.ap(tt * D + h * 64, tt * D + h * 64 + 64),
                    rhs=vaug.ap(tt * 16 * 65 + h * 65, tt * 16 * 65 + h * 65 + 65),
                    start=(tt == 0), stop=(tt == 15))),
                    reads=expk.reg(tt * D + h * 64, tt * D + h * 64 + 64) + vaug.reg(tt * 16 * 65 + h * 65, tt * 16 * 65 + h * 65 + 65),
                    writes=[breg], tag=f"cm{h}_{tt}")
            vb.op("dve", (lambda e, bank=bank, h=h: e.tensor_copy(
                out=fap(stg, 0, 65)[0:64, :] if False else fprt(stg, 0, 64, 0, 65),
                in_=bank[0:64, 0:65])),
                reads=[breg], writes=freg(stg), tag=f"ce{h}")
            vb.dma("sp", (lambda e, h=h: e.dma_start(
                out=cn_d[h * 64: h * 64 + 64, :], in_=fprt(stg, 0, 64, 0, 65))),
                reads=freg(stg), writes=[("dram", "cn")], tag=f"cs{h}")
        stats = vb.finalize_and_emit(es, final_queue="pool")
    return nc, stats



def build_linattn_c_program():
    nc = bass.Bass("TRN2", target_bir_lowering=False, debug=False)
    qs_dd = nc.dram_tensor("qs_fm", [D, T], F32, kind="ExternalInput").ap()
    cn_d = nc.dram_tensor("cnum2", [2, D, 65], F32, kind="ExternalInput").ap()
    wo_d = nc.dram_tensor("w_out", [D, D], F32, kind="ExternalInput").ap()
    x_d = nc.dram_tensor("x_fm", [D, T], F32, kind="ExternalInput").ap()
    xo_d = nc.dram_tensor("xo_fm", [D, T], F32, kind="ExternalOutput").ap()
    es = ExitStack()
    with es:
        A = Arena(nc, es, "arn", 100 * 1024, BF16)
        vb = VB(nc)

        def falloc(cols, tag):
            return A.alloc(2 * cols, tag)

        def fap(buf, c0=0, c1=None):
            c1 = buf.cols // 2 if c1 is None else c1
            return buf.arena[:, buf.off + 2 * c0: buf.off + 2 * c1].bitcast(F32)

        def fprt(buf, p0, p1, c0=0, c1=None):
            c1 = buf.cols // 2 if c1 is None else c1
            return buf.arena[p0:p1, buf.off + 2 * c0: buf.off + 2 * c1].bitcast(F32)

        def freg(buf, c0=0, c1=None):
            c1 = buf.cols // 2 if c1 is None else c1
            return buf.reg(2 * c0, 2 * c1)

        ca = falloc(DT * 65, "ca")
        cb = falloc(DT * 65, "cb")
        rcp = falloc(DT, "rcp")
        cbf = A.alloc(DT * 64, "cbf")
        qsb = A.alloc(DT * T, "qsb")
        wob = A.alloc(DT * D, "wob")
        aof = A.alloc(DT * 512, "aof")
        xsb = [falloc(512, f"xs{s}") for s in range(2)]

        pbf = [es.enter_context(nc.psum_tensor(f"pf{i}", [P, 512], F32)) for i in range(8)]
        PSF = [("psf", i) for i in range(8)]
        rot = {"mm": 0}

        def mmbank(lo=0, n=6):
            i = lo + rot["mm"] % n
            rot["mm"] += 1
            return pbf[i], PSF[i]

        vb.dma("sp", (lambda e: e.dma_start(
            out=fap(ca).rearrange("p (c u) -> p c u", u=65),
            in_=cn_d[0].rearrange("(c p) u -> p c u", p=P))),
            reads=[("dram", "cn")], writes=freg(ca), tag="ld_ca")
        vb.dma("sp", (lambda e: e.dma_start(
            out=fap(cb).rearrange("p (c u) -> p c u", u=65),
            in_=cn_d[1].rearrange("(c p) u -> p c u", p=P))),
            reads=[("dram", "cn")], writes=freg(cb), tag="ld_cb")
        vb.dma("pool", (lambda e: e.dma_start(
            out=qsb.ap().rearrange("p (dt c) -> p dt c", c=T),
            in_=qs_dd.rearrange("(dt p) c -> p dt c", p=P))),
            reads=[("dram", "qs")], writes=qsb.reg(), tag="ld_qs")
        vb.dma("pool", (lambda e: e.dma_start(
            out=wob.ap().rearrange("p (kt o) -> p kt o", kt=DT),
            in_=wo_d.rearrange("(kt p) o -> p kt o", p=P))),
            reads=[("dram", "wo")], writes=wob.reg(), tag="ld_wo")
        vb.op("dve", (lambda e: e.tensor_tensor(
            out=fap(ca), in0=fap(ca), in1=fap(cb), op=ALU.add)),
            reads=freg(ca) + freg(cb), writes=freg(ca), tag="cadd")
        vb.op("dve", (lambda e: e.reciprocal(
            out=fap(rcp).rearrange("p c -> p c 1", c=DT) if False else fap(rcp),
            in_=fap(ca).rearrange("p (c u) -> p c u", u=65)[:, :, 64:65].rearrange("p c u -> p (c u)"))),
            reads=freg(ca), writes=freg(rcp), tag="crcp")
        vb.op("dve", (lambda e: e.tensor_tensor(
            out=cbf.ap().rearrange("p (c u) -> p c u", u=64),
            in0=fap(ca).rearrange("p (c u) -> p c u", u=65)[:, :, 0:64],
            in1=fap(rcp).rearrange("p (c u) -> p c u", u=1).broadcast_to([P, DT, 64]),
            op=ALU.mult)),
            reads=freg(ca) + freg(rcp), writes=cbf.reg(), tag="cdiv")

        for tc in range(4):
            c0 = tc * 512
            for h in range(16):
                bank, breg = mmbank()
                ch = h // 2
                r0 = (h % 2) * 64
                vb.op("pe", (lambda e, bank=bank, ch=ch, r0=r0, h=h, c0=c0: e.matmul(
                    out=bank[0:64, 0:512],
                    lhsT=cbf.prt(r0, r0 + 64, ch * P + (h % 2) * 64 if False else 0, 0) if False else
                         cbf.prt(r0, r0 + 64, (h // 2) * 128 + (h % 2) * 64, (h // 2) * 128 + (h % 2) * 64 + 64),
                    rhs=qsb.prt(r0, r0 + 64, (h // 2) * T + c0, (h // 2) * T + c0 + 512),
                    start=True, stop=True)),
                    reads=cbf.reg() + qsb.reg((h // 2) * T + c0, (h // 2) * T + c0 + 512),
                    writes=[breg], tag=f"ao{tc}_{h}")
                dst0 = (h // 2) * 512
                vb.op("dve", (lambda e, bank=bank, h=h, dst0=dst0: e.tensor_copy(
                    out=aof.prt((h % 2) * 64, (h % 2) * 64 + 64, dst0, dst0 + 512),
                    in_=bank[0:64, 0:512])),
                    reads=[breg], writes=aof.reg(dst0, dst0 + 512), tag=f"ae{tc}_{h}")
            for ot in range(DT):
                bank, breg = mmbank(6, 2)
                for kt in range(DT):
                    vb.op("pe", (lambda e, bank=bank, ot=ot, kt=kt: e.matmul(
                        out=bank[:, 0:512],
                        lhsT=wob.ap(kt * D + ot * P, kt * D + ot * P + P),
                        rhs=aof.ap(kt * 512, (kt + 1) * 512),
                        start=(kt == 0), stop=(kt == DT - 1))),
                        reads=wob.reg(kt * D + ot * P, kt * D + ot * P + P) + aof.reg(kt * 512, (kt + 1) * 512),
                        writes=[breg], tag=f"wo{tc}_{ot}_{kt}")
                s = ot % 2
                vb.dma("sp", (lambda e, s=s, ot=ot, c0=c0: e.dma_start(
                    out=fap(xsb[s]), in_=x_d[ot * P: ot * P + P, c0: c0 + 512])),
                    reads=[("dram", "x")], writes=freg(xsb[s]), tag=f"lx{tc}_{ot}")
                vb.op("dve", (lambda e, bank=bank, s=s: e.scalar_tensor_tensor(
                    out=fap(xsb[s]), in0=bank[:, 0:512], scalar=1.0, in1=fap(xsb[s]),
                    op0=ALU.mult, op1=ALU.add)),
                    reads=[breg] + freg(xsb[s]), writes=freg(xsb[s]), tag=f"xa{tc}_{ot}")
                vb.dma("sp", (lambda e, s=s, ot=ot, c0=c0: e.dma_start(
                    out=xo_d[ot * P: ot * P + P, c0: c0 + 512], in_=fap(xsb[s]))),
                    reads=freg(xsb[s]), writes=[("dram", "xo")], tag=f"sx{tc}_{ot}")
        stats = vb.finalize_and_emit(es, final_queue="pool")
    return nc, stats



def build_l1qkv_program():
    nc = bass.Bass("TRN2", target_bir_lowering=False, debug=False)
    x_d = nc.dram_tensor("x_fm", [D, T], F32, kind="ExternalInput").ap()
    wq_d = nc.dram_tensor("wqT", [D, D], F32, kind="ExternalInput").ap()
    wk_d = nc.dram_tensor("wkT", [D, D], F32, kind="ExternalInput").ap()
    wv_d = nc.dram_tensor("wvT", [D, D], F32, kind="ExternalInput").ap()
    bq_d = nc.dram_tensor("bq", [P, DT], F32, kind="ExternalInput").ap()
    bk_d = nc.dram_tensor("bk", [P, DT], F32, kind="ExternalInput").ap()
    bv_d = nc.dram_tensor("bv", [P, DT], F32, kind="ExternalInput").ap()
    ones_d = nc.dram_tensor("cst_ones", [P, P], F32, kind="ExternalInput").ap()
    q_o = nc.dram_tensor("q_fm", [D, T], F32, kind="ExternalOutput").ap()
    k_o = nc.dram_tensor("k_fm", [D, T], F32, kind="ExternalOutput").ap()
    v_o = nc.dram_tensor("v_fm", [D, T], F32, kind="ExternalOutput").ap()
    es = ExitStack()
    with es:
        A = Arena(nc, es, "arn", 100 * 1024, BF16)
        vb = VB(nc)

        def falloc(cols, tag):
            return A.alloc(2 * cols, tag)

        def fap(buf, c0=0, c1=None):
            c1 = buf.cols // 2 if c1 is None else c1
            return buf.arena[:, buf.off + 2 * c0: buf.off + 2 * c1].bitcast(F32)

        def fprt(buf, p0, p1, c0=0, c1=None):
            c1 = buf.cols // 2 if c1 is None else c1
            return buf.arena[p0:p1, buf.off + 2 * c0: buf.off + 2 * c1].bitcast(F32)

        def freg(buf, c0=0, c1=None):
            c1 = buf.cols // 2 if c1 is None else c1
            return buf.reg(2 * c0, 2 * c1)

        cstb = falloc(16, "consts")
        ones = falloc(P, "ones")
        xlnb = A.alloc(DT * T, "xlnb")
        wres = [A.alloc(DT * D, f"w{i}") for i in range(3)]
        bcols = [falloc(DT, f"b{i}") for i in range(3)]
        stg = [falloc(512, f"stg{s}") for s in range(2)]
        cst_map = {}

        def cst(value):
            value = float(value)
            if value not in cst_map:
                i = len(cst_map)
                ap = fap(cstb, i, i + 1)
                rg = freg(cstb, i, i + 1)
                vb.op("dve", (lambda e, ap=ap, value=value: e.memset(ap, value)),
                      reads=[], writes=rg, tag=f"cst{value}")
                cst_map[value] = (ap, rg)
            return cst_map[value]

        pbf = [es.enter_context(nc.psum_tensor(f"pf{i}", [P, 512], F32)) for i in range(6)]
        PSF = [("psf", i) for i in range(6)]
        rot = {"mm": 0}

        def mmbank():
            i = rot["mm"] % 4
            rot["mm"] += 1
            return pbf[i], PSF[i]

        vb.dma("sp", lambda e: e.dma_start(out=fap(ones), in_=ones_d),
               writes=freg(ones), tag="ld_ones")
        for i, (wd, bd) in enumerate(((wq_d, bq_d), (wk_d, bk_d), (wv_d, bv_d))):
            vb.dma("pool", (lambda e, i=i, wd=wd: e.dma_start(
                out=wres[i].ap().rearrange("p (dt o) -> p dt o", o=D),
                in_=wd.rearrange("(dt p) o -> p dt o", p=P))),
                writes=wres[i].reg(), tag=f"ld_w{i}")
            vb.dma("sp", (lambda e, i=i, bd=bd: e.dma_start(out=fap(bcols[i]), in_=bd)),
                   writes=freg(bcols[i]), tag=f"ld_b{i}")

        A.push()
        xres = falloc(DT * T, "xres")
        abp = falloc(T, "Ab")
        bbp = falloc(T, "Bb")
        rows = falloc(2 * T, "rows")
        sqs = falloc(512, "sqs")
        vb.dma("sp", (lambda e: e.dma_start(
            out=fap(xres).rearrange("p (dt c) -> p dt c", c=T),
            in_=x_d.rearrange("(dt p) c -> p dt c", p=P))),
            reads=[("dram", "x")], writes=freg(xres), tag="ld_x")

        class _BV:
            def __init__(self, b):
                self.b = b
                self.cols = b.cols // 2

            def ap(self, c0=0, c1=None):
                return fap(self.b, c0, c1)

            def prt(self, p0, p1, c0=0, c1=None):
                return fprt(self.b, p0, p1, c0, c1)

            def reg(self, c0=0, c1=None):
                return freg(self.b, c0, c1)

        class _CA:
            def col(self, v):
                return cst(v)

        emit_ln_fm(vb, pbf[:4], PSF[:4], _BV(xres), xlnb, _BV(rows), _BV(abp), _BV(bbp),
                   ones_col=fprt(ones, 0, P, 0, 1), ones_col_reg=freg(ones, 0, 1),
                   ones_row=fprt(ones, 0, 1, 0, P), ones_row_reg=freg(ones, 0, P),
                   TT=T, eps=1e-5, consts=_CA(), sq_tmp=_BV(sqs), prefix="qln")
        A.pop()

        outs = [q_o, k_o, v_o]
        for i in range(3):
            for ot in range(DT):
                for tc in range(4):
                    c0 = tc * 512
                    bank, breg = mmbank()
                    for dt in range(DT):
                        wc0 = dt * D + ot * P
                        vb.op("pe", (lambda e, bank=bank, i=i, wc0=wc0, dt=dt, c0=c0:
                              e.matmul(out=bank[:, 0:512],
                                       lhsT=wres[i].ap(wc0, wc0 + P),
                                       rhs=xlnb.ap(dt * T + c0, dt * T + c0 + 512),
                                       start=(dt == 0), stop=(dt == DT - 1))),
                              reads=wres[i].reg(wc0, wc0 + P) + xlnb.reg(dt * T + c0, dt * T + c0 + 512),
                              writes=[breg], tag=f"qk{i}_{ot}_{tc}")
                    s = tc % 2
                    vb.op("act", (lambda e, bank=bank, i=i, ot=ot, s=s: e.activation(
                        out=fap(stg[s]), in_=bank[:, 0:512], func=AF.Identity,
                        bias=fap(bcols[i], ot, ot + 1))),
                        reads=[breg] + freg(bcols[i]), writes=freg(stg[s]),
                        tag=f"qe{i}_{ot}_{tc}")
                    vb.dma("sp", (lambda e, i=i, ot=ot, c0=c0, s=s: e.dma_start(
                        out=outs[i][ot * P: ot * P + P, c0: c0 + 512], in_=fap(stg[s]))),
                        reads=freg(stg[s]), writes=[("dram", f"o{i}")], tag=f"qs{i}_{ot}_{tc}")
        stats = vb.finalize_and_emit(es, final_queue="pool")
    return nc, stats



L1W = 3584
L1CB = 640


def host_l1_selAB(coreh):
    selA = np.zeros((1, 128), np.float32)
    selB = np.zeros((1, 128), np.float32)
    for it in range(16):
        for kb in range(8):
            u = 4 * kb - it + 15
            if u in range(11, 17) or u in range(27, 33):
                continue
            d0 = 512 * kb - 128 * it - 2048 * coreh
            if d0 + 511 <= -15:
                selA[0, it * 8 + kb] = 1.0
            elif d0 - 127 >= 15:
                selB[0, it * 8 + kb] = 1.0
            else:
                raise AssertionError("non-saturated off-band block")
    return selA, selB


def host_l1_shift(coreh):
    return np.full((1, 1), -float(L1CB + 2048 * coreh), np.float32)


def host_flip_eye():
    return np.eye(P, dtype=np.float32)[::-1].copy()


def build_l1core_program():
    nc = bass.Bass("TRN2", target_bir_lowering=False, debug=False)
    q_d = nc.dram_tensor("q_fm", [D, T], F32, kind="ExternalInput").ap()
    k_d = nc.dram_tensor("k_fm", [D, 2 * T], F32, kind="ExternalInput").ap()
    v_d = nc.dram_tensor("v_tm", [2 * T, D], F32, kind="ExternalInput").ap()
    x_d = nc.dram_tensor("x_fm", [D, T], F32, kind="ExternalInput").ap()
    wo_d = nc.dram_tensor("woT", [D, D], F32, kind="ExternalInput").ap()
    bo_d = nc.dram_tensor("bo", [P, DT], F32, kind="ExternalInput").ap()
    rb_d = nc.dram_tensor("relb", [31, 16], F32, kind="ExternalInput").ap()
    sa_d = nc.dram_tensor("selA", [1, 128], F32, kind="ExternalInput").ap()
    sb_d = nc.dram_tensor("selB", [1, 128], F32, kind="ExternalInput").ap()
    sh_d = nc.dram_tensor("shiftv", [1, 1], F32, kind="ExternalInput").ap()
    ones_d = nc.dram_tensor("cst_ones", [P, P], F32, kind="ExternalInput").ap()
    eye_d = nc.dram_tensor("cst_eye", [P, P], F32, kind="ExternalInput").ap()
    fey_d = nc.dram_tensor("cst_fey", [P, P], F32, kind="ExternalInput").ap()
    xo_d = nc.dram_tensor("xo_fm", [D, T], F32, kind="ExternalOutput").ap()
    mst_d = nc.dram_tensor("master_scr", [16, L1W], F32, kind="Internal")

    es = ExitStack()
    with es:
        A = Arena(nc, es, "arn", 103 * 1024, BF16)
        vb = VB(nc)

        def falloc(cols, tag):
            return A.alloc(2 * cols, tag)

        def fap(buf, c0=0, c1=None):
            c1 = buf.cols // 2 if c1 is None else c1
            return buf.arena[:, buf.off + 2 * c0: buf.off + 2 * c1].bitcast(F32)

        def fprt(buf, p0, p1, c0=0, c1=None):
            c1 = buf.cols // 2 if c1 is None else c1
            return buf.arena[p0:p1, buf.off + 2 * c0: buf.off + 2 * c1].bitcast(F32)

        def freg(buf, c0=0, c1=None):
            c1 = buf.cols // 2 if c1 is None else c1
            return buf.reg(2 * c0, 2 * c1)

        cstb = falloc(16, "consts")
        ones = falloc(P, "ones")
        fey = A.alloc(P, "fey")
        idn_b = A.alloc(P, "idnb")
        kres = A.alloc(DT * 2 * T, "kres")
        vres = A.alloc(32 * D, "vres")
        wob = A.alloc(DT * D, "wob")
        boc = falloc(DT, "boc")
        relb = falloc(16, "relbv")
        rcol = falloc(1, "rcol")
        shc = falloc(1, "shc")
        sel = falloc(2 * P, "sel")
        CT = A.alloc(128 * 16, "CT")
        c0p = falloc(16, "c0p")
        c30p = falloc(16, "c30p")
        qtb = [A.alloc(DT * P, f"qt{s}") for s in range(2)]
        bnd = A.alloc(2 * 512, "bnd")
        prbs = A.alloc(8 * 512, "prbs")
        ptb = A.alloc(4 * 1024, "ptb")
        nmc = falloc(3 * 16, "nmc")
        dnm = falloc(16, "dnm")
        aot = A.alloc(D, "aot")
        aof = A.alloc(DT * P, "aof")
        stg = [falloc(512, f"stg{s}") for s in range(2)]
        itr = falloc(512, "itr")
        msk = falloc(512, "msk2")

        cst_map = {}

        def cst(value):
            value = float(value)
            if value not in cst_map:
                i = len(cst_map)
                ap = fap(cstb, i, i + 1)
                rg = freg(cstb, i, i + 1)
                vb.op("dve", (lambda e, ap=ap, value=value: e.memset(ap, value)),
                      reads=[], writes=rg, tag=f"cst{value}")
                cst_map[value] = (ap, rg)
            return cst_map[value]

        pbf = [es.enter_context(nc.psum_tensor(f"pf{i}", [P, 512], F32)) for i in range(6)]
        PSF = [("psf", i) for i in range(6)]
        pbb = [es.enter_context(nc.psum_tensor(f"pb{i}", [P, 1024], BF16)) for i in range(2)]
        PSB = [("psb", i) for i in range(2)]
        rot = {"mm": 0, "sim": 0, "tp": 0, "av": 0}

        def simbank():
            i = rot["sim"] % 2
            rot["sim"] += 1
            return pbf[i], PSF[i]

        def avbank():
            i = 2 + rot["av"] % 2
            rot["av"] += 1
            return pbf[i], PSF[i]

        def mmbank():
            i = 4 + rot["mm"] % 2
            rot["mm"] += 1
            return pbf[i], PSF[i]

        def tpbank():
            i = rot["tp"] % 2
            rot["tp"] += 1
            return pbb[i], PSB[i]

        vb.dma("sp", lambda e: e.dma_start(out=fap(ones), in_=ones_d),
               writes=freg(ones), tag="ld_ones")
        vb.dma("pool", lambda e: e.dma_start(out=fey.ap(), in_=fey_d),
               writes=fey.reg(), tag="ld_fey")
        vb.dma("pool", lambda e: e.dma_start(out=idn_b.ap(), in_=eye_d),
               writes=idn_b.reg(), tag="ld_idnb")
        vb.dma("pool", (lambda e: e.dma_start(
            out=kres.ap().rearrange("p (dt c) -> p dt c", c=2 * T),
            in_=k_d.rearrange("(dt p) c -> p dt c", p=P))),
            writes=kres.reg(), tag="ld_k")
        vb.dma("pool", (lambda e: e.dma_start(
            out=vres.ap().rearrange("p (tt c) -> p tt c", c=D),
            in_=v_d.rearrange("(tt p) c -> p tt c", p=P))),
            writes=vres.reg(), tag="ld_v")
        vb.dma("pool", (lambda e: e.dma_start(
            out=wob.ap().rearrange("p (kt o) -> p kt o", kt=DT),
            in_=wo_d.rearrange("(kt p) o -> p kt o", p=P))),
            writes=wob.reg(), tag="ld_wo")
        vb.dma("sp", lambda e: e.dma_start(out=fap(boc), in_=bo_d),
               writes=freg(boc), tag="ld_bo")
        vb.dma("sp", (lambda e: e.dma_start(out=fprt(relb, 0, 31), in_=rb_d)),
               writes=freg(relb), tag="ld_rb")
        relb0 = falloc(16, "relb0")
        relb30 = falloc(16, "relb30")
        vb.dma("sp", (lambda e: e.dma_start(out=fprt(relb0, 0, 1), in_=rb_d[0:1, :])),
               writes=freg(relb0), tag="ld_rb0")
        vb.dma("sp", (lambda e: e.dma_start(out=fprt(relb30, 0, 1), in_=rb_d[30:31, :])),
               writes=freg(relb30), tag="ld_rb30")
        vb.dma("sp", (lambda e: e.dma_start(out=fprt(sel, 0, 1, 0, P), in_=sa_d)),
               writes=freg(sel), tag="ld_sa")
        vb.dma("sp", (lambda e: e.dma_start(out=fprt(sel, 0, 1, P, 2 * P), in_=sb_d)),
               writes=freg(sel), tag="ld_sb")
        vb.dma("sp", (lambda e: e.dma_start(out=fprt(shc, 0, 1), in_=sh_d)),
               writes=freg(shc), tag="ld_sh")

        vb.op("pool", (lambda e: e.iota(out=fap(rcol).bitcast(F32)[0:31, :] if False else fprt(rcol, 0, 31),
                                        pattern=[[0, 1]], base=-15, channel_multiplier=1,
                                        allow_small_or_imprecise_dtypes=True)),
              writes=freg(rcol), tag="rcol")
        shb = falloc(1, "shb")
        bank, breg = mmbank()
        vb.op("pe", (lambda e, bank=bank: e.matmul(
            out=bank[0:31, 0:1], lhsT=fprt(ones, 0, 1, 0, 31), rhs=fprt(shc, 0, 1),
            start=True, stop=True)),
            reads=freg(shc) + freg(ones), writes=[breg], tag="shbm")
        vb.op("dve", (lambda e, bank=bank: e.tensor_copy(
            out=fprt(shb, 0, 31), in_=bank[0:31, 0:1])),
            reads=[breg], writes=freg(shb), tag="shbe")

        for ch in range(L1W // 512):
            vb.op("pool", (lambda e, ch=ch: e.iota(
                out=fprt(itr, 0, 31), pattern=[[1, 512]], base=ch * 512,
                channel_multiplier=0, allow_small_or_imprecise_dtypes=True)),
                writes=freg(itr), tag=f"mi{ch}")
            vb.op("dve", (lambda e: e.tensor_scalar(
                out=fprt(itr, 0, 31), in0=fprt(itr, 0, 31), scalar1=fprt(shb, 0, 31),
                scalar2=15.0, op0=ALU.add, op1=ALU.min)),
                reads=freg(itr) + freg(shb), writes=freg(itr), tag=f"ms{ch}")
            vb.op("dve", (lambda e: e.tensor_scalar(
                out=fprt(itr, 0, 31), in0=fprt(itr, 0, 31), scalar1=-15.0,
                scalar2=None, op0=ALU.max)),
                reads=freg(itr), writes=freg(itr), tag=f"mc{ch}")
            vb.op("dve", (lambda e: e.tensor_scalar(
                out=fprt(msk, 0, 31), in0=fprt(itr, 0, 31), scalar1=fprt(rcol, 0, 31),
                scalar2=None, op0=ALU.is_equal)),
                reads=freg(itr) + freg(rcol), writes=freg(msk), tag=f"me{ch}")
            bank2, breg2 = mmbank()
            vb.op("pe", (lambda e, bank2=bank2: e.matmul(
                out=bank2[0:16, 0:512], lhsT=fprt(relb, 0, 31), rhs=fprt(msk, 0, 31),
                start=True, stop=True)),
                reads=freg(msk) + freg(relb), writes=[breg2], tag=f"mm{ch}")
            vb.op("dve", (lambda e, bank2=bank2: e.tensor_scalar(
                out=fprt(itr, 0, 16), in0=bank2[0:16, 0:512], scalar1=8.0,
                scalar2=None, op0=ALU.mult)),
                reads=[breg2], writes=freg(itr), tag=f"mev{ch}")
            vb.dma("sp", (lambda e, ch=ch: e.dma_start(
                out=mst_d.ap()[:, ch * 512: ch * 512 + 512], in_=fprt(itr, 0, 16))),
                reads=freg(itr), writes=[("dram", "mst")], tag=f"mso{ch}")

        for nm_, srcb, dstb in (("c0", relb0, c0p), ("c30", relb30, c30p)):
            bank3, breg3 = mmbank()
            vb.op("pe", (lambda e, bank3=bank3, srcb=srcb: e.matmul(
                out=bank3[:, 0:16], lhsT=fprt(ones, 0, 1), rhs=fprt(srcb, 0, 1),
                start=True, stop=True)),
                reads=freg(srcb) + freg(ones), writes=[breg3], tag=f"{nm_}m")
            vb.op("dve", (lambda e, bank3=bank3, dstb=dstb: e.tensor_copy(
                out=fap(dstb), in_=bank3[:, 0:16])),
                reads=[breg3], writes=freg(dstb), tag=f"{nm_}e")
        selp = falloc(2 * P, "selp")
        for i in range(2):
            bank4, breg4 = mmbank()
            vb.op("pe", (lambda e, bank4=bank4, i=i: e.matmul(
                out=bank4[:, 0:128], lhsT=fprt(ones, 0, 1),
                rhs=fprt(sel, 0, 1, i * P, (i + 1) * P), start=True, stop=True)),
                reads=freg(sel) + freg(ones), writes=[breg4], tag=f"selm{i}")
            vb.op("dve", (lambda e, bank4=bank4, i=i: e.tensor_copy(
                out=fap(selp, i * P, (i + 1) * P), in_=bank4[:, 0:128])),
                reads=[breg4], writes=freg(selp), tag=f"sele{i}")
        for ik in range(128):
            vb.op("dve", (lambda e, ik=ik: e.tensor_scalar(
                out=fap(cstb, 0, 0) if False else CT.ap(ik * 16, ik * 16 + 16),
                in0=fap(c30p), scalar1=fap(selp, P + ik, P + ik + 1),
                scalar2=None, op0=ALU.mult)),
                reads=freg(c30p) + freg(selp), writes=CT.reg(ik * 16, ik * 16 + 16),
                tag=f"ct1_{ik}")
            vb.op("dve", (lambda e, ik=ik: e.scalar_tensor_tensor(
                out=CT.ap(ik * 16, ik * 16 + 16), in0=fap(c0p),
                scalar=fap(selp, ik, ik + 1), in1=CT.ap(ik * 16, ik * 16 + 16),
                op0=ALU.mult, op1=ALU.add)),
                reads=freg(c0p) + freg(selp) + CT.reg(ik * 16, ik * 16 + 16),
                writes=CT.reg(ik * 16, ik * 16 + 16), tag=f"ct2_{ik}")

        eighth, eighth_rg = cst(0.125)

        BAND_U = set(range(11, 17)) | set(range(27, 33))
        for it in range(16):
            s = it % 2
            vb.dma("pool", (lambda e, s=s, it=it: e.dma_start(
                out=qtb[s].ap().rearrange("p (dt c) -> p dt c", c=P),
                in_=q_d[:, it * P: it * P + P].rearrange("(dt p) c -> p dt c", p=P))),
                reads=[("dram", "q")], writes=qtb[s].reg(), tag=f"lq{it}")
            band_kbs = [kb for kb in range(8) if (4 * kb - it + 15) in BAND_U]
            bslot = {"i": 0}
            for h in range(16):
                hp, r0 = h // 2, (h % 2) * 64
                nm = fap(nmc, 0, 8)
                nmr = freg(nmc, 0, 8)
                dn = fap(dnm, 0, 8)
                dnr = freg(dnm, 0, 8)
                for kb in range(8):
                    bank, breg = simbank()
                    vb.op("pe", (lambda e, bank=bank, s=s, hp=hp, r0=r0, h=h, kb=kb:
                          e.matmul(out=bank[:, 0:512],
                                   lhsT=qtb[s].prt(r0, r0 + 64, hp * P, hp * P + P),
                                   rhs=kres.prt(r0, r0 + 64, hp * 2 * T + kb * 512, hp * 2 * T + kb * 512 + 512),
                                   start=True, stop=True)),
                          reads=qtb[s].reg(hp * P, hp * P + P) + kres.reg(hp * 2 * T + kb * 512, hp * 2 * T + kb * 512 + 512),
                          writes=[breg], tag=f"sc{it}_{h}_{kb}")
                    if kb in band_kbs:
                        si = bslot["i"] % 2
                        bslot["i"] += 1
                        w0 = L1CB + 512 * kb - 128 * it - 127
                        vb.dma("pool", (lambda e, si=si, w0=w0, h=h: e.dma_start(
                            out=bnd.ap(si * 512, si * 512 + 512),
                            in_=bass_types_AP(tensor=mst_d.ap().tensor,
                                              offset=h * L1W + w0,
                                              ap=[[1, 128], [1, 512]]))),
                            reads=[("dram", "mst")], writes=bnd.reg(si * 512, si * 512 + 512),
                            tag=f"lb{it}_{h}_{kb}")
                        vb.op("pe", (lambda e, bank=bank, si=si: e.matmul(
                            out=bank[:, 0:512], lhsT=fey.ap(),
                            rhs=bnd.ap(si * 512, si * 512 + 512),
                            start=False, stop=True, skip_group_check=True)),
                            reads=fey.reg() + bnd.reg(si * 512, si * 512 + 512),
                            writes=[breg], tag=f"bd{it}_{h}_{kb}")
                    vb.op("dve", (lambda e, bank=bank, nm=nm, kb=kb: e.reduce_max(
                        out=nm[:, kb: kb + 1], in_=bank[:, 0:512], axis=AX.X,
                        negate=True)),
                        reads=[breg], writes=nmr, tag=f"nx{it}_{h}_{kb}")
                    bcol = fap(nmc, 8 + kb, 9 + kb)
                    vb.op("dve", (lambda e, nm=nm, kb=kb, bcol=bcol: e.tensor_scalar(
                        out=bcol, in0=nm[:, kb: kb + 1], scalar1=0.125, scalar2=None,
                        op0=ALU.mult)),
                        reads=nmr, writes=freg(nmc, 8, 16), tag=f"n8{it}_{h}_{kb}")
                    vb.op("act", (lambda e, bank=bank, kb=kb, bcol=bcol: e.activation(
                        out=prbs.ap(kb * 512, kb * 512 + 512), in_=bank[:, 0:512],
                        func=AF.Exp, bias=bcol, scale=eighth,
                        accum_out=dn[:, kb: kb + 1])),
                        reads=[breg] + freg(nmc, 8, 16) + eighth_rg,
                        writes=prbs.reg(kb * 512, kb * 512 + 512) + freg(dnm, 0, 8),
                        tag=f"ex{it}_{h}_{kb}")
                    ctc = CT.ap((it * 8 + kb) * 16 + h, (it * 8 + kb) * 16 + h + 1)
                    vb.op("dve", (lambda e, nm=nm, kb=kb, bcol=bcol, ctc=ctc:
                          e.tensor_tensor(out=nm[:, kb: kb + 1], in0=bcol, in1=ctc,
                                          op=ALU.subtract)),
                          reads=freg(nmc, 8, 16) + CT.reg((it * 8 + kb) * 16, (it * 8 + kb) * 16 + 16),
                          writes=nmr, tag=f"nc{it}_{h}_{kb}")
                nmg = fap(nmc, 16, 17)
                nmgr = freg(nmc, 16, 17)
                vb.op("dve", (lambda e, nm=nm, nmg=nmg: e.tensor_reduce(
                    out=nmg, in_=nm, axis=AX.X, op=ALU.min)),
                    reads=nmr, writes=nmgr, tag=f"ng{it}_{h}")
                fct = fap(nmc, 24, 32)
                fctr = freg(nmc, 24, 32)
                vb.op("dve", (lambda e, nm=nm, nmg=nmg, fct=fct: e.tensor_scalar(
                    out=fct, in0=nm, scalar1=-1.0, scalar2=nmg, op0=ALU.mult,
                    op1=ALU.add)),
                    reads=nmr + nmgr, writes=fctr, tag=f"fd{it}_{h}")
                vb.op("act", (lambda e, fct=fct: e.activation(
                    out=fct, in_=fct, func=AF.Exp)),
                    reads=fctr, writes=fctr, tag=f"fe{it}_{h}")
                vb.op("dve", (lambda e, dn=dn, fct=fct: e.tensor_tensor(
                    out=dn, in0=dn, in1=fct, op=ALU.mult)),
                    reads=dnr + fctr, writes=dnr, tag=f"dr{it}_{h}")
                dg = fap(dnm, 8, 9)
                dgr = freg(dnm, 8, 9)
                vb.op("dve", (lambda e, dn=dn, dg=dg: e.reduce_sum(
                    out=dg, in_=dn, axis=AX.X)),
                    reads=dnr, writes=dgr, tag=f"ds{it}_{h}")
                vb.op("dve", (lambda e, dg=dg: e.reciprocal(out=dg, in_=dg)),
                    reads=dgr, writes=dgr, tag=f"dc{it}_{h}")
                for kb in range(8):
                    vb.op("dve", (lambda e, fct=fct, kb=kb: e.tensor_scalar(
                        out=prbs.ap(kb * 512, kb * 512 + 512),
                        in0=prbs.ap(kb * 512, kb * 512 + 512),
                        scalar1=fct[:, kb: kb + 1], scalar2=None, op0=ALU.mult)),
                        reads=prbs.reg(kb * 512, kb * 512 + 512) + fctr,
                        writes=prbs.reg(kb * 512, kb * 512 + 512), tag=f"pr{it}_{h}_{kb}")
                abank, abreg = avbank()
                for grp in range(4):
                    tbank, tbreg = tpbank()
                    for cc in range(8):
                        ci = grp * 8 + cc
                        vb.op("pe", (lambda e, tbank=tbank, ci=ci, cc=cc: e.transpose(
                            out=tbank[:, cc * 128: cc * 128 + 128],
                            in_=prbs.ap(ci * 128, ci * 128 + 128),
                            identity=idn_b.ap())),
                            reads=prbs.reg(ci * 128, ci * 128 + 128) + idn_b.reg(),
                            writes=[tbreg], tag=f"pt{it}_{h}_{ci}")
                    vb.op("dve", (lambda e, tbank=tbank, grp=grp: e.tensor_copy(
                        out=ptb.ap(grp * 1024, grp * 1024 + 1024), in_=tbank[:, 0:1024])),
                        reads=[tbreg], writes=ptb.reg(grp * 1024, grp * 1024 + 1024),
                        tag=f"pe{it}_{h}_{grp}")
                for jt in range(32):
                    vb.op("pe", (lambda e, abank=abank, jt=jt, h=h: e.matmul(
                        out=abank[:, 0:64],
                        lhsT=ptb.ap(jt * 128, jt * 128 + 128),
                        rhs=vres.ap(jt * D + h * 64, jt * D + h * 64 + 64),
                        start=(jt == 0), stop=(jt == 31))),
                        reads=ptb.reg(jt * 128, jt * 128 + 128) + vres.reg(jt * D + h * 64, jt * D + h * 64 + 64),
                        writes=[abreg], tag=f"av{it}_{h}_{jt}")
                vb.op("dve", (lambda e, abank=abank, dg=dg, h=h: e.tensor_scalar(
                    out=aot.ap(h * 64, h * 64 + 64), in0=abank[:, 0:64],
                    scalar1=dg, scalar2=None, op0=ALU.mult)),
                    reads=[abreg] + dgr, writes=aot.reg(h * 64, h * 64 + 64),
                    tag=f"ao{it}_{h}")
            for dt in range(DT):
                tbank, tbreg = tpbank()
                vb.op("pe", (lambda e, tbank=tbank, dt=dt: e.transpose(
                    out=tbank[:, 0:128], in_=aot.ap(dt * 128, dt * 128 + 128),
                    identity=idn_b.ap())),
                    reads=aot.reg(dt * 128, dt * 128 + 128) + idn_b.reg(),
                    writes=[tbreg], tag=f"at{it}_{dt}")
                vb.op("dve", (lambda e, tbank=tbank, dt=dt: e.tensor_copy(
                    out=aof.ap(dt * 128, dt * 128 + 128), in_=tbank[:, 0:128])),
                    reads=[tbreg], writes=aof.reg(dt * 128, dt * 128 + 128),
                    tag=f"ae{it}_{dt}")
            for ot in range(DT):
                bank, breg = mmbank()
                for kt in range(DT):
                    vb.op("pe", (lambda e, bank=bank, ot=ot, kt=kt: e.matmul(
                        out=bank[:, 0:128],
                        lhsT=wob.ap(kt * D + ot * P, kt * D + ot * P + P),
                        rhs=aof.ap(kt * P, kt * P + P),
                        start=(kt == 0), stop=(kt == DT - 1))),
                        reads=wob.reg(kt * D + ot * P, kt * D + ot * P + P) + aof.reg(kt * P, kt * P + P),
                        writes=[breg], tag=f"wo{it}_{ot}_{kt}")
                ss = ot % 2
                vb.dma("sp", (lambda e, ss=ss, ot=ot, it=it: e.dma_start(
                    out=fap(stg[ss], 0, 128), in_=x_d[ot * P: ot * P + P, it * P: it * P + P])),
                    reads=[("dram", "x")], writes=freg(stg[ss]), tag=f"lx{it}_{ot}")
                vb.op("dve", (lambda e, bank=bank, ss=ss, ot=ot: e.scalar_tensor_tensor(
                    out=fap(stg[ss], 0, 128), in0=bank[:, 0:128],
                    scalar=fap(boc, ot, ot + 1), in1=fap(stg[ss], 0, 128),
                    op0=ALU.add, op1=ALU.add)),
                    reads=[breg] + freg(boc) + freg(stg[ss]), writes=freg(stg[ss]),
                    tag=f"xa{it}_{ot}")
                vb.dma("sp", (lambda e, ss=ss, ot=ot, it=it: e.dma_start(
                    out=xo_d[ot * P: ot * P + P, it * P: it * P + P],
                    in_=fap(stg[ss], 0, 128))),
                    reads=freg(stg[ss]), writes=[("dram", "xo")], tag=f"sx{it}_{ot}")

        stats = vb.finalize_and_emit(es, final_queue="pool")
    return nc, stats


def host_vec_fm(v):
    return np.ascontiguousarray(v.reshape(-1, P).T)



_PROGRAMS = {}


def _prog(name, builder):
    if name not in _PROGRAMS:
        _PROGRAMS[name] = builder()[0] if isinstance(builder(), tuple) else builder()
    return _PROGRAMS[name]


def _prog2(name, builder):
    if name not in _PROGRAMS:
        r = builder()
        _PROGRAMS[name] = r[0] if isinstance(r, tuple) else r
    return _PROGRAMS[name]


def _shards(X):
    out = []
    for c in range(8):
        b, h = c // 2, c % 2
        out.append(np.ascontiguousarray(X[b, h * T:(h + 1) * T].T))
    return out


def _unshard(parts):
    X = np.empty((B_, SEQ, D), np.float32)
    for c in range(8):
        b, h = c // 2, c % 2
        X[b, h * T:(h + 1) * T] = parts[c].T
    return X


def _launch(nc, in_maps, outs):
    res = run_bass_kernel_spmd(nc, in_maps, core_ids=list(range(8)))
    return [{o: r[o] for o in outs} for r in res.results]


def _l0ffn_call(X, W):
    ones, eye = _cst_ones(), host_eye()
    nc_ln = _prog2("ln", build_ln_program)
    r = _launch(nc_ln, [{"x_fm": s, "cst_ones": ones} for s in _shards(X)], ["y_fm"])
    S = _unshard([x["y_fm"] for x in r])
    nc_l0 = _prog2("l0layer", build_l0layer_program)
    masks = [host_l0_masks(0), host_l0_masks(1)]
    tbls = [host_l0_tables(0), host_l0_tables(1)]
    for i in range(6):
        im = []
        for c in range(8):
            b, h = c // 2, c % 2
            if h == 0:
                halo = np.zeros((HALO, D), np.float32)
            else:
                halo = S[b, T - HALO: T]
            xe = np.concatenate([halo, S[b, h * T:(h + 1) * T]], 0)
            im.append({
                "xe_fm": np.ascontiguousarray(xe.T),
                "w_qkv": W["l0f_wqkv"][i], "w_o": W["l0f_wo"][i],
                "w1p": W["l0f_w1p"][i], "w2p": W["l0f_w2p"][i],
                "qscale": W["l0f_qs"][i], "kscale": W["l0f_ks"][i],
                "ln_g": W["l0f_lng"][i], "ln_b": W["l0f_lnb"][i],
                "b1p": W["l0f_b1p"][i], "b2v": W["l0f_b2v"][i],
                "tbl": tbls[h], "masks": masks[h],
                "cst_eye": eye, "cst_ones": ones,
            })
        r = _launch(nc_l0, im, ["xo_fm"])
        S = _unshard([x["xo_fm"] for x in r])
    nc_ax = _prog2("axpy", build_axpy_program)
    xs = _shards(X)
    ys = _shards(S)
    r = _launch(nc_ax, [{"x_fm": xs[c], "y_fm": ys[c]} for c in range(8)], ["xo_fm"])
    return _unshard([x["xo_fm"] for x in r])


def _l0attn_call(X, W):
    ones, eye = _cst_ones(), host_eye()
    nc_a = _prog2("lina", build_linattn_a_program)
    r = _launch(nc_a, [{"x_fm": s, "w_qkv": W["l0a_wqkv"], "cst_ones": ones,
                        "cst_eye": eye} for s in _shards(X)],
                ["qs_tm", "k_tm", "v_tm", "kmax_fm"])
    nc_b = _prog2("linb", build_linattn_b_program)
    im = []
    for c in range(8):
        pair = c // 2
        km2 = np.concatenate([r[2 * pair]["kmax_fm"], r[2 * pair + 1]["kmax_fm"]], 1)
        im.append({"k_tm": r[c]["k_tm"], "v_tm": r[c]["v_tm"],
                   "kmax2": np.ascontiguousarray(km2), "cst_ones": ones})
    rb = _launch(nc_b, im, ["cnum"])
    nc_c = _prog2("linc", build_linattn_c_program)
    im = []
    xs = _shards(X)
    for c in range(8):
        pair = c // 2
        cn2 = np.stack([rb[2 * pair]["cnum"], rb[2 * pair + 1]["cnum"]], 0)
        im.append({"qs_fm": np.ascontiguousarray(r[c]["qs_tm"].T),
                   "cnum2": np.ascontiguousarray(cn2),
                   "w_out": W["l0a_wout"], "x_fm": xs[c]})
    rc = _launch(nc_c, im, ["xo_fm"])
    return _unshard([x["xo_fm"] for x in rc])


def _l1ffn_call(X, W):
    nc = _prog2("l1ffn", build_l1ffn_program)
    im = [{"x_fm": s, "gate_w": W["l1f_gw"], "e_w1": W["l1f_w1"],
           "e_w2": W["l1f_w2"], "e_b1": W["l1f_b1"], "e_b2": W["l1f_b2"],
           "e_lng": W["l1f_lng"], "e_lnb": W["l1f_lnb"],
           "cst_ones": _cst_ones(), "cst_eye": host_eye()} for s in _shards(X)]
    r = _launch(nc, im, ["xo_fm"])
    return _unshard([x["xo_fm"] for x in r])


def _l1attn_call(X, W):
    ones, eye, fey = _cst_ones(), host_eye(), host_flip_eye()
    nc_a = _prog2("l1qkv", build_l1qkv_program)
    im = [{"x_fm": s, "wqT": W["l1a_wqT"], "wkT": W["l1a_wkT"], "wvT": W["l1a_wvT"],
           "bq": W["l1a_bq"], "bk": W["l1a_bk"], "bv": W["l1a_bv"],
           "cst_ones": ones} for s in _shards(X)]
    r = _launch(nc_a, im, ["q_fm", "k_fm", "v_fm"])
    nc_b = _prog2("l1core", build_l1core_program)
    im = []
    xs = _shards(X)
    for c in range(8):
        pair, h = c // 2, c % 2
        k_full = np.concatenate([r[2 * pair]["k_fm"], r[2 * pair + 1]["k_fm"]], 1)
        v_full = np.concatenate([r[2 * pair]["v_fm"].T, r[2 * pair + 1]["v_fm"].T], 0)
        selA, selB = host_l1_selAB(h)
        im.append({"q_fm": r[c]["q_fm"], "k_fm": np.ascontiguousarray(k_full),
                   "v_tm": np.ascontiguousarray(v_full), "x_fm": xs[c],
                   "woT": W["l1a_woT"], "bo": W["l1a_bo"], "relb": W["l1a_relb"],
                   "selA": selA, "selB": selB, "shiftv": host_l1_shift(h),
                   "cst_ones": ones, "cst_eye": eye, "cst_fey": fey})
    rb = _launch(nc_b, im, ["xo_fm"])
    return _unshard([x["xo_fm"] for x in rb])


def _prep_weights(ip):
    W = {}
    W["l0f_wqkv"] = [np.ascontiguousarray(ip["L0_ffn_Wqkv"][i]) for i in range(6)]
    W["l0f_wo"] = [np.ascontiguousarray(ip["L0_ffn_Wo"][i]) for i in range(6)]
    W["l0f_w1p"] = [host_l0_w1p(ip["L0_ffn_W1"][i]) for i in range(6)]
    W["l0f_w2p"] = [host_l0_w2p(ip["L0_ffn_W2"][i]) for i in range(6)]
    W["l0f_qs"] = [np.ascontiguousarray(ip["L0_ffn_q_scale"][i][None, :]) for i in range(6)]
    W["l0f_ks"] = [np.ascontiguousarray(ip["L0_ffn_k_scale"][i][None, :]) for i in range(6)]
    W["l0f_lng"] = [host_vec_fm(ip["L0_ffn_ln_g"][i]) for i in range(6)]
    W["l0f_lnb"] = [host_vec_fm(ip["L0_ffn_ln_b"][i]) for i in range(6)]
    W["l0f_b1p"] = [host_l0_b1p(ip["L0_ffn_b1"][i]) for i in range(6)]
    W["l0f_b2v"] = [host_vec_fm(ip["L0_ffn_b2"][i]) for i in range(6)]
    W["l0a_wqkv"] = np.ascontiguousarray(ip["L0_attn_w_qkv"])
    W["l0a_wout"] = np.ascontiguousarray(ip["L0_attn_w_out"])
    W["l1f_gw"] = np.ascontiguousarray(ip["L1_ffn_gate_w"])
    W["l1f_w1"] = np.ascontiguousarray(ip["L1_ffn_w1"])
    W["l1f_w2"] = np.ascontiguousarray(ip["L1_ffn_w2"])
    W["l1f_b1"] = np.stack([host_vec_fm(ip["L1_ffn_b1"][e]) for e in range(NE)])
    W["l1f_b2"] = np.stack([host_vec_fm(ip["L1_ffn_b2"][e]) for e in range(NE)])
    W["l1f_lng"] = np.stack([host_vec_fm(ip["L1_ffn_ln_g"][e]) for e in range(NE)])
    W["l1f_lnb"] = np.stack([host_vec_fm(ip["L1_ffn_ln_b"][e]) for e in range(NE)])
    W["l1a_wqT"] = np.ascontiguousarray(ip["L1_attn_Wq"].T)
    W["l1a_wkT"] = np.ascontiguousarray(ip["L1_attn_Wk"].T)
    W["l1a_wvT"] = np.ascontiguousarray(ip["L1_attn_Wv"].T)
    W["l1a_woT"] = np.ascontiguousarray(ip["L1_attn_Wo"].T)
    W["l1a_bq"] = host_vec_fm(ip["L1_attn_bq"])
    W["l1a_bk"] = host_vec_fm(ip["L1_attn_bk"])
    W["l1a_bv"] = host_vec_fm(ip["L1_attn_bv"])
    W["l1a_bo"] = host_vec_fm(ip["L1_attn_bo"])
    W["l1a_relb"] = np.ascontiguousarray(ip["L1_attn_rel_bias"])
    return W


def kernel(**inputs):
    ip = {k: np.asarray(v, dtype=np.float32) for k, v in inputs.items()}
    W = _prep_weights(ip)
    X = np.ascontiguousarray(ip["L0_attn_x"])
    X = _l0ffn_call(X, W)
    X = _l0attn_call(X, W)
    X = _l0ffn_call(X, W)
    X = _l1ffn_call(X, W)
    X = _l1attn_call(X, W)
    X = _l1ffn_call(X, W)
    return X


# version 03 (final)
# speedup vs baseline: 1.0311x; 1.0311x over previous
import math
from contextlib import ExitStack

import numpy as np
import ml_dtypes

import concourse.bass as bass
import concourse.mybir as mybir
from concourse.bass_types import AP as bass_types_AP
from concourse.alu_op_type import AluOpType as ALU
from concourse.bass_utils import run_bass_kernel_spmd

F32 = mybir.dt.float32
BF16 = mybir.dt.bfloat16
AF = mybir.ActivationFunctionType
AX = mybir.AxisListType
BF16NP = ml_dtypes.bfloat16

P = 128
D = 1024
DT = D // P
T = 2048
B_, SEQ = 4, 4096

SEM_ROT = 24000
DMA_K = 4

QUEUES = ("pe", "act", "dve", "pool", "sp")
_ENG_ATTR = {"pe": "tensor", "act": "scalar", "dve": "vector", "pool": "gpsimd", "sp": "sync"}


class VOp:
    __slots__ = ("queue", "fn", "kind", "idx", "deps", "mark", "waits", "tag",
                 "dma_idx", "cc_idx", "inc_info")

    def __init__(self, queue, fn, kind, idx, tag=""):
        self.queue = queue
        self.fn = fn
        self.kind = kind
        self.idx = idx
        self.deps = set()
        self.mark = False
        self.waits = []
        self.tag = tag


class VB:

    def __init__(self, nc):
        self.nc = nc
        self.q = {qn: [] for qn in QUEUES}
        self.dma_count = {qn: 0 for qn in QUEUES}
        self.cc_count = 0
        self.rw = {}
        self.rr = {}

    def op(self, queue, fn, reads=(), writes=(), tag=""):
        return self._add(queue, fn, "e", reads, writes, tag)

    def dma(self, queue, fn, reads=(), writes=(), tag=""):
        return self._add(queue, fn, "d", reads, writes, tag)

    def cc(self, fn, reads=(), writes=(), tag=""):
        return self._add("pool", fn, "c", reads, writes, tag)

    PSUM_PREFIXES = ("psf", "psb", "ps")

    def _add(self, queue, fn, kind, reads, writes, tag):
        reads = list(reads)
        writes = list(writes)
        ps_reads = [r for r in reads
                    if isinstance(r, tuple) and isinstance(r[0], str) and r[0] in self.PSUM_PREFIXES]
        if ps_reads:
            reads = [r for r in reads if r not in ps_reads]
            writes = writes + ps_reads
        ops = self.q[queue]
        idx = len(ops)
        o = VOp(queue, fn, kind, idx, tag)
        if kind == "d":
            o.dma_idx = self.dma_count[queue]
            self.dma_count[queue] += 1
            myref = ("d", queue, o.dma_idx)
        elif kind == "c":
            o.cc_idx = self.cc_count
            self.cc_count += 1
            myref = ("c", o.cc_idx)
        else:
            myref = ("e", queue, idx)
        deps = set()
        for r in reads:
            w = self.rw.get(r)
            if w is not None:
                deps.add(w)
        for r in writes:
            w = self.rw.get(r)
            if w is not None:
                deps.add(w)
            for rd in self.rr.get(r, ()):
                deps.add(rd)
        for r in reads:
            self.rr.setdefault(r, []).append(myref)
        for r in writes:
            self.rw[r] = myref
            self.rr[r] = []
        if queue == "pe" and kind == "e":
            o.deps = {d for d in deps if not (d[0] == "e" and d[1] == "pe")}
        else:
            o.deps = {d for d in deps if d != myref}
        ops.append(o)
        return myref

    def barrier_all(self):
        raise NotImplementedError

    def finalize_and_emit(self, es, final_queue="pool"):
        nc = self.nc
        for qn in QUEUES:
            for o in self.q[qn]:
                for d in o.deps:
                    if d[0] == "e":
                        self.q[d[1]][d[2]].mark = True
        mark_val = {}
        n_sems = {}
        for qn in QUEUES:
            cnt = 0
            for o in self.q[qn]:
                if o.kind == "e" and o.mark:
                    mark_val[("e", qn, o.idx)] = (cnt // SEM_ROT, cnt % SEM_ROT + 1)
                    cnt += 1
            n_sems[qn] = cnt // SEM_ROT + 1
        qsems = {qn: [es.enter_context(nc.semaphore(f"sq_{qn}_{i}"))
                      for i in range(n_sems[qn])] for qn in QUEUES}
        dsems = {qn: [es.enter_context(nc.semaphore(f"sd_{qn}_{i}"))
                      for i in range(min(DMA_K, max(1, self.dma_count[qn])))]
                 for qn in QUEUES}
        ccsems = [es.enter_context(nc.semaphore(f"scc_{i}")) for i in range(self.cc_count)]
        for qn in QUEUES:
            for o in self.q[qn]:
                if o.kind == "d" and o.dma_idx >= DMA_K:
                    o.deps.add(("d", qn, o.dma_idx - DMA_K))

        def resolve(d):
            if d[0] == "e":
                g, v = mark_val[d]
                return (qsems[d[1]][g], v)
            if d[0] == "d":
                qn, k = d[1], d[2]
                return (dsems[qn][k % DMA_K], (k // DMA_K + 1) * 16)
            return (ccsems[d[1]], 1)

        def sort_key(d):
            return (d[0], str(d[1]), d[2] if len(d) > 2 else 0)

        for qn in QUEUES:
            floor = {}
            for o in self.q[qn]:
                for d in sorted(o.deps, key=sort_key):
                    sem, val = resolve(d)
                    key = id(sem)
                    if floor.get(key, 0) >= val:
                        continue
                    o.waits.append((sem, val))
                    floor[key] = val

        final_waits = []
        for qn in QUEUES:
            nd = self.dma_count[qn]
            for kk in range(min(DMA_K, nd)):
                cnt = (nd - kk + DMA_K - 1) // DMA_K
                final_waits.append((dsems[qn][kk], cnt * 16))

        block = es.enter_context(nc.Block())
        for qn in QUEUES:
            ops = self.q[qn]
            if not ops and not (qn == final_queue and final_waits):
                continue

            def body(eng, ops=ops, qn=qn):
                for o in ops:
                    for sem, val in o.waits:
                        eng.wait_ge(sem, val)
                    ins = o.fn(eng)
                    if o.kind == "d":
                        ins.then_inc(dsems[qn][o.dma_idx % DMA_K], 16)
                    elif o.kind == "c":
                        ins.then_inc(ccsems[o.cc_idx], 1)
                    elif o.mark:
                        g, _ = mark_val[("e", qn, o.idx)]
                        ins.then_inc(qsems[qn][g], 1)
                if qn == final_queue:
                    for sem, val in final_waits:
                        eng.wait_ge(sem, val)

            getattr(block, _ENG_ATTR[qn])(body)

        stats = {qn: (len(self.q[qn]), sum(1 for o in self.q[qn] if o.mark),
                      sum(len(o.waits) for o in self.q[qn])) for qn in QUEUES}
        return stats


class Buf:
    RGRAN = 512

    def __init__(self, arena, arena_name, off, cols, dtype, tag):
        self.arena = arena
        self.aname = arena_name
        self.off = off
        self.cols = cols
        self.dtype = dtype
        self.tag = tag

    def ap(self, c0=0, c1=None):
        c1 = self.cols if c1 is None else c1
        return self.arena[:, self.off + c0: self.off + c1]

    def prt(self, p0, p1, c0=0, c1=None):
        c1 = self.cols if c1 is None else c1
        return self.arena[p0:p1, self.off + c0: self.off + c1]

    def reg(self, c0=0, c1=None):
        c1 = self.cols if c1 is None else c1
        b0 = (self.off + c0) // self.RGRAN
        b1 = (self.off + c1 - 1) // self.RGRAN
        return [(self.aname, b) for b in range(b0, b1 + 1)]


class Arena:
    def __init__(self, nc, es, name, cols, dtype):
        self.t = es.enter_context(nc.sbuf_tensor(name, [P, cols], dtype))
        self.name = name
        self.cols = cols
        self.ptr = 0
        self.marks = []

    def alloc(self, cols, tag=""):
        assert self.ptr + cols <= self.cols, \
            f"arena {self.name} overflow: {self.ptr}+{cols}>{self.cols} ({tag})"
        b = Buf(self.t, self.name, self.ptr, cols,
                None, tag)
        self.ptr += cols
        return b

    def push(self):
        self.marks.append(self.ptr)

    def pop(self):
        self.ptr = self.marks.pop()


def ceil_div(a, b):
    return (a + b - 1) // b


class Consts:

    def __init__(self, vb, arena, n=16):
        self.vb = vb
        self.buf = arena.alloc(n, "consts")
        self.n = n
        self.map = {}

    def col(self, value):
        value = float(value)
        if value not in self.map:
            i = len(self.map)
            assert i < self.n
            ap = self.buf.ap(i, i + 1)
            rg = self.buf.reg(i, i + 1)
            self.vb.op("dve", (lambda e, ap=ap, value=value: e.memset(ap, value)),
                       reads=[], writes=rg, tag=f"cst_{value}")
            self.map[value] = (ap, rg)
        return self.map[value]



def emit_ln_fm(vb, pb, psr, x_buf, y_buf, rows, ab, bb, ones_col, ones_col_reg,
               ones_row, ones_row_reg, TT, eps, consts=None,
               g_aps=None, b_aps=None, gb_regs=(), sq_tmp=None, prefix="ln"):
    eps_ap, eps_rg = consts.col(float(eps))
    """pb: list of >=4 psum tensors [128,512] f32; psr their region keys.
    rows: Buf with >= 2*TT cols (partition-0 scratch rows).
    ab, bb: Bufs [128, TT] f32 broadcast planes. sq_tmp: Buf [128,512] f32."""
    NCH = TT // 512
    for ch in range(NCH):
        c0 = 512 * ch
        bs, bq, bba, bbb = pb[0], pb[1], pb[2], pb[3]
        rs, rq, rba, rbb = psr[0], psr[1], psr[2], psr[3]
        for d in range(DT):
            vb.op("pe", (lambda e, d=d, c0=c0, bs=bs: e.matmul(
                out=bs[0:1, 0:512], lhsT=ones_col,
                rhs=x_buf.ap(d * TT + c0, d * TT + c0 + 512),
                start=(d == 0), stop=(d == DT - 1))),
                reads=x_buf.reg(d * TT + c0, d * TT + c0 + 512) + ones_col_reg,
                writes=[rs], tag=f"{prefix}sx{ch}")
        for d in range(DT):
            vb.op("act", (lambda e, d=d, c0=c0: e.activation(
                out=sq_tmp.ap(0, 512), in_=x_buf.ap(d * TT + c0, d * TT + c0 + 512),
                func=AF.Square)),
                reads=x_buf.reg(d * TT + c0, d * TT + c0 + 512),
                writes=sq_tmp.reg(0, 512), tag=f"{prefix}sq{ch}_{d}")
            vb.op("pe", (lambda e, d=d, bq=bq: e.matmul(
                out=bq[0:1, 0:512], lhsT=ones_col, rhs=sq_tmp.ap(0, 512),
                start=(d == 0), stop=(d == DT - 1))),
                reads=sq_tmp.reg(0, 512) + ones_col_reg,
                writes=[rq], tag=f"{prefix}sxx{ch}")
        mu_sl = (rows.prt(0, 1, c0, c0 + 512), rows.reg(c0, c0 + 512))
        va_sl = (rows.prt(0, 1, TT + c0, TT + c0 + 512), rows.reg(TT + c0, TT + c0 + 512))
        m_rg = mu_sl[1]; v_rg = va_sl[1]
        m_ap, m_rg = mu_sl
        v_ap, v_rg = va_sl
        vb.op("dve", (lambda e, m_ap=m_ap, bs=bs: e.tensor_scalar(
            out=m_ap, in0=bs[0:1, 0:512], scalar1=1.0 / D, scalar2=None,
            op0=ALU.mult)),
            reads=[rs], writes=m_rg, tag=f"{prefix}mu{ch}")
        vb.op("dve", (lambda e, m_ap=m_ap, v_ap=v_ap: e.tensor_tensor(
            out=v_ap, in0=m_ap, in1=m_ap, op=ALU.mult)),
            reads=m_rg, writes=v_rg, tag=f"{prefix}m2{ch}")
        vb.op("dve", (lambda e, v_ap=v_ap, bq=bq: e.scalar_tensor_tensor(
            out=v_ap, in0=bq[0:1, 0:512], scalar=1.0 / D, in1=v_ap,
            op0=ALU.mult, op1=ALU.subtract)),
            reads=[rq] + v_rg, writes=v_rg, tag=f"{prefix}var{ch}")
        vb.op("act", (lambda e, v_ap=v_ap: e.activation(
            out=v_ap, in_=v_ap, func=AF.Sqrt, bias=eps_ap[0:1, :])),
            reads=v_rg + eps_rg, writes=v_rg, tag=f"{prefix}std{ch}")
        vb.op("dve", (lambda e, v_ap=v_ap: e.reciprocal(out=v_ap, in_=v_ap)),
            reads=v_rg, writes=v_rg, tag=f"{prefix}rstd{ch}")
        vb.op("dve", (lambda e, m_ap=m_ap, v_ap=v_ap: e.scalar_tensor_tensor(
            out=m_ap, in0=m_ap, scalar=-1.0, in1=v_ap,
            op0=ALU.mult, op1=ALU.mult)),
            reads=m_rg + v_rg, writes=m_rg, tag=f"{prefix}nm{ch}")
        vb.op("pe", (lambda e, v_ap=v_ap, bba=bba: e.matmul(
            out=bba[:, 0:512], lhsT=ones_row, rhs=v_ap, start=True, stop=True)),
            reads=v_rg + ones_row_reg, writes=[rba], tag=f"{prefix}bca{ch}")
        vb.op("dve", (lambda e, c0=c0, bba=bba: e.tensor_copy(
            out=ab.ap(c0, c0 + 512), in_=bba[:, 0:512])),
            reads=[rba], writes=ab.reg(c0, c0 + 512), tag=f"{prefix}eva{ch}")
        vb.op("pe", (lambda e, m_ap=m_ap, bbb=bbb: e.matmul(
            out=bbb[:, 0:512], lhsT=ones_row, rhs=m_ap, start=True, stop=True)),
            reads=m_rg + ones_row_reg, writes=[rbb], tag=f"{prefix}bcb{ch}")
        vb.op("dve", (lambda e, c0=c0, bbb=bbb: e.tensor_copy(
            out=bb.ap(c0, c0 + 512), in_=bbb[:, 0:512])),
            reads=[rbb], writes=bb.reg(c0, c0 + 512), tag=f"{prefix}evb{ch}")

    for d in range(DT):
        yr = y_buf.reg(d * TT, (d + 1) * TT)
        vb.op("dve", (lambda e, d=d: e.tensor_tensor(
            out=y_buf.ap(d * TT, (d + 1) * TT), in0=x_buf.ap(d * TT, (d + 1) * TT),
            in1=ab.ap(), op=ALU.mult)),
            reads=x_buf.reg(d * TT, (d + 1) * TT) + ab.reg(), writes=yr,
            tag=f"{prefix}ap1_{d}")
        if g_aps is None:
            vb.op("dve", (lambda e, d=d: e.tensor_tensor(
                out=y_buf.ap(d * TT, (d + 1) * TT), in0=y_buf.ap(d * TT, (d + 1) * TT),
                in1=bb.ap(), op=ALU.add)),
                reads=yr + bb.reg(), writes=yr, tag=f"{prefix}ap2_{d}")
        else:
            vb.op("dve", (lambda e, d=d: e.tensor_tensor(
                out=y_buf.ap(d * TT, (d + 1) * TT), in0=y_buf.ap(d * TT, (d + 1) * TT),
                in1=bb.ap(), op=ALU.add)),
                reads=yr + bb.reg(), writes=yr, tag=f"{prefix}ap2_{d}")
            vb.op("dve", (lambda e, d=d: e.tensor_scalar(
                out=y_buf.ap(d * TT, (d + 1) * TT), in0=y_buf.ap(d * TT, (d + 1) * TT),
                scalar1=g_aps[d], scalar2=b_aps[d], op0=ALU.mult, op1=ALU.add)),
                reads=yr + list(gb_regs), writes=yr, tag=f"{prefix}ap3_{d}")



def build_ln_program(TT=T, eps=1e-5):
    nc = bass.Bass("TRN2", target_bir_lowering=False, debug=False)
    x_d = nc.dram_tensor("x_fm", [D, TT], F32, kind="ExternalInput").ap()
    y_d = nc.dram_tensor("y_fm", [D, TT], F32, kind="ExternalOutput").ap()
    ones_d = nc.dram_tensor("cst_ones", [P, P], F32, kind="ExternalInput").ap()

    es = ExitStack()
    with es:
        af = Arena(nc, es, "a_f32", 2 * (DT * TT) + 2 * TT + 2 * TT + P + 512 + 16, F32)
        x = af.alloc(DT * TT, "x")
        y = af.alloc(DT * TT, "y")
        ab = af.alloc(TT, "Ab")
        bb = af.alloc(TT, "Bb")
        rows = af.alloc(2 * TT, "rows")
        ones = af.alloc(P, "ones")
        sq = af.alloc(512, "sq")
        pb = [es.enter_context(nc.psum_tensor(f"ps{i}", [P, 512], F32)) for i in range(4)]
        psr = [("ps", i) for i in range(4)]

        vb = VB(nc)
        cst = Consts(vb, af)
        vb.dma("sp", lambda e: e.dma_start(out=x.ap(), in_=x_d),
               reads=[("dram", "x")], writes=x.reg(), tag="ld_x")
        vb.dma("sp", lambda e: e.dma_start(out=ones.ap(), in_=ones_d),
               reads=[("dram", "ones")], writes=ones.reg(), tag="ld_ones")

        emit_ln_fm(vb, pb, psr, x, y, rows, ab, bb,
                   ones_col=ones.prt(0, P, 0, 1), ones_col_reg=ones.reg(0, 1),
                   ones_row=ones.prt(0, 1, 0, P), ones_row_reg=ones.reg(0, P),
                   TT=TT, eps=eps, consts=cst, sq_tmp=sq)

        vb.dma("sp", lambda e: e.dma_start(out=y_d, in_=y.ap()),
               reads=y.reg(), writes=[("dram", "y")], tag="st_y")
        vb.finalize_and_emit(es, final_queue="pool")
    return nc


def _ln_np(x, axis=-1, eps=1e-5):
    m = x.mean(axis, keepdims=True)
    v = x.var(axis, keepdims=True)
    return (x - m) / np.sqrt(v + eps)


def _cst_ones():
    return np.ones((P, P), np.float32)



NTT = 19
TE = NTT * P
HALO = 384
QT0 = 2
NQT = NTT - QT0
XMC = NQT * P
DIP = 2816
NPT = DIP // P
W1C = 2 * DIP
WIN = 384
QK_SCALE = 8.0


def build_l0layer_program(dbg=False, it_dbg=5, h_dbg=3, stage=99, sub=9, rsub=9, ntl=NTT):
    nc = bass.Bass("TRN2", target_bir_lowering=False, debug=False)
    xe_d = nc.dram_tensor("xe_fm", [D, TE], F32, kind="ExternalInput").ap()
    wqkv_d = nc.dram_tensor("w_qkv", [D, 3 * D], F32, kind="ExternalInput").ap()
    wo_d = nc.dram_tensor("w_o", [D, D], F32, kind="ExternalInput").ap()
    w1_d = nc.dram_tensor("w1p", [D, W1C], F32, kind="ExternalInput").ap()
    w2_d = nc.dram_tensor("w2p", [DIP, D], F32, kind="ExternalInput").ap()
    qs_d = nc.dram_tensor("qscale", [1, 64], F32, kind="ExternalInput").ap()
    ks_d = nc.dram_tensor("kscale", [1, 64], F32, kind="ExternalInput").ap()
    lng_d = nc.dram_tensor("ln_g", [P, DT], F32, kind="ExternalInput").ap()
    lnb_d = nc.dram_tensor("ln_b", [P, DT], F32, kind="ExternalInput").ap()
    b1_d = nc.dram_tensor("b1p", [P, 2 * NPT], F32, kind="ExternalInput").ap()
    b2_d = nc.dram_tensor("b2v", [P, DT], F32, kind="ExternalInput").ap()
    tbl_d = nc.dram_tensor("tbl", [4, TE, 64], F32, kind="ExternalInput").ap()
    msk_d = nc.dram_tensor("masks", [NTT, P, WIN], F32, kind="ExternalInput").ap()
    eye_d = nc.dram_tensor("cst_eye", [P, P], F32, kind="ExternalInput").ap()
    ones_d = nc.dram_tensor("cst_ones", [P, P], F32, kind="ExternalInput").ap()
    xo_d = nc.dram_tensor("xo_fm", [D, T], F32, kind="ExternalOutput").ap()
    if dbg:
        dbg_qn = nc.dram_tensor("dbg_qn", [P, D], BF16, kind="ExternalOutput").ap()
        dbg_kn = nc.dram_tensor("dbg_kn", [P, D], BF16, kind="ExternalOutput").ap()
        dbg_prm = nc.dram_tensor("dbg_prm", [P, WIN], BF16, kind="ExternalOutput").ap()
        dbg_aot = nc.dram_tensor("dbg_aot", [P, D], BF16, kind="ExternalOutput").ap()
        dbg_xmid = nc.dram_tensor("dbg_xmid", [P, DT * XMC], F32, kind="ExternalOutput").ap()
        dbg_xln = nc.dram_tensor("dbg_xln", [P, DT * T], BF16, kind="ExternalOutput").ap()
        dbg_v = nc.dram_tensor("dbg_v", [P, 1040], BF16, kind="ExternalOutput").ap()
        dbg_h = nc.dram_tensor("dbg_h", [P, NPT * 1024], BF16, kind="ExternalOutput").ap()
        dbg_w2 = nc.dram_tensor("dbg_w2", [P, NPT * P], BF16, kind="ExternalOutput").ap()
        dbg_ps = nc.dram_tensor("dbg_ps", [P, 512], F32, kind="ExternalOutput").ap()

    es = ExitStack()
    with es:
        A = Arena(nc, es, "arn", 103 * 1024, BF16)
        vb = VB(nc)

        def falloc(cols, tag):
            b = A.alloc(2 * cols, tag)
            return b

        def fap(buf, c0=0, c1=None):
            c1 = buf.cols // 2 if c1 is None else c1
            return buf.arena[:, buf.off + 2 * c0: buf.off + 2 * c1].bitcast(F32)

        def fprt(buf, p0, p1, c0=0, c1=None):
            c1 = buf.cols // 2 if c1 is None else c1
            return buf.arena[p0:p1, buf.off + 2 * c0: buf.off + 2 * c1].bitcast(F32)

        def freg(buf, c0=0, c1=None):
            c1 = buf.cols // 2 if c1 is None else c1
            return buf.reg(2 * c0, 2 * c1)

        cstb = falloc(16, "consts")
        ones = falloc(P, "ones")
        idn = A.alloc(P, "idn")
        lng = falloc(DT, "lng")
        lnb = falloc(DT, "lnb")
        b1c = falloc(2 * NPT, "b1")
        b2c = falloc(DT, "b2")
        qsb = A.alloc(64, "qsb")
        ksb = A.alloc(64, "ksb")
        teff = [A.alloc(NTT * 64, f"tble{k}") for k in range(4)]
        wo = A.alloc(DT * D, "wo")
        mskb = [A.alloc(WIN, f"msk{s}") for s in range(2)]
        xmid = falloc(DT * XMC, "xmid")

        cst_map = {}

        def cst(value):
            value = float(value)
            if value not in cst_map:
                i = len(cst_map)
                assert i < 16
                ap = fap(cstb, i, i + 1)
                rg = freg(cstb, i, i + 1)
                vb.op("dve", (lambda e, ap=ap, value=value: e.memset(ap, value)),
                      reads=[], writes=rg, tag=f"cst{value}")
                cst_map[value] = (ap, rg)
            return cst_map[value]

        pbf = [es.enter_context(nc.psum_tensor(f"pf{i}", [P, 512], F32)) for i in range(6)]
        pbb = [es.enter_context(nc.psum_tensor(f"pb{i}", [P, 1024], BF16)) for i in range(2)]
        PSF = [("psf", i) for i in range(6)]
        PSB = [("psb", i) for i in range(2)]
        rot = {"mm": 0, "sim": 0, "av": 0, "tp": 0}

        def mmbank():
            i = rot["mm"] % 2; rot["mm"] += 1
            return pbf[i], PSF[i]

        def simbank():
            i = 2 + rot["sim"] % 2; rot["sim"] += 1
            return pbf[i], PSF[i]

        def avbank():
            i = 4 + rot["av"] % 2; rot["av"] += 1
            return pbf[i], PSF[i]

        def tpbank():
            i = rot["tp"] % 2; rot["tp"] += 1
            return pbb[i], PSB[i]

        for dst, srcd, tg in ((ones, ones_d, "ones"), (lng, lng_d, "lng"),
                              (lnb, lnb_d, "lnb"), (b1c, b1_d, "b1"), (b2c, b2_d, "b2")):
            vb.dma("sp", (lambda e, dst=dst, srcd=srcd: e.dma_start(out=fap(dst), in_=srcd)),
                   writes=freg(dst), tag=f"ld_{tg}")
        vb.dma("pool", (lambda e: e.dma_start(out=idn.ap(), in_=eye_d)),
               writes=idn.reg(), tag="ld_idn")

        A.push()
        tbl_raw = [A.alloc(NTT * 64, f"tblr{k}") for k in range(4)]
        wqkv = A.alloc(DT * 3 * D, "wqkv")
        knf = A.alloc(3 * DT * P, "knf")
        vbuf = A.alloc(3 * 16 * 65, "vbuf")
        xf = [falloc(DT * 129, f"xf{s}") for s in range(3)]
        xbf = [A.alloc(DT * 129, f"xbf{s}") for s in range(2)]
        qnst = A.alloc(D, "qnst")
        knst = A.alloc(D, "knst")
        qnf = A.alloc(DT * P, "qnf")
        u1s = [A.alloc(512, f"u1_{s}") for s in range(2)]
        u2s = [A.alloc(512, f"u2_{s}") for s in range(2)]
        sqts = [falloc(512, f"sq_{s}") for s in range(2)]
        nrms = [falloc(48, f"nrm_{s}") for s in range(2)]
        prbs_ = [A.alloc(WIN, f"prb{s}") for s in range(2)]
        prms_ = [A.alloc(WIN, f"prm{s}") for s in range(2)]
        ptbs_ = [A.alloc(WIN, f"ptb{s}") for s in range(2)]
        aot = A.alloc(D, "aot")
        aof = A.alloc(DT * P, "aof")
        rcps_ = [falloc(4, f"rcp{s}") for s in range(2)]
        rsc = {"i": 0}
        qsr = falloc(64, "qsr")
        ksr = falloc(64, "ksr")

        for k in range(4):
            vb.dma("pool", (lambda e, k=k: e.dma_start(
                out=tbl_raw[k].ap().rearrange("p (tt d) -> p tt d", d=64),
                in_=tbl_d[k].rearrange("(tt p) d -> p tt d", p=P))),
                writes=tbl_raw[k].reg(), tag=f"ld_tbl{k}")
        for oc in range(6):
            vb.dma("pool", (lambda e, oc=oc: e.dma_start(
                out=wqkv.ap().rearrange("p (d o) -> p d o", o=3 * D)[:, :, 512 * oc: 512 * oc + 512],
                in_=wqkv_d[:, 512 * oc: 512 * oc + 512].rearrange("(dt p) o -> p dt o", p=P))),
                writes=wqkv.reg(), tag=f"ld_wqkv{oc}")
        vb.dma("pool", (lambda e: e.dma_start(
            out=wo.ap().rearrange("p (kt o) -> p kt o", kt=DT),
            in_=wo_d.rearrange("(kt p) o -> p kt o", p=P))),
            writes=wo.reg(), tag="ld_wo")
        vb.dma("sp", lambda e: e.dma_start(out=fprt(qsr, 0, 1), in_=qs_d),
               writes=freg(qsr), tag="ld_qs")
        vb.dma("sp", lambda e: e.dma_start(out=fprt(ksr, 0, 1), in_=ks_d),
               writes=freg(ksr), tag="ld_ks")
        for name, srcb, dst in (("q", qsr, qsb), ("k", ksr, ksb)):
            bank, breg = mmbank()
            vb.op("pe", (lambda e, srcb=srcb, bank=bank: e.matmul(
                out=bank[:, 0:64], lhsT=fprt(ones, 0, 1), rhs=fprt(srcb, 0, 1),
                start=True, stop=True)),
                reads=freg(srcb) + freg(ones), writes=[breg], tag=f"bc_{name}s")
            vb.op("dve", (lambda e, dst=dst, bank=bank: e.tensor_copy(
                out=dst.ap(), in_=bank[:, 0:64])),
                reads=[breg], writes=dst.reg(), tag=f"ev_{name}s")
        for k, (scl, rotd) in enumerate(((qsb, False), (qsb, True), (ksb, False), (ksb, True))):
            for tt in range(NTT):
                c0 = tt * 64
                if not rotd:
                    vb.op("dve", (lambda e, k=k, c0=c0, scl=scl: e.tensor_tensor(
                        out=teff[k].ap(c0, c0 + 64), in0=tbl_raw[k].ap(c0, c0 + 64),
                        in1=scl.ap(), op=ALU.mult)),
                        reads=tbl_raw[k].reg(c0, c0 + 64) + scl.reg(),
                        writes=teff[k].reg(c0, c0 + 64), tag=f"te{k}_{tt}")
                else:
                    vb.op("dve", (lambda e, k=k, c0=c0, scl=scl: e.tensor_tensor(
                        out=teff[k].ap(c0, c0 + 32), in0=tbl_raw[k].ap(c0, c0 + 32),
                        in1=scl.ap(32, 64), op=ALU.mult)),
                        reads=tbl_raw[k].reg(c0, c0 + 64) + scl.reg(),
                        writes=teff[k].reg(c0, c0 + 64), tag=f"te{k}a_{tt}")
                    vb.op("dve", (lambda e, k=k, c0=c0, scl=scl: e.tensor_tensor(
                        out=teff[k].ap(c0 + 32, c0 + 64), in0=tbl_raw[k].ap(c0 + 32, c0 + 64),
                        in1=scl.ap(0, 32), op=ALU.mult)),
                        reads=tbl_raw[k].reg(c0, c0 + 64) + scl.reg(),
                        writes=teff[k].reg(c0, c0 + 64), tag=f"te{k}b_{tt}")

        sc8, sc8_rg = cst(QK_SCALE)

        def emit_rotary(it, oc, bank, breg, is_q, rsub=rsub):
            par = rsc["i"] % 2
            rsc["i"] += 1
            u1, u2, sqt, nrm = u1s[par], u2s[par], sqts[par], nrms[par]
            st_buf = qnst if is_q else knst
            base = (oc % 2) * 512
            ta = teff[0] if is_q else teff[2]
            tb = teff[1] if is_q else teff[3]
            tc0 = it * 64
            vb.op("dve", (lambda e, bank=bank, st_buf=st_buf, base=base: e.tensor_copy(
                out=st_buf.ap(base, base + 512), in_=bank[:, 0:512])),
                reads=[breg], writes=st_buf.reg(base, base + 512), tag=f"r{it}_{oc}raw")
            qb = st_buf.ap(base, base + 512)
            qb3 = qb.rearrange("p (h d) -> p h d", d=64)
            qbr = st_buf.reg(base, base + 512)
            if rsub < 2:
                return
            vb.op("act", (lambda e, bank=bank: e.activation(
                out=fap(sqt), in_=bank[:, 0:512], func=AF.Square)),
                reads=[breg], writes=freg(sqt), tag=f"r{it}_{oc}sq")
            if rsub == 12:
                return
            nA = fap(nrm, 0, 8)
            nAr = freg(nrm, 0, 8)
            vb.op("dve", (lambda e, nA=nA: e.reduce_sum(
                out=nA, in_=fap(sqt).rearrange("p (h d) -> p h d", d=64), axis=AX.X)),
                reads=freg(sqt), writes=nAr, tag=f"r{it}_{oc}ss")
            if rsub == 13:
                return
            vb.op("act", (lambda e, nA=nA: e.activation(out=nA, in_=nA, func=AF.Sqrt)),
                reads=nAr, writes=nAr, tag=f"r{it}_{oc}nr")
            if rsub == 14:
                return
            vb.op("dve", (lambda e, nA=nA: e.tensor_scalar(
                out=nA, in0=nA, scalar1=1e-12, scalar2=None, op0=ALU.max)),
                reads=nAr, writes=nAr, tag=f"r{it}_{oc}cl")
            vb.op("dve", (lambda e, nA=nA: e.reciprocal(out=nA, in_=nA)),
                reads=nAr, writes=nAr, tag=f"r{it}_{oc}rc")
            if rsub < 3:
                return
            aab = ta.ap(tc0, tc0 + 64).unsqueeze(1).broadcast_to([P, 8, 64])
            vb.op("dve", (lambda e, qb3=qb3, aab=aab: e.tensor_tensor(
                out=u1.ap().rearrange("p (h d) -> p h d", d=64), in0=qb3, in1=aab,
                op=ALU.mult)),
                reads=qbr + ta.reg(tc0, tc0 + 64), writes=u1.reg(), tag=f"r{it}_{oc}u1")
            if rsub < 4:
                return
            bb1 = tb.ap(tc0, tc0 + 32).unsqueeze(1).broadcast_to([P, 8, 32])
            bb2 = tb.ap(tc0 + 32, tc0 + 64).unsqueeze(1).broadcast_to([P, 8, 32])
            vb.op("dve", (lambda e, qb3=qb3, bb1=bb1: e.tensor_tensor(
                out=u2.ap().rearrange("p (h d) -> p h d", d=64)[:, :, 0:32],
                in0=qb3[:, :, 32:64], in1=bb1, op=ALU.mult)),
                reads=qbr + tb.reg(tc0, tc0 + 64), writes=u2.reg(), tag=f"r{it}_{oc}u2a")
            vb.op("dve", (lambda e, qb3=qb3, bb2=bb2: e.tensor_tensor(
                out=u2.ap().rearrange("p (h d) -> p h d", d=64)[:, :, 32:64],
                in0=qb3[:, :, 0:32], in1=bb2, op=ALU.mult)),
                reads=qbr + tb.reg(tc0, tc0 + 64), writes=u2.reg(), tag=f"r{it}_{oc}u2b")
            if rsub < 5:
                return
            vb.op("dve", (lambda e: e.tensor_tensor(
                out=u1.ap(), in0=u1.ap(), in1=u2.ap(), op=ALU.add)),
                reads=u1.reg() + u2.reg(), writes=u1.reg(), tag=f"r{it}_{oc}add")
            nb = nA.unsqueeze(2).broadcast_to([P, 8, 64])
            vb.op("dve", (lambda e, qb3=qb3, nb=nb: e.tensor_tensor(
                out=qb3, in0=u1.ap().rearrange("p (h d) -> p h d", d=64), in1=nb,
                op=ALU.mult)),
                reads=u1.reg() + nAr, writes=qbr, tag=f"r{it}_{oc}fin")

        for it in range(ntl if stage >= 1 else 0):
            t0 = it * P
            sx = it % 3
            sbi = it % 2
            sl = it % 3
            if it == 0:
                vb.dma("sp", (lambda e, sx=sx: e.dma_start(
                    out=fap(xf[sx]).rearrange("p (dt c) -> p dt c", c=129)[:, :, 1:129],
                    in_=xe_d[:, 0:128].rearrange("(dt p) c -> p dt c", p=P))),
                    reads=[("dram", "xe")], writes=freg(xf[sx]), tag="ldx0")
                vb.op("dve", (lambda e, sx=sx: e.tensor_copy(
                    out=fap(xf[sx]).rearrange("p (dt c) -> p dt c", c=129)[:, :, 0:1],
                    in_=fap(xf[sx]).rearrange("p (dt c) -> p dt c", c=129)[:, :, 1:2])),
                    reads=freg(xf[sx]), writes=freg(xf[sx]), tag="fix0")
            else:
                vb.dma("sp", (lambda e, sx=sx, t0=t0: e.dma_start(
                    out=fap(xf[sx]).rearrange("p (dt c) -> p dt c", c=129),
                    in_=xe_d[:, t0 - 1: t0 + 128].rearrange("(dt p) c -> p dt c", p=P))),
                    reads=[("dram", "xe")], writes=freg(xf[sx]), tag=f"ldx{it}")
            vb.op("dve", (lambda e, sx=sx, sbi=sbi: e.tensor_copy(
                out=xbf[sbi].ap(), in_=fap(xf[sx]))),
                reads=freg(xf[sx]), writes=xbf[sbi].reg(), tag=f"cx{it}")

            for oc in range(6):
                if oc < 2 and it < QT0:
                    continue
                if sub < 2:
                    continue
                bank, breg = mmbank()
                for dt in range(DT):
                    cofs = dt * 129 + (0 if dt >= 4 else 1)
                    wc0 = dt * 3 * D + 512 * oc
                    vb.op("pe", (lambda e, bank=bank, sbi=sbi, cofs=cofs, wc0=wc0, dt=dt:
                          e.matmul(out=bank[:, 0:512],
                                   lhsT=xbf[sbi].ap(cofs, cofs + 128),
                                   rhs=wqkv.ap(wc0, wc0 + 512),
                                   start=(dt == 0), stop=(dt == DT - 1))),
                          reads=xbf[sbi].reg(cofs, cofs + 128) + wqkv.reg(wc0, wc0 + 512),
                          writes=[breg], tag=f"qkv{it}_{oc}_{dt}")
                if oc < 2:
                    if sub >= 4:
                        emit_rotary(it, oc, bank, breg, is_q=True)
                elif oc < 4:
                    if sub >= 4:
                        emit_rotary(it, oc, bank, breg, is_q=False)
                elif sub >= 3:
                    h0 = 8 * (oc - 4)
                    vc0 = sl * 1040 + h0 * 65
                    vb.op("dve", (lambda e, bank=bank, vc0=vc0: e.tensor_copy(
                        out=vbuf.ap(vc0, vc0 + 8 * 65).rearrange("p (h d) -> p h d", d=65)[:, :, 0:64],
                        in_=bank[:, 0:512].rearrange("p (h d) -> p h d", d=64))),
                        reads=[breg], writes=vbuf.reg(vc0, vc0 + 8 * 65),
                        tag=f"vev{it}_{oc}")
            if sub >= 3:
                vb.op("dve", (lambda e, sl=sl: e.memset(
                    vbuf.ap(sl * 1040, sl * 1040 + 1040).rearrange("p (h d) -> p h d", d=65)[:, :, 64:65], 1.0)),
                    reads=[], writes=vbuf.reg(sl * 1040, sl * 1040 + 1040), tag=f"vone{it}")

            for hp in range(DT if sub >= 5 else 0):
                bank, breg = tpbank()
                vb.op("pe", (lambda e, bank=bank, hp=hp: e.transpose(
                    out=bank[:, 0:128], in_=knst.ap(hp * 128, hp * 128 + 128),
                    identity=idn.ap())),
                    reads=knst.reg(hp * 128, hp * 128 + 128) + idn.reg(),
                    writes=[breg], tag=f"ktp{it}_{hp}")
                kc0 = sl * D + hp * 128
                vb.op("dve", (lambda e, bank=bank, kc0=kc0: e.tensor_copy(
                    out=knf.ap(kc0, kc0 + 128), in_=bank[:, 0:128])),
                    reads=[breg], writes=knf.reg(kc0, kc0 + 128), tag=f"kte{it}_{hp}")

            if it < QT0 or stage < 2:
                continue

            ms = mskb[it % 2]
            vb.dma("pool", (lambda e, ms=ms, it=it: e.dma_start(
                out=ms.ap(), in_=msk_d[it])),
                reads=[("dram", "masks")], writes=ms.reg(), tag=f"ldm{it}")
            for hp in range(DT):
                bank, breg = tpbank()
                vb.op("pe", (lambda e, bank=bank, hp=hp: e.transpose(
                    out=bank[:, 0:128], in_=qnst.ap(hp * 128, hp * 128 + 128),
                    identity=idn.ap())),
                    reads=qnst.reg(hp * 128, hp * 128 + 128) + idn.reg(),
                    writes=[breg], tag=f"qtp{it}_{hp}")
                vb.op("dve", (lambda e, bank=bank, hp=hp: e.tensor_copy(
                    out=qnf.ap(hp * 128, hp * 128 + 128), in_=bank[:, 0:128])),
                    reads=[breg], writes=qnf.reg(hp * 128, hp * 128 + 128),
                    tag=f"qte{it}_{hp}")

            for h in range(16):
                hp, r0 = h // 2, (h % 2) * 64
                prb, prm, ptb, rcp = prbs_[h % 2], prms_[h % 2], ptbs_[h % 2], rcps_[h % 2]
                nrm = nrms[h % 2]
                sbank, sbreg = simbank()
                for kt in range(3):
                    ksl = (it - 2 + kt) % 3
                    kc0 = ksl * D + hp * 128
                    vb.op("pe", (lambda e, sbank=sbank, hp=hp, r0=r0, kt=kt, kc0=kc0:
                          e.matmul(out=sbank[:, 128 * kt: 128 * kt + 128],
                                   lhsT=qnf.prt(r0, r0 + 64, hp * 128, hp * 128 + 128),
                                   rhs=knf.prt(r0, r0 + 64, kc0, kc0 + 128),
                                   start=True, stop=True)),
                          reads=qnf.reg(hp * 128, hp * 128 + 128) + knf.reg(kc0, kc0 + 128),
                          writes=[sbreg], tag=f"sim{it}_{h}_{kt}")
                nM = fap(nrm, 32, 33)
                nMr = freg(nrm, 32, 33)
                vb.op("dve", (lambda e, sbank=sbank, nM=nM: e.reduce_max(
                    out=nM, in_=sbank[:, 0:WIN], axis=AX.X, negate=True)),
                    reads=[sbreg], writes=nMr, tag=f"mx{it}_{h}")
                nM8 = fap(nrm, 33, 34)
                nM8r = freg(nrm, 33, 34)
                vb.op("dve", (lambda e, nM=nM, nM8=nM8: e.tensor_scalar(
                    out=nM8, in0=nM, scalar1=QK_SCALE, scalar2=None, op0=ALU.mult)),
                    reads=nMr, writes=nM8r, tag=f"mx8{it}_{h}")
                vb.op("act", (lambda e, sbank=sbank, nM8=nM8, prb=prb: e.activation(
                    out=prb.ap(), in_=sbank[:, 0:WIN], func=AF.Exp, bias=nM8,
                    scale=sc8)),
                    reads=[sbreg] + nM8r + sc8_rg, writes=prb.reg(), tag=f"ex{it}_{h}")
                vb.op("dve", (lambda e, ms=ms, prm=prm, prb=prb: e.tensor_tensor(
                    out=prm.ap(), in0=prb.ap(), in1=ms.ap(), op=ALU.mult)),
                    reads=prb.reg() + ms.reg(), writes=prm.reg(), tag=f"mk{it}_{h}")
                if dbg and it == it_dbg and h == h_dbg:
                    vb.dma("sp", (lambda e, prm=prm: e.dma_start(out=dbg_prm, in_=prm.ap())),
                           reads=prm.reg(), writes=[("dram", "dprm")], tag="dbgprm")
                abank, abreg = avbank()
                for kt in range(3):
                    tbank, tbreg = tpbank()
                    vb.op("pe", (lambda e, tbank=tbank, kt=kt, prm=prm: e.transpose(
                        out=tbank[:, 0:128], in_=prm.ap(128 * kt, 128 * kt + 128),
                        identity=idn.ap())),
                        reads=prm.reg() + idn.reg(), writes=[tbreg], tag=f"ptp{it}_{h}_{kt}")
                    vb.op("dve", (lambda e, tbank=tbank, kt=kt, ptb=ptb: e.tensor_copy(
                        out=ptb.ap(128 * kt, 128 * kt + 128), in_=tbank[:, 0:128])),
                        reads=[tbreg], writes=ptb.reg(128 * kt, 128 * kt + 128),
                        tag=f"pte{it}_{h}_{kt}")
                for kt in range(3):
                    ksl = (it - 2 + kt) % 3
                    vc0 = ksl * 1040 + h * 65
                    vb.op("pe", (lambda e, abank=abank, kt=kt, vc0=vc0, ptb=ptb:
                          e.matmul(out=abank[:, 0:65],
                                   lhsT=ptb.ap(128 * kt, 128 * kt + 128),
                                   rhs=vbuf.ap(vc0, vc0 + 65),
                                   start=(kt == 0), stop=(kt == 2))),
                          reads=ptb.reg(128 * kt, 128 * kt + 128) + vbuf.reg(vc0, vc0 + 65),
                          writes=[abreg], tag=f"av{it}_{h}_{kt}")
                rc = fap(rcp, 0, 1)
                rcr = freg(rcp, 0, 1)
                vb.op("dve", (lambda e, abank=abank, rc=rc: e.reciprocal(
                    out=rc, in_=abank[:, 64:65])),
                    reads=[abreg], writes=rcr, tag=f"rcd{it}_{h}")
                vb.op("dve", (lambda e, abank=abank, rc=rc, h=h: e.tensor_scalar(
                    out=aot.ap(h * 64, h * 64 + 64), in0=abank[:, 0:64],
                    scalar1=rc, scalar2=None, op0=ALU.mult)),
                    reads=[abreg] + rcr, writes=aot.reg(h * 64, h * 64 + 64),
                    tag=f"sca{it}_{h}")

            if stage < 3:
                continue
            if dbg and it == it_dbg:
                vb.dma("sp", (lambda e: e.dma_start(out=dbg_qn, in_=qnst.ap())),
                       reads=qnst.reg(), writes=[("dram", "dqn")], tag="dbgqn")
                vb.dma("sp", (lambda e: e.dma_start(out=dbg_kn, in_=knst.ap())),
                       reads=knst.reg(), writes=[("dram", "dkn")], tag="dbgkn")
                vb.dma("sp", (lambda e: e.dma_start(out=dbg_aot, in_=aot.ap())),
                       reads=aot.reg(), writes=[("dram", "daot")], tag="dbgaot")
                sl_ = it % 4
                vb.dma("sp", (lambda e, sl_=sl_: e.dma_start(
                    out=dbg_v, in_=vbuf.ap(sl_ * 1040, sl_ * 1040 + 1040))),
                       reads=vbuf.reg(sl_ * 1040, sl_ * 1040 + 1040),
                       writes=[("dram", "dv")], tag="dbgv")
            for dt in range(DT):
                bank, breg = tpbank()
                vb.op("pe", (lambda e, bank=bank, dt=dt: e.transpose(
                    out=bank[:, 0:128], in_=aot.ap(dt * 128, dt * 128 + 128),
                    identity=idn.ap())),
                    reads=aot.reg(dt * 128, dt * 128 + 128) + idn.reg(),
                    writes=[breg], tag=f"atp{it}_{dt}")
                vb.op("dve", (lambda e, bank=bank, dt=dt: e.tensor_copy(
                    out=aof.ap(dt * 128, dt * 128 + 128), in_=bank[:, 0:128])),
                    reads=[breg], writes=aof.reg(dt * 128, dt * 128 + 128),
                    tag=f"ate{it}_{dt}")
            xmc0 = (it - 2) * 128
            for ot in range(DT):
                bank, breg = mmbank()
                for kt in range(DT):
                    wc0 = kt * D + ot * 128
                    vb.op("pe", (lambda e, bank=bank, kt=kt, wc0=wc0: e.matmul(
                        out=bank[:, 0:128],
                        lhsT=wo.ap(wc0, wc0 + 128),
                        rhs=aof.ap(kt * 128, kt * 128 + 128),
                        start=(kt == 0), stop=(kt == DT - 1))),
                        reads=wo.reg(wc0, wc0 + 128) + aof.reg(kt * 128, kt * 128 + 128),
                        writes=[breg], tag=f"wog{it}_{ot}_{kt}")
                vb.op("dve", (lambda e, bank=bank, ot=ot, sx=sx, xmc0=xmc0:
                      e.scalar_tensor_tensor(
                          out=fap(xmid, ot * XMC + xmc0, ot * XMC + xmc0 + 128),
                          in0=bank[:, 0:128], scalar=1.0,
                          in1=fap(xf[sx], ot * 129 + 1, ot * 129 + 129),
                          op0=ALU.mult, op1=ALU.add)),
                      reads=[breg] + freg(xf[sx], ot * 129 + 1, ot * 129 + 129),
                      writes=freg(xmid, ot * XMC + xmc0, ot * XMC + xmc0 + 128),
                      tag=f"xm{it}_{ot}")

        if dbg and stage >= 3:
            vb.dma("sp", (lambda e: e.dma_start(out=dbg_xmid, in_=fap(xmid))),
                   reads=freg(xmid), writes=[("dram", "dxmid")], tag="dbgxmid")
        A.pop()
        if stage < 1:
            zstg0 = falloc(512, "zstg0")
            vb.op("dve", (lambda e: e.memset(fap(zstg0), 0.0)), writes=freg(zstg0), tag="zz0")
            for ot in range(DT):
                for tcq in range(4):
                    vb.dma("sp", (lambda e, ot=ot, tcq=tcq: e.dma_start(
                        out=xo_d[ot * P: ot * P + P, tcq * 512: tcq * 512 + 512],
                        in_=fap(zstg0))),
                        reads=freg(zstg0), writes=[("dram", "xo")], tag=f"zz{ot}_{tcq}")
            stats = vb.finalize_and_emit(es, final_queue="pool")
            return nc, stats
        if stage < 4:
            zstg = falloc(512, "zstg")
            vb.op("dve", (lambda e: e.memset(fap(zstg), 0.0)), writes=freg(zstg), tag="z0")
            for ot in range(DT):
                for tcq in range(4):
                    vb.dma("sp", (lambda e, ot=ot, tcq=tcq: e.dma_start(
                        out=xo_d[ot * P: ot * P + P, tcq * 512: tcq * 512 + 512],
                        in_=fap(zstg))),
                        reads=freg(zstg), writes=[("dram", "xo")], tag=f"z{ot}_{tcq}")
            stats = vb.finalize_and_emit(es, final_queue="pool")
            return nc, stats

        xln = A.alloc(DT * T, "xln")
        A.push()
        abp = falloc(T, "Ab")
        bbp = falloc(T, "Bb")
        rws = falloc(4 * XMC, "rws")
        rws2 = falloc(2 * T, "rws2")
        sqf = falloc(512, "sqf")
        eps5, eps5_rg = cst(1e-5)

        chunks = [(0, 512), (512, 1024), (1024, 1536), (1536, 2048), (2048, 2176)]
        for ci, (c0, c1) in enumerate(chunks):
            w = c1 - c0
            for half in range(2):
                bank, breg = pbf[half], PSF[half]
                qbank, qbreg = pbf[2 + half], PSF[2 + half]
                for dd in range(4):
                    dt = half * 4 + dd
                    xc0 = dt * XMC + c0
                    vb.op("pe", (lambda e, bank=bank, xc0=xc0, w=w, dd=dd: e.matmul(
                        out=bank[0:1, 0:w], lhsT=fprt(ones, 0, P, 0, 1),
                        rhs=fap(xmid, xc0, xc0 + w),
                        start=(dd == 0), stop=(dd == 3))),
                        reads=freg(xmid, xc0, xc0 + w) + freg(ones, 0, 1),
                        writes=[breg], tag=f"ls{half}_{ci}_{dd}")
                    vb.op("act", (lambda e, xc0=xc0, w=w: e.activation(
                        out=fap(sqf, 0, w), in_=fap(xmid, xc0, xc0 + w), func=AF.Square)),
                        reads=freg(xmid, xc0, xc0 + w), writes=freg(sqf), tag=f"lq{half}_{ci}_{dd}")
                    vb.op("pe", (lambda e, qbank=qbank, w=w, dd=dd: e.matmul(
                        out=qbank[0:1, 0:w], lhsT=fprt(ones, 0, P, 0, 1),
                        rhs=fap(sqf, 0, w),
                        start=(dd == 0), stop=(dd == 3))),
                        reads=freg(sqf) + freg(ones, 0, 1),
                        writes=[qbreg], tag=f"lqs{half}_{ci}_{dd}")
                po = half * XMC + c0
                qo = (2 + half) * XMC + c0
                vb.op("dve", (lambda e, bank=bank, po=po, w=w: e.tensor_copy(
                    out=fprt(rws, 0, 1, po, po + w), in_=bank[0:1, 0:w])),
                    reads=[breg], writes=freg(rws, po, po + w), tag=f"lse{half}_{ci}")
                vb.op("dve", (lambda e, qbank=qbank, qo=qo, w=w: e.tensor_copy(
                    out=fprt(rws, 0, 1, qo, qo + w), in_=qbank[0:1, 0:w])),
                    reads=[qbreg], writes=freg(rws, qo, qo + w), tag=f"lqe{half}_{ci}")

        for cc in range(4):
            c0 = cc * 512
            s1 = fprt(rws, 0, 1, 0 * XMC + 128 + c0, 0 * XMC + 128 + c0 + 512)
            s2 = fprt(rws, 0, 1, 1 * XMC + 127 + c0, 1 * XMC + 127 + c0 + 512)
            q1 = fprt(rws, 0, 1, 2 * XMC + 128 + c0, 2 * XMC + 128 + c0 + 512)
            q2 = fprt(rws, 0, 1, 3 * XMC + 127 + c0, 3 * XMC + 127 + c0 + 512)
            mu = fprt(rws2, 0, 1, c0, c0 + 512)
            va = fprt(rws2, 0, 1, T + c0, T + c0 + 512)
            mur = freg(rws2)
            rall = freg(rws)
            vb.op("dve", (lambda e, s1=s1, s2=s2, mu=mu: e.tensor_tensor(
                out=mu, in0=s1, in1=s2, op=ALU.add)),
                reads=rall, writes=mur, tag=f"cmu{cc}")
            vb.op("dve", (lambda e, mu=mu: e.tensor_scalar(
                out=mu, in0=mu, scalar1=1.0 / D, scalar2=None, op0=ALU.mult)),
                reads=mur, writes=mur, tag=f"cmu2{cc}")
            vb.op("dve", (lambda e, q1=q1, q2=q2, va=va: e.tensor_tensor(
                out=va, in0=q1, in1=q2, op=ALU.add)),
                reads=rall, writes=mur, tag=f"cva{cc}")
            vb.op("dve", (lambda e, mu=mu: e.scalar_tensor_tensor(
                out=fap(sqf, 0, 512)[0:1, :], in0=mu, scalar=1.0, in1=mu,
                op0=ALU.mult, op1=ALU.mult)),
                reads=mur, writes=freg(sqf), tag=f"cm2{cc}")
            vb.op("dve", (lambda e, va=va: e.scalar_tensor_tensor(
                out=va, in0=va, scalar=1.0 / D, in1=fap(sqf, 0, 512)[0:1, :],
                op0=ALU.mult, op1=ALU.subtract)),
                reads=mur + freg(sqf), writes=mur, tag=f"cv2{cc}")
            vb.op("act", (lambda e, va=va: e.activation(
                out=va, in_=va, func=AF.Sqrt, bias=eps5[0:1, :])),
                reads=mur + eps5_rg, writes=mur, tag=f"cstd{cc}")
            vb.op("dve", (lambda e, va=va: e.reciprocal(out=va, in_=va)),
                reads=mur, writes=mur, tag=f"crs{cc}")
            vb.op("dve", (lambda e, mu=mu, va=va: e.scalar_tensor_tensor(
                out=mu, in0=mu, scalar=-1.0, in1=va, op0=ALU.mult, op1=ALU.mult)),
                reads=mur, writes=mur, tag=f"cnm{cc}")
            bka, brga = pbf[4], PSF[4]
            bkb, brgb = pbf[5], PSF[5]
            vb.op("pe", (lambda e, bka=bka, va=va: e.matmul(
                out=bka[:, 0:512], lhsT=fprt(ones, 0, 1), rhs=va, start=True, stop=True)),
                reads=mur + freg(ones), writes=[brga], tag=f"cba{cc}")
            vb.op("dve", (lambda e, bka=bka, c0=c0: e.tensor_copy(
                out=fap(abp, c0, c0 + 512), in_=bka[:, 0:512])),
                reads=[brga], writes=freg(abp, c0, c0 + 512), tag=f"cbae{cc}")
            vb.op("pe", (lambda e, bkb=bkb, mu=mu: e.matmul(
                out=bkb[:, 0:512], lhsT=fprt(ones, 0, 1), rhs=mu, start=True, stop=True)),
                reads=mur + freg(ones), writes=[brgb], tag=f"cbb{cc}")
            vb.op("dve", (lambda e, bkb=bkb, c0=c0: e.tensor_copy(
                out=fap(bbp, c0, c0 + 512), in_=bkb[:, 0:512])),
                reads=[brgb], writes=freg(bbp, c0, c0 + 512), tag=f"cbbe{cc}")

        for dt in range(DT):
            srcc0 = dt * XMC + (128 if dt < 4 else 127)
            yr = xln.reg(dt * T, (dt + 1) * T)
            vb.op("dve", (lambda e, dt=dt, srcc0=srcc0: e.tensor_tensor(
                out=xln.ap(dt * T, (dt + 1) * T),
                in0=fap(xmid, srcc0, srcc0 + T), in1=fap(abp), op=ALU.mult)),
                reads=freg(xmid, srcc0, srcc0 + T) + freg(abp), writes=yr,
                tag=f"la1_{dt}")
            vb.op("dve", (lambda e, dt=dt: e.tensor_tensor(
                out=xln.ap(dt * T, (dt + 1) * T), in0=xln.ap(dt * T, (dt + 1) * T),
                in1=fap(bbp), op=ALU.add)),
                reads=yr + freg(bbp), writes=yr, tag=f"la2_{dt}")
            vb.op("dve", (lambda e, dt=dt: e.tensor_scalar(
                out=xln.ap(dt * T, (dt + 1) * T), in0=xln.ap(dt * T, (dt + 1) * T),
                scalar1=fap(lng, dt, dt + 1), scalar2=fap(lnb, dt, dt + 1),
                op0=ALU.mult, op1=ALU.add)),
                reads=yr + freg(lng) + freg(lnb), writes=yr, tag=f"la3_{dt}")
        if dbg:
            vb.dma("sp", (lambda e: e.dma_start(out=dbg_xln, in_=xln.ap())),
                   reads=xln.reg(), writes=[("dram", "dxln")], tag="dbgxln")
        A.pop()
        if stage < 5:
            zstg = falloc(512, "zstg")
            vb.op("dve", (lambda e: e.memset(fap(zstg), 0.0)), writes=freg(zstg), tag="z0")
            for ot in range(DT):
                for tcq in range(4):
                    vb.dma("sp", (lambda e, ot=ot, tcq=tcq: e.dma_start(
                        out=xo_d[ot * P: ot * P + P, tcq * 512: tcq * 512 + 512],
                        in_=fap(zstg))),
                        reads=freg(zstg), writes=[("dram", "xo")], tag=f"z{ot}_{tcq}")
            stats = vb.finalize_and_emit(es, final_queue="pool")
            return nc, stats
        hbuf = A.alloc(NPT * 1024, "hbuf")
        glt = A.alloc(512, "glt")
        w1b = [A.alloc(DT * P, f"w1b{s}") for s in range(2)]
        w2b = [A.alloc(NPT * P, f"w2b{s}") for s in range(2)]
        stg = [falloc(512, f"stg{s}") for s in range(2)]

        for half in range(2):
            th0 = half * 1024
            for pt in range(NPT):
                for m in range(2):
                    ot = 2 * pt + m
                    wb = w1b[(2 * pt + m) % 2]
                    vb.dma("pool", (lambda e, wb=wb, ot=ot: e.dma_start(
                        out=wb.ap().rearrange("p (dt o) -> p dt o", o=P),
                        in_=w1_d[:, ot * P: ot * P + P].rearrange("(dt p) o -> p dt o", p=P))),
                        reads=[("dram", "w1")], writes=wb.reg(), tag=f"lw1_{half}_{ot}")
                for tc in range(2):
                    c0 = th0 + tc * 512
                    banka, brega = pbf[2 * (tc % 2)], PSF[2 * (tc % 2)]
                    bankg, bregg = pbf[2 * (tc % 2) + 1], PSF[2 * (tc % 2) + 1]
                    for m, (bank, breg) in ((0, (banka, brega)), (1, (bankg, bregg))):
                        wb = w1b[(2 * pt + m) % 2]
                        for dt in range(DT):
                            vb.op("pe", (lambda e, bank=bank, wb=wb, dt=dt, c0=c0:
                                  e.matmul(out=bank[:, 0:512],
                                           lhsT=wb.ap(dt * P, dt * P + P),
                                           rhs=xln.ap(dt * T + c0, dt * T + c0 + 512),
                                           start=(dt == 0), stop=(dt == DT - 1))),
                                  reads=wb.reg(dt * P, dt * P + P) + xln.reg(dt * T + c0, dt * T + c0 + 512),
                                  writes=[breg], tag=f"w1m{half}_{pt}_{m}_{tc}")
                    hc0 = pt * 1024 + tc * 512
                    vb.op("act", (lambda e, bankg=bankg, pt=pt: e.activation(
                        out=glt.ap(), in_=bankg[:, 0:512], func=AF.Gelu,
                        bias=fap(b1c, 2 * pt + 1, 2 * pt + 2))),
                        reads=[bregg] + freg(b1c), writes=glt.reg(), tag=f"gel{half}_{pt}_{tc}")
                    vb.op("dve", (lambda e, banka=banka, pt=pt, hc0=hc0:
                          e.scalar_tensor_tensor(
                              out=hbuf.ap(hc0, hc0 + 512), in0=banka[:, 0:512],
                              scalar=fap(b1c, 2 * pt, 2 * pt + 1), in1=glt.ap(),
                              op0=ALU.add, op1=ALU.mult)),
                          reads=[brega] + glt.reg() + freg(b1c),
                          writes=hbuf.reg(hc0, hc0 + 512), tag=f"glu{half}_{pt}_{tc}")

            if dbg and half == 0:
                vb.dma("sp", (lambda e: e.dma_start(out=dbg_h, in_=hbuf.ap())),
                       reads=hbuf.reg(), writes=[("dram", "dh")], tag="dbgh")
            for ot in range(DT):
                wb = w2b[ot % 2]
                vb.dma("pool", (lambda e, wb=wb, ot=ot: e.dma_start(
                    out=wb.ap().rearrange("p (kt o) -> p kt o", o=P),
                    in_=w2_d[:, ot * P: ot * P + P].rearrange("(kt p) o -> p kt o", p=P))),
                    reads=[("dram", "w2")], writes=wb.reg(), tag=f"lw2_{half}_{ot}")
                for tc in range(2):
                    c0 = th0 + tc * 512
                    bank, breg = pbf[4 + (rot["mm"] % 2)], PSF[4 + (rot["mm"] % 2)]
                    rot["mm"] += 1
                    for kt in range(NPT):
                        vb.op("pe", (lambda e, bank=bank, wb=wb, kt=kt, c0=c0, th0=th0:
                              e.matmul(out=bank[:, 0:512],
                                       lhsT=wb.ap(kt * P, kt * P + P),
                                       rhs=hbuf.ap(kt * 1024 + (c0 - th0), kt * 1024 + (c0 - th0) + 512),
                                       start=(kt == 0), stop=(kt == NPT - 1))),
                              reads=wb.reg(kt * P, kt * P + P) + hbuf.reg(kt * 1024 + (c0 - th0), kt * 1024 + (c0 - th0) + 512),
                              writes=[breg], tag=f"w2m{half}_{ot}_{tc}_{kt}")
                    if dbg and half == 0 and ot == 0 and tc == 0:
                        vb.dma("sp", (lambda e, wb=wb: e.dma_start(out=dbg_w2, in_=wb.ap())),
                               reads=wb.reg(), writes=[("dram", "dw2")], tag="dbgw2")
                        vb.op("dve", (lambda e, bank=bank: e.tensor_copy(
                            out=fap(stg[1]), in_=bank[:, 0:512])),
                            reads=[breg], writes=freg(stg[1]), tag="dbgpscp")
                        vb.dma("sp", (lambda e: e.dma_start(out=dbg_ps, in_=fap(stg[1]))),
                               reads=freg(stg[1]), writes=[("dram", "dps")], tag="dbgps")
                    st = stg[tc % 2]
                    xmc0 = ot * XMC + 128 + c0
                    vb.op("dve", (lambda e, bank=bank, st=st, ot=ot, xmc0=xmc0:
                          e.scalar_tensor_tensor(
                              out=fap(st), in0=bank[:, 0:512],
                              scalar=fap(b2c, ot, ot + 1), in1=fap(xmid, xmc0, xmc0 + 512),
                              op0=ALU.add, op1=ALU.add)),
                          reads=[breg] + freg(b2c) + freg(xmid, xmc0, xmc0 + 512),
                          writes=freg(st), tag=f"w2e{half}_{ot}_{tc}")
                    vb.dma("sp", (lambda e, st=st, ot=ot, c0=c0: e.dma_start(
                        out=xo_d[ot * P: ot * P + P, c0: c0 + 512], in_=fap(st))),
                        reads=freg(st), writes=[("dram", "xo")], tag=f"sto{half}_{ot}_{tc}")

        stats = vb.finalize_and_emit(es, final_queue="pool")
    return nc, stats



def host_eye():
    return np.eye(P, dtype=np.float32)


def host_l0_masks(h):
    m = np.zeros((NTT, P, WIN), np.float32)
    for it in range(QT0, NTT):
        k = np.arange(P)[:, None]
        c = np.arange(WIN)[None, :]
        band = (c >= k) & (c <= k + 256)
        if h == 0:
            keyrow = it * P - 256 + c
            valid = band & ((keyrow >= HALO) | (c == k + 256))
        else:
            valid = band
        m[it] = valid.astype(np.float32)
    return m


def host_l0_tables(h):
    r = np.arange(TE)
    pos = np.clip(h * T + r - HALO, 0, SEQ - 1).astype(np.float64)
    half = np.arange(0, 64, 2, dtype=np.float64)
    inv_freq = 1.0 / 10000.0 ** (half / 64.0)
    f32_ = lambda a: a.astype(np.float32)
    fr = pos[:, None] * inv_freq[None, :]
    fr = np.concatenate([fr, fr], -1)
    base = (half + 0.4 * 64) / (1.4 * 64)
    power = (pos - SEQ // 2) / (256 // 2)
    scale = base[None, :] ** power[:, None]
    scale = np.concatenate([scale, scale], -1)
    sign = np.where(np.arange(64) < 32, -1.0, 1.0)[None, :]
    cosf, sinf = np.cos(fr), np.sin(fr)
    aq = f32_(cosf * scale)
    bq = f32_(sign * sinf * scale)
    ak = f32_(cosf / scale)
    bk = f32_(sign * sinf / scale)
    return np.stack([aq, bq, ak, bk], 0)


def host_l0_w1p(W1):
    a = np.zeros((D, DIP), np.float32)
    g = np.zeros((D, DIP), np.float32)
    a[:, :2730] = W1[:, :2730]
    g[:, :2730] = W1[:, 2730:]
    out = np.zeros((D, W1C), np.float32)
    for pt in range(NPT):
        out[:, (2 * pt) * P:(2 * pt + 1) * P] = a[:, pt * P:(pt + 1) * P]
        out[:, (2 * pt + 1) * P:(2 * pt + 2) * P] = g[:, pt * P:(pt + 1) * P]
    return out


def host_l0_b1p(b1):
    a = np.zeros(DIP, np.float32); a[:2730] = b1[:2730]
    g = np.zeros(DIP, np.float32); g[:2730] = b1[2730:]
    out = np.zeros((P, 2 * NPT), np.float32)
    for pt in range(NPT):
        out[:, 2 * pt] = a[pt * P:(pt + 1) * P]
        out[:, 2 * pt + 1] = g[pt * P:(pt + 1) * P]
    return out


def host_l0_w2p(W2):
    return np.pad(W2, ((0, DIP - 2730), (0, 0))).astype(np.float32)





NE = 8
EH = 2048


def build_l1ffn_program(dbg=False):
    nc = bass.Bass("TRN2", target_bir_lowering=False, debug=False)
    x_d = nc.dram_tensor("x_fm", [D, T], F32, kind="ExternalInput").ap()
    gw_d = nc.dram_tensor("gate_w", [D, NE], F32, kind="ExternalInput").ap()
    w1_d = nc.dram_tensor("e_w1", [NE, D, EH], F32, kind="ExternalInput").ap()
    b1_d = nc.dram_tensor("e_b1", [NE, P, EH // P], F32, kind="ExternalInput").ap()
    w2_d = nc.dram_tensor("e_w2", [NE, EH, D], F32, kind="ExternalInput").ap()
    b2_d = nc.dram_tensor("e_b2", [NE, P, DT], F32, kind="ExternalInput").ap()
    lg_d = nc.dram_tensor("e_lng", [NE, P, DT], F32, kind="ExternalInput").ap()
    lb_d = nc.dram_tensor("e_lnb", [NE, P, DT], F32, kind="ExternalInput").ap()
    ones_d = nc.dram_tensor("cst_ones", [P, P], F32, kind="ExternalInput").ap()
    eye_d = nc.dram_tensor("cst_eye", [P, P], F32, kind="ExternalInput").ap()
    xo_d = nc.dram_tensor("xo_fm", [D, T], F32, kind="ExternalOutput").ap()
    if dbg:
        dbg_xln = nc.dram_tensor("dbg_xln", [P, DT * T], BF16, kind="ExternalOutput").ap()
        dbg_sc = nc.dram_tensor("dbg_sc", [P, 16 * NE], F32, kind="ExternalOutput").ap()
        dbg_wE = nc.dram_tensor("dbg_wE", [P, 16 * NE], F32, kind="ExternalOutput").ap()
        dbg_web = nc.dram_tensor("dbg_web", [P, 1024], F32, kind="ExternalOutput").ap()
        dbg_z = nc.dram_tensor("dbg_z", [P, DT * 1024], F32, kind="ExternalOutput").ap()
        dbg_h = nc.dram_tensor("dbg_h", [P, 16 * 1024], BF16, kind="ExternalOutput").ap()
        dbg_acc = nc.dram_tensor("dbg_acc", [P, DT * 1024], F32, kind="ExternalOutput").ap()

    es = ExitStack()
    with es:
        A = Arena(nc, es, "arn", 103 * 1024, BF16)
        vb = VB(nc)

        def falloc(cols, tag):
            return A.alloc(2 * cols, tag)

        def fap(buf, c0=0, c1=None):
            c1 = buf.cols // 2 if c1 is None else c1
            return buf.arena[:, buf.off + 2 * c0: buf.off + 2 * c1].bitcast(F32)

        def fprt(buf, p0, p1, c0=0, c1=None):
            c1 = buf.cols // 2 if c1 is None else c1
            return buf.arena[p0:p1, buf.off + 2 * c0: buf.off + 2 * c1].bitcast(F32)

        def freg(buf, c0=0, c1=None):
            c1 = buf.cols // 2 if c1 is None else c1
            return buf.reg(2 * c0, 2 * c1)

        cstb = falloc(16, "consts")
        ones = falloc(P, "ones")
        eyef = falloc(P, "eyef")
        gwb = A.alloc(DT * NE, "gw")
        b1c = [falloc(EH // P, f"b1_{e}") for e in range(NE)]
        b2c = [falloc(DT, f"b2_{e}") for e in range(NE)]
        lgc = [falloc(DT, f"lg_{e}") for e in range(NE)]
        lbc = [falloc(DT, f"lb_{e}") for e in range(NE)]
        xlnb = A.alloc(DT * T, "xlnb")
        wE = falloc(16 * NE, "wE")
        scb = falloc(16 * NE, "scb")
        g1 = falloc(16, "g1")
        g2 = falloc(16, "g2")
        g3 = falloc(16, "g3")

        cst_map = {}

        def cst(value):
            value = float(value)
            if value not in cst_map:
                i = len(cst_map)
                assert i < 16
                ap = fap(cstb, i, i + 1)
                rg = freg(cstb, i, i + 1)
                vb.op("dve", (lambda e, ap=ap, value=value: e.memset(ap, value)),
                      reads=[], writes=rg, tag=f"cst{value}")
                cst_map[value] = (ap, rg)
            return cst_map[value]

        pbf = [es.enter_context(nc.psum_tensor(f"pf{i}", [P, 512], F32)) for i in range(8)]
        PSF = [("psf", i) for i in range(8)]
        rot = {"mm": 0}

        def mmbank(lo=0, n=4):
            i = lo + rot["mm"] % n
            rot["mm"] += 1
            return pbf[i], PSF[i]

        vb.dma("sp", lambda e: e.dma_start(out=fap(ones), in_=ones_d),
               writes=freg(ones), tag="ld_ones")
        vb.dma("sp", lambda e: e.dma_start(out=fap(eyef), in_=eye_d),
               writes=freg(eyef), tag="ld_eye")
        vb.dma("pool", (lambda e: e.dma_start(
            out=gwb.ap().rearrange("p (dt e) -> p dt e", e=NE),
            in_=gw_d.rearrange("(dt p) e -> p dt e", p=P))),
            writes=gwb.reg(), tag="ld_gw")
        for e_ in range(NE):
            for dst, srcd in ((b1c[e_], b1_d[e_]), (b2c[e_], b2_d[e_]),
                              (lgc[e_], lg_d[e_]), (lbc[e_], lb_d[e_])):
                vb.dma("sp", (lambda en, dst=dst, srcd=srcd: en.dma_start(
                    out=fap(dst), in_=srcd)),
                    writes=freg(dst), tag=f"ldv{e_}")

        A.push()
        xres = falloc(DT * T, "xres")
        abp = falloc(T, "Ab")
        bbp = falloc(T, "Bb")
        rows = falloc(2 * T, "rows")
        sqs = falloc(512, "sqs")
        vb.dma("sp", (lambda e: e.dma_start(
            out=fap(xres).rearrange("p (dt c) -> p dt c", c=T),
            in_=x_d.rearrange("(dt p) c -> p dt c", p=P))),
            reads=[("dram", "x")], writes=freg(xres), tag="ld_x")

        class _BufView:
            def __init__(self, buf, fap, freg):
                self.b, self._fap, self._freg = buf, fap, freg
                self.cols = buf.cols // 2

            def ap(self, c0=0, c1=None):
                return self._fap(self.b, c0, c1)

            def prt(self, p0, p1, c0=0, c1=None):
                return fprt(self.b, p0, p1, c0, c1)

            def reg(self, c0=0, c1=None):
                return self._freg(self.b, c0, c1)

        xv = _BufView(xres, fap, freg)
        av = _BufView(abp, fap, freg)
        bv = _BufView(bbp, fap, freg)
        rv = _BufView(rows, fap, freg)
        sv = _BufView(sqs, fap, freg)

        class _CstAdapter:
            def col(self, v):
                return cst(v)

        emit_ln_fm(vb, pbf[:4], PSF[:4], xv, xlnb, rv, av, bv,
                   ones_col=fprt(ones, 0, P, 0, 1), ones_col_reg=freg(ones, 0, 1),
                   ones_row=fprt(ones, 0, 1, 0, P), ones_row_reg=freg(ones, 0, P),
                   TT=T, eps=1e-5, consts=_CstAdapter(), sq_tmp=sv, prefix="mln")

        half_c, half_rg = cst(0.5)
        for tt in range(16):
            bank, breg = mmbank()
            for dt in range(DT):
                vb.op("pe", (lambda e, bank=bank, dt=dt, tt=tt: e.matmul(
                    out=bank[:, 0:NE],
                    lhsT=xlnb.ap(dt * T + tt * P, dt * T + tt * P + P),
                    rhs=gwb.ap(dt * NE, dt * NE + NE),
                    start=(dt == 0), stop=(dt == DT - 1))),
                    reads=xlnb.reg(dt * T + tt * P, dt * T + tt * P + P) + gwb.reg(),
                    writes=[breg], tag=f"gt{tt}_{dt}")
            sc = fap(scb, tt * NE, tt * NE + NE)
            scr = freg(scb, tt * NE, tt * NE + NE)
            vb.op("dve", (lambda e, bank=bank, sc=sc: e.tensor_copy(
                out=sc, in_=bank[:, 0:NE])),
                reads=[breg], writes=scr, tag=f"gte{tt}")
            m1 = fap(g1, 0, 1); m1r = freg(g1, 0, 1)
            m2 = fap(g1, 1, 2); m2r = freg(g1, 1, 2)
            w1g = fap(g1, 2, 3); w1r = freg(g1, 2, 3)
            w2g = fap(g1, 3, 4); w2r = freg(g1, 3, 4)
            eq1 = fap(g2, 0, NE); eq1r = freg(g2, 0, NE)
            eq2 = fap(g3, 0, NE); eq2r = freg(g3, 0, NE)
            vb.op("dve", (lambda e, sc=sc, m1=m1: e.reduce_max(
                out=m1, in_=sc, axis=AX.X)),
                reads=scr, writes=m1r, tag=f"g_m1_{tt}")
            vb.op("dve", (lambda e, sc=sc, m1=m1, eq1=eq1: e.tensor_scalar(
                out=eq1, in0=sc, scalar1=m1, scalar2=None, op0=ALU.is_equal)),
                reads=scr + m1r, writes=eq1r, tag=f"g_e1_{tt}")
            vb.op("dve", (lambda e, sc=sc, eq1=eq1, m2=m2: e.scalar_tensor_tensor(
                out=fap(g2, NE, 2 * NE), in0=eq1, scalar=-1e30, in1=sc,
                op0=ALU.mult, op1=ALU.add)),
                reads=eq1r + scr, writes=freg(g2, NE, 2 * NE), tag=f"g_msk_{tt}")
            vb.op("dve", (lambda e, m2=m2: e.reduce_max(
                out=m2, in_=fap(g2, NE, 2 * NE), axis=AX.X)),
                reads=freg(g2, NE, 2 * NE), writes=m2r, tag=f"g_m2_{tt}")
            vb.op("dve", (lambda e, m2=m2, eq2=eq2: e.tensor_scalar(
                out=eq2, in0=fap(g2, NE, 2 * NE), scalar1=m2, scalar2=None,
                op0=ALU.is_equal)),
                reads=freg(g2, NE, 2 * NE) + m2r, writes=eq2r, tag=f"g_e2_{tt}")
            vb.op("dve", (lambda e, m1=m1, m2=m2, w1g=w1g: e.tensor_tensor(
                out=w1g, in0=m1, in1=m2, op=ALU.subtract)),
                reads=m1r + m2r, writes=w1r, tag=f"g_d_{tt}")
            vb.op("act", (lambda e, w1g=w1g: e.activation(
                out=w1g, in_=w1g, func=AF.Sigmoid)),
                reads=w1r, writes=w1r, tag=f"g_s_{tt}")
            vb.op("dve", (lambda e, w1g=w1g, w2g=w2g: e.tensor_scalar(
                out=w2g, in0=w1g, scalar1=-1.0, scalar2=1.0, op0=ALU.mult,
                op1=ALU.add)),
                reads=w1r, writes=w2r, tag=f"g_w2_{tt}")
            wEsl = fap(wE, tt * NE, tt * NE + NE)
            wEr = freg(wE, tt * NE, tt * NE + NE)
            vb.op("dve", (lambda e, eq2=eq2, w2g=w2g, wEsl=wEsl: e.tensor_scalar(
                out=wEsl, in0=eq2, scalar1=w2g, scalar2=None, op0=ALU.mult)),
                reads=eq2r + w2r, writes=wEr, tag=f"g_wa_{tt}")
            vb.op("dve", (lambda e, eq1=eq1, w1g=w1g, wEsl=wEsl: e.scalar_tensor_tensor(
                out=wEsl, in0=eq1, scalar=w1g, in1=wEsl, op0=ALU.mult, op1=ALU.add)),
                reads=eq1r + w1r + wEr, writes=wEr, tag=f"g_wb_{tt}")

        if dbg:
            vb.dma("sp", (lambda e: e.dma_start(out=dbg_xln, in_=xlnb.ap())),
                   reads=xlnb.reg(), writes=[("dram", "dxln")], tag="dxln")
            vb.dma("sp", (lambda e: e.dma_start(out=dbg_sc, in_=fap(scb))),
                   reads=freg(scb), writes=[("dram", "dsc")], tag="dsc")
            vb.dma("sp", (lambda e: e.dma_start(out=dbg_wE, in_=fap(wE))),
                   reads=freg(wE), writes=[("dram", "dwE")], tag="dwE")
        A.pop()

        hbe = A.alloc(16 * 1024, "hbe")
        zbes = [falloc(DT * 1024, f"zbe{s}") for s in range(2)]
        acc = falloc(DT * 1024, "acc")
        ab2 = falloc(1024, "ab2")
        bb2 = falloc(1024, "bb2")
        rws = falloc(2 * 1024, "rws")
        sqf = falloc(512, "sqf")
        w1b = [A.alloc(DT * P, f"w1b{s}") for s in range(2)]
        w2b = [A.alloc(16 * P, f"w2b{s}") for s in range(2)]
        web = falloc(1024, "web")
        stg = [falloc(512, f"stg{s}") for s in range(2)]
        eps6, eps6_rg = cst(1e-6)

        for half in range(2):
            th0 = half * 1024
            for dt in range(DT):
                vb.op("dve", (lambda e, dt=dt: e.memset(
                    fap(acc, dt * 1024, (dt + 1) * 1024), 0.0)),
                    writes=freg(acc, dt * 1024, (dt + 1) * 1024), tag=f"z{half}_{dt}")
            for e_ in range(NE):
                zbe = zbes[e_ % 2]
                for ot in range(16):
                    wb = w1b[ot % 2]
                    vb.dma("pool", (lambda en, wb=wb, e_=e_, ot=ot: en.dma_start(
                        out=wb.ap().rearrange("p (dt o) -> p dt o", o=P),
                        in_=w1_d[e_][:, ot * P: ot * P + P].rearrange("(dt p) o -> p dt o", p=P))),
                        reads=[("dram", "w1")], writes=wb.reg(), tag=f"lw1_{e_}_{ot}")
                    for tc in range(2):
                        c0 = th0 + tc * 512
                        bank, breg = mmbank(0, 4)
                        for dt in range(DT):
                            vb.op("pe", (lambda en, bank=bank, wb=wb, dt=dt, c0=c0:
                                  en.matmul(out=bank[:, 0:512],
                                            lhsT=wb.ap(dt * P, dt * P + P),
                                            rhs=xlnb.ap(dt * T + c0, dt * T + c0 + 512),
                                            start=(dt == 0), stop=(dt == DT - 1))),
                                  reads=wb.reg(dt * P, dt * P + P) + xlnb.reg(dt * T + c0, dt * T + c0 + 512),
                                  writes=[breg], tag=f"w1m{e_}_{ot}_{tc}")
                        hc0 = ot * 1024 + tc * 512
                        vb.op("act", (lambda en, bank=bank, e_=e_, ot=ot, hc0=hc0:
                              en.activation(out=hbe.ap(hc0, hc0 + 512), in_=bank[:, 0:512],
                                            func=AF.Gelu, bias=fap(b1c[e_], ot, ot + 1))),
                              reads=[breg] + freg(b1c[e_]), writes=hbe.reg(hc0, hc0 + 512),
                              tag=f"gel{e_}_{ot}_{tc}")
                for ot in range(DT):
                    wb = w2b[ot % 2]
                    vb.dma("pool", (lambda en, wb=wb, e_=e_, ot=ot: en.dma_start(
                        out=wb.ap().rearrange("p (kt o) -> p kt o", o=P),
                        in_=w2_d[e_][:, ot * P: ot * P + P].rearrange("(kt p) o -> p kt o", p=P))),
                        reads=[("dram", "w2")], writes=wb.reg(), tag=f"lw2_{e_}_{ot}")
                    for tc in range(2):
                        c0 = th0 + tc * 512
                        bank, breg = mmbank(4, 2)
                        for kt in range(16):
                            vb.op("pe", (lambda en, bank=bank, wb=wb, kt=kt, tc=tc:
                                  en.matmul(out=bank[:, 0:512],
                                            lhsT=wb.ap(kt * P, kt * P + P),
                                            rhs=hbe.ap(kt * 1024 + tc * 512, kt * 1024 + tc * 512 + 512),
                                            start=(kt == 0), stop=(kt == 15))),
                                  reads=wb.reg(kt * P, kt * P + P) + hbe.reg(kt * 1024 + tc * 512, kt * 1024 + tc * 512 + 512),
                                  writes=[breg], tag=f"w2m{e_}_{ot}_{tc}")
                        zc0 = ot * 1024 + tc * 512
                        vb.op("dve", (lambda en, bank=bank, e_=e_, ot=ot, zc0=zc0, c0=c0, zbe=zbe:
                              en.scalar_tensor_tensor(
                                  out=fap(zbe, zc0, zc0 + 512), in0=bank[:, 0:512],
                                  scalar=fap(b2c[e_], ot, ot + 1),
                                  in1=xlnb.ap(ot * T + c0, ot * T + c0 + 512),
                                  op0=ALU.add, op1=ALU.add)),
                              reads=[breg] + freg(b2c[e_]) + xlnb.reg(ot * T + c0, ot * T + c0 + 512),
                              writes=freg(zbe, zc0, zc0 + 512), tag=f"w2e{e_}_{ot}_{tc}")
                for ci in range(2):
                    c0 = ci * 512
                    bank, breg = pbf[6], PSF[6]
                    qbank, qbreg = pbf[7], PSF[7]
                    for dt in range(DT):
                        zc0 = dt * 1024 + c0
                        vb.op("pe", (lambda en, bank=bank, zc0=zc0, dt=dt, zbe=zbe: en.matmul(
                            out=bank[0:1, 0:512], lhsT=fprt(ones, 0, P, 0, 1),
                            rhs=fap(zbe, zc0, zc0 + 512),
                            start=(dt == 0), stop=(dt == DT - 1))),
                            reads=freg(zbe, zc0, zc0 + 512) + freg(ones, 0, 1),
                            writes=[breg], tag=f"es{e_}_{ci}_{dt}")
                        vb.op("act", (lambda en, zc0=zc0, zbe=zbe: en.activation(
                            out=fap(sqf), in_=fap(zbe, zc0, zc0 + 512), func=AF.Square)),
                            reads=freg(zbe, zc0, zc0 + 512), writes=freg(sqf),
                            tag=f"eq{e_}_{ci}_{dt}")
                        vb.op("pe", (lambda en, qbank=qbank, dt=dt: en.matmul(
                            out=qbank[0:1, 0:512], lhsT=fprt(ones, 0, P, 0, 1),
                            rhs=fap(sqf),
                            start=(dt == 0), stop=(dt == DT - 1))),
                            reads=freg(sqf) + freg(ones, 0, 1),
                            writes=[qbreg], tag=f"eqs{e_}_{ci}_{dt}")
                    mu = fprt(rws, 0, 1, c0, c0 + 512)
                    va = fprt(rws, 0, 1, 1024 + c0, 1024 + c0 + 512)
                    mur = freg(rws)
                    vb.op("dve", (lambda en, bank=bank, mu=mu: en.tensor_scalar(
                        out=mu, in0=bank[0:1, 0:512], scalar1=1.0 / D, scalar2=None,
                        op0=ALU.mult)),
                        reads=[breg], writes=mur, tag=f"emu{e_}_{ci}")
                    vb.op("dve", (lambda en, mu=mu, va=va: en.tensor_tensor(
                        out=va, in0=mu, in1=mu, op=ALU.mult)),
                        reads=mur, writes=mur, tag=f"em2{e_}_{ci}")
                    vb.op("dve", (lambda en, qbank=qbank, va=va: en.scalar_tensor_tensor(
                        out=va, in0=qbank[0:1, 0:512], scalar=1.0 / D, in1=va,
                        op0=ALU.mult, op1=ALU.subtract)),
                        reads=[qbreg] + mur, writes=mur, tag=f"ev{e_}_{ci}")
                    vb.op("act", (lambda en, va=va: en.activation(
                        out=va, in_=va, func=AF.Sqrt, bias=eps6[0:1, :])),
                        reads=mur + eps6_rg, writes=mur, tag=f"est{e_}_{ci}")
                    vb.op("dve", (lambda en, va=va: en.reciprocal(out=va, in_=va)),
                        reads=mur, writes=mur, tag=f"ers{e_}_{ci}")
                    vb.op("dve", (lambda en, mu=mu, va=va: en.scalar_tensor_tensor(
                        out=mu, in0=mu, scalar=-1.0, in1=va, op0=ALU.mult, op1=ALU.mult)),
                        reads=mur, writes=mur, tag=f"enm{e_}_{ci}")
                    bka, brga = pbf[6], PSF[6]
                    bkb, brgb = pbf[7], PSF[7]
                    vb.op("pe", (lambda en, bka=bka, va=va: en.matmul(
                        out=bka[:, 0:512], lhsT=fprt(ones, 0, 1), rhs=va,
                        start=True, stop=True)),
                        reads=mur + freg(ones), writes=[brga], tag=f"eba{e_}_{ci}")
                    vb.op("dve", (lambda en, bka=bka, c0=c0: en.tensor_copy(
                        out=fap(ab2, c0, c0 + 512), in_=bka[:, 0:512])),
                        reads=[brga], writes=freg(ab2, c0, c0 + 512), tag=f"ebae{e_}_{ci}")
                    vb.op("pe", (lambda en, bkb=bkb, mu=mu: en.matmul(
                        out=bkb[:, 0:512], lhsT=fprt(ones, 0, 1), rhs=mu,
                        start=True, stop=True)),
                        reads=mur + freg(ones), writes=[brgb], tag=f"ebb{e_}_{ci}")
                    vb.op("dve", (lambda en, bkb=bkb, c0=c0: en.tensor_copy(
                        out=fap(bb2, c0, c0 + 512), in_=bkb[:, 0:512])),
                        reads=[brgb], writes=freg(bb2, c0, c0 + 512), tag=f"ebbe{e_}_{ci}")
                for sc_ in range(2):
                    bank2, breg2 = pbf[6], PSF[6]
                    for q4 in range(4):
                        tt = half * 8 + sc_ * 4 + q4
                        vb.op("pe", (lambda en, bank2=bank2, tt=tt, q4=q4, e_=e_: en.matmul(
                            out=bank2[0:1, q4 * P: q4 * P + P],
                            lhsT=fap(wE, tt * NE + e_, tt * NE + e_ + 1),
                            rhs=fap(eyef),
                            start=True, stop=True)),
                            reads=freg(wE) + freg(eyef), writes=[breg2], tag=f"wt{e_}_{sc_}_{q4}")
                    vb.op("dve", (lambda en, bank2=bank2, sc_=sc_: en.tensor_copy(
                        out=fprt(web, 0, 1, sc_ * 512, sc_ * 512 + 512),
                        in_=bank2[0:1, 0:512])),
                        reads=[breg2], writes=freg(web, sc_ * 512, sc_ * 512 + 512),
                        tag=f"wte{e_}_{sc_}")
                for sc_ in range(2):
                    bank3, breg3 = pbf[7], PSF[7]
                    vb.op("pe", (lambda en, bank3=bank3, sc_=sc_: en.matmul(
                        out=bank3[:, 0:512], lhsT=fprt(ones, 0, 1),
                        rhs=fprt(web, 0, 1, sc_ * 512, sc_ * 512 + 512),
                        start=True, stop=True)),
                        reads=freg(web, sc_ * 512, sc_ * 512 + 512) + freg(ones),
                        writes=[breg3], tag=f"wb{e_}_{sc_}")
                    vb.op("dve", (lambda en, bank3=bank3, sc_=sc_: en.tensor_copy(
                        out=fap(web, sc_ * 512, sc_ * 512 + 512), in_=bank3[:, 0:512])),
                        reads=[breg3], writes=freg(web, sc_ * 512, sc_ * 512 + 512),
                        tag=f"wbe{e_}_{sc_}")
                if dbg and half == 0 and e_ == 0:
                    vb.dma("sp", (lambda e: e.dma_start(out=dbg_web, in_=fap(web))),
                           reads=freg(web), writes=[("dram", "dweb")], tag="dweb")
                    vb.dma("sp", (lambda e, zbe=zbe: e.dma_start(out=dbg_z, in_=fap(zbe))),
                           reads=freg(zbe), writes=[("dram", "dz")], tag="dz")
                    vb.dma("sp", (lambda e: e.dma_start(out=dbg_h, in_=hbe.ap())),
                           reads=hbe.reg(), writes=[("dram", "dh")], tag="dhh")
                for dt in range(DT):
                    zc0 = dt * 1024
                    t1 = fap(zbe, zc0, zc0 + 1024)
                    t1r = freg(zbe, zc0, zc0 + 1024)
                    vb.op("dve", (lambda en, t1=t1: en.tensor_tensor(
                        out=t1, in0=t1, in1=fap(ab2), op=ALU.mult)),
                        reads=t1r + freg(ab2), writes=t1r, tag=f"ea1{e_}_{dt}")
                    vb.op("dve", (lambda en, t1=t1: en.tensor_tensor(
                        out=t1, in0=t1, in1=fap(bb2), op=ALU.add)),
                        reads=t1r + freg(bb2), writes=t1r, tag=f"ea2{e_}_{dt}")
                    vb.op("dve", (lambda en, t1=t1, e_=e_, dt=dt: en.tensor_scalar(
                        out=t1, in0=t1, scalar1=fap(lgc[e_], dt, dt + 1),
                        scalar2=fap(lbc[e_], dt, dt + 1), op0=ALU.mult, op1=ALU.add)),
                        reads=t1r + freg(lgc[e_]) + freg(lbc[e_]), writes=t1r,
                        tag=f"ea3{e_}_{dt}")
                    vb.op("dve", (lambda en, t1=t1: en.tensor_tensor(
                        out=t1, in0=t1, in1=fap(web), op=ALU.mult)),
                        reads=t1r + freg(web), writes=t1r, tag=f"ea4{e_}_{dt}")
                    ac = fap(acc, dt * 1024, (dt + 1) * 1024)
                    acr = freg(acc, dt * 1024, (dt + 1) * 1024)
                    vb.op("dve", (lambda en, t1=t1, ac=ac: en.tensor_tensor(
                        out=ac, in0=ac, in1=t1, op=ALU.add)),
                        reads=acr + t1r, writes=acr, tag=f"ea5{e_}_{dt}")
            if dbg and half == 0:
                vb.dma("sp", (lambda e: e.dma_start(out=dbg_acc, in_=fap(acc))),
                       reads=freg(acc), writes=[("dram", "dacc")], tag="dacc")
            for dt in range(DT):
                for tc in range(2):
                    c0 = th0 + tc * 512
                    st = stg[tc % 2]
                    vb.dma("sp", (lambda en, st=st, dt=dt, c0=c0: en.dma_start(
                        out=fap(st), in_=x_d[dt * P: dt * P + P, c0: c0 + 512])),
                        reads=[("dram", "x")], writes=freg(st), tag=f"lxr{half}_{dt}_{tc}")
                    vb.op("dve", (lambda en, st=st, dt=dt, tc=tc, c0=c0, th0=th0:
                          en.scalar_tensor_tensor(
                              out=fap(st), in0=fap(acc, dt * 1024 + tc * 512, dt * 1024 + tc * 512 + 512),
                              scalar=0.5, in1=fap(st), op0=ALU.mult, op1=ALU.add)),
                          reads=freg(acc, dt * 1024 + tc * 512, dt * 1024 + tc * 512 + 512) + freg(st),
                          writes=freg(st), tag=f"xos{half}_{dt}_{tc}")
                    vb.dma("sp", (lambda en, st=st, dt=dt, c0=c0: en.dma_start(
                        out=xo_d[dt * P: dt * P + P, c0: c0 + 512], in_=fap(st))),
                        reads=freg(st), writes=[("dram", "xo")], tag=f"sxo{half}_{dt}_{tc}")

        stats = vb.finalize_and_emit(es, final_queue="pool")
    return nc, stats



def build_axpy_program():
    nc = bass.Bass("TRN2", target_bir_lowering=False, debug=False)
    x_d = nc.dram_tensor("x_fm", [D, T], F32, kind="ExternalInput").ap()
    y_d = nc.dram_tensor("y_fm", [D, T], F32, kind="ExternalInput").ap()
    xo_d = nc.dram_tensor("xo_fm", [D, T], F32, kind="ExternalOutput").ap()
    es = ExitStack()
    with es:
        A = Arena(nc, es, "arn", 16 * 1024, BF16)
        vb = VB(nc)

        def falloc(cols, tag):
            return A.alloc(2 * cols, tag)

        def fap(buf, c0=0, c1=None):
            c1 = buf.cols // 2 if c1 is None else c1
            return buf.arena[:, buf.off + 2 * c0: buf.off + 2 * c1].bitcast(F32)

        def freg(buf, c0=0, c1=None):
            c1 = buf.cols // 2 if c1 is None else c1
            return buf.reg(2 * c0, 2 * c1)

        xb = [falloc(512, f"x{s}") for s in range(3)]
        yb = [falloc(512, f"y{s}") for s in range(3)]
        for dt in range(DT):
            for tc in range(4):
                s = (dt * 4 + tc) % 3
                c0 = tc * 512
                vb.dma("sp", (lambda e, s=s, dt=dt, c0=c0: e.dma_start(
                    out=fap(xb[s]), in_=x_d[dt * P: dt * P + P, c0: c0 + 512])),
                    reads=[("dram", "x")], writes=freg(xb[s]), tag=f"lx{dt}_{tc}")
                vb.dma("pool", (lambda e, s=s, dt=dt, c0=c0: e.dma_start(
                    out=fap(yb[s]), in_=y_d[dt * P: dt * P + P, c0: c0 + 512])),
                    reads=[("dram", "y")], writes=freg(yb[s]), tag=f"ly{dt}_{tc}")
                vb.op("dve", (lambda e, s=s: e.scalar_tensor_tensor(
                    out=fap(xb[s]), in0=fap(yb[s]), scalar=0.5, in1=fap(xb[s]),
                    op0=ALU.mult, op1=ALU.add)),
                    reads=freg(xb[s]) + freg(yb[s]), writes=freg(xb[s]),
                    tag=f"ax{dt}_{tc}")
                vb.dma("sp", (lambda e, s=s, dt=dt, c0=c0: e.dma_start(
                    out=xo_d[dt * P: dt * P + P, c0: c0 + 512], in_=fap(xb[s]))),
                    reads=freg(xb[s]), writes=[("dram", "xo")], tag=f"sx{dt}_{tc}")
        stats = vb.finalize_and_emit(es, final_queue="pool")
    return nc, stats



def build_linattn_a_program():
    nc = bass.Bass("TRN2", target_bir_lowering=False, debug=False)
    x_d = nc.dram_tensor("x_fm", [D, T], F32, kind="ExternalInput").ap()
    wq_d = nc.dram_tensor("w_qkv", [D, 3 * D], F32, kind="ExternalInput").ap()
    ones_d = nc.dram_tensor("cst_ones", [P, P], F32, kind="ExternalInput").ap()
    eye_d = nc.dram_tensor("cst_eye", [P, P], F32, kind="ExternalInput").ap()
    qs_d = nc.dram_tensor("qs_tm", [T, D], F32, kind="ExternalOutput").ap()
    k_d = nc.dram_tensor("k_tm", [T, D], F32, kind="ExternalOutput").ap()
    v_d = nc.dram_tensor("v_tm", [T, D], F32, kind="ExternalOutput").ap()
    km_d = nc.dram_tensor("kmax_fm", [P, DT], F32, kind="ExternalOutput").ap()
    es = ExitStack()
    with es:
        A = Arena(nc, es, "arn", 100 * 1024, BF16)
        vb = VB(nc)

        def falloc(cols, tag):
            return A.alloc(2 * cols, tag)

        def fap(buf, c0=0, c1=None):
            c1 = buf.cols // 2 if c1 is None else c1
            return buf.arena[:, buf.off + 2 * c0: buf.off + 2 * c1].bitcast(F32)

        def fprt(buf, p0, p1, c0=0, c1=None):
            c1 = buf.cols // 2 if c1 is None else c1
            return buf.arena[p0:p1, buf.off + 2 * c0: buf.off + 2 * c1].bitcast(F32)

        def freg(buf, c0=0, c1=None):
            c1 = buf.cols // 2 if c1 is None else c1
            return buf.reg(2 * c0, 2 * c1)

        cstb = falloc(16, "consts")
        ones = falloc(P, "ones")
        idn = A.alloc(P, "idn")
        xlnb = A.alloc(DT * T, "xlnb")
        wqkv = A.alloc(DT * 3 * D, "wqkv")
        mx = falloc(D, "mx")
        tmb = falloc(512, "tm0")
        nrm = falloc(3 * 16, "nrm")
        cst_map = {}

        def cst(value):
            value = float(value)
            if value not in cst_map:
                i = len(cst_map)
                ap = fap(cstb, i, i + 1)
                rg = freg(cstb, i, i + 1)
                vb.op("dve", (lambda e, ap=ap, value=value: e.memset(ap, value)),
                      reads=[], writes=rg, tag=f"cst{value}")
                cst_map[value] = (ap, rg)
            return cst_map[value]

        pbf = [es.enter_context(nc.psum_tensor(f"pf{i}", [P, 512], F32)) for i in range(6)]
        PSF = [("psf", i) for i in range(6)]
        pbb = [es.enter_context(nc.psum_tensor(f"pb{i}", [P, 1024], BF16)) for i in range(2)]
        PSB = [("psb", i) for i in range(2)]
        rot = {"mm": 0, "tp": 0}

        def mmbank():
            i = rot["mm"] % 4
            rot["mm"] += 1
            return pbf[i], PSF[i]

        def tpbank():
            i = rot["tp"] % 2
            rot["tp"] += 1
            return pbb[i], PSB[i]

        vb.dma("sp", lambda e: e.dma_start(out=fap(ones), in_=ones_d),
               writes=freg(ones), tag="ld_ones")
        vb.dma("pool", lambda e: e.dma_start(out=idn.ap(), in_=eye_d),
               writes=idn.reg(), tag="ld_idn")
        for oc in range(6):
            vb.dma("pool", (lambda e, oc=oc: e.dma_start(
                out=wqkv.ap().rearrange("p (d o) -> p d o", o=3 * D)[:, :, 512 * oc: 512 * oc + 512],
                in_=wq_d[:, 512 * oc: 512 * oc + 512].rearrange("(dt p) o -> p dt o", p=P))),
                writes=wqkv.reg(), tag=f"ld_w{oc}")

        A.push()
        xres = falloc(DT * T, "xres")
        abp = falloc(T, "Ab")
        bbp = falloc(T, "Bb")
        rows = falloc(2 * T, "rows")
        sqs = falloc(512, "sqs")
        vb.dma("sp", (lambda e: e.dma_start(
            out=fap(xres).rearrange("p (dt c) -> p dt c", c=T),
            in_=x_d.rearrange("(dt p) c -> p dt c", p=P))),
            reads=[("dram", "x")], writes=freg(xres), tag="ld_x")

        class _BV:
            def __init__(self, b):
                self.b = b
                self.cols = b.cols // 2

            def ap(self, c0=0, c1=None):
                return fap(self.b, c0, c1)

            def prt(self, p0, p1, c0=0, c1=None):
                return fprt(self.b, p0, p1, c0, c1)

            def reg(self, c0=0, c1=None):
                return freg(self.b, c0, c1)

        class _CA:
            def col(self, v):
                return cst(v)

        emit_ln_fm(vb, pbf[:4], PSF[:4], _BV(xres), xlnb, _BV(rows), _BV(abp), _BV(bbp),
                   ones_col=fprt(ones, 0, P, 0, 1), ones_col_reg=freg(ones, 0, 1),
                   ones_row=fprt(ones, 0, 1, 0, P), ones_row_reg=freg(ones, 0, P),
                   TT=T, eps=1e-5, consts=_CA(), sq_tmp=_BV(sqs), prefix="aln")
        A.pop()

        vb.op("dve", (lambda e: e.memset(fap(mx), -3.0e38)),
              writes=freg(mx), tag="mxinit")

        for tt in range(16):
            for oc in range(6):
                bank, breg = mmbank()
                for dt in range(DT):
                    wc0 = dt * 3 * D + 512 * oc
                    vb.op("pe", (lambda e, bank=bank, tt=tt, dt=dt, wc0=wc0: e.matmul(
                        out=bank[:, 0:512],
                        lhsT=xlnb.ap(dt * T + tt * P, dt * T + tt * P + P),
                        rhs=wqkv.ap(wc0, wc0 + 512),
                        start=(dt == 0), stop=(dt == DT - 1))),
                        reads=xlnb.reg(dt * T + tt * P, dt * T + tt * P + P) + wqkv.reg(wc0, wc0 + 512),
                        writes=[breg], tag=f"qkv{tt}_{oc}")
                if oc < 2:
                    nA = fap(nrm, 0, 8)
                    nAr = freg(nrm, 0, 8)
                    vb.op("dve", (lambda e, bank=bank, nA=nA: e.reduce_max(
                        out=nA, in_=bank[:, 0:512].rearrange("p (h d) -> p h d", d=64),
                        axis=AX.X, negate=True)),
                        reads=[breg], writes=nAr, tag=f"qm{tt}_{oc}")
                    vb.op("dve", (lambda e, bank=bank, nA=nA: e.tensor_tensor(
                        out=fap(tmb), in0=bank[:, 0:512],
                        in1=nA.unsqueeze(2).broadcast_to([P, 8, 64]).rearrange("p h d -> p (h d)") if False else nA.unsqueeze(2).broadcast_to([P, 8, 64]),
                        op=ALU.add)) if False else
                        (lambda e, bank=bank, nA=nA: e.tensor_tensor(
                            out=fap(tmb).rearrange("p (h d) -> p h d", d=64),
                            in0=bank[:, 0:512].rearrange("p (h d) -> p h d", d=64),
                            in1=nA.unsqueeze(2).broadcast_to([P, 8, 64]),
                            op=ALU.add)),
                        reads=[breg] + nAr, writes=freg(tmb), tag=f"qs{tt}_{oc}")
                    vb.op("act", (lambda e: e.activation(
                        out=fap(tmb), in_=fap(tmb), func=AF.Exp)),
                        reads=freg(tmb), writes=freg(tmb), tag=f"qe{tt}_{oc}")
                    sA = fap(nrm, 16, 24)
                    sAr = freg(nrm, 16, 24)
                    vb.op("dve", (lambda e, sA=sA: e.reduce_sum(
                        out=sA, in_=fap(tmb).rearrange("p (h d) -> p h d", d=64),
                        axis=AX.X)),
                        reads=freg(tmb), writes=sAr, tag=f"qss{tt}_{oc}")
                    vb.op("dve", (lambda e, sA=sA: e.reciprocal(out=sA, in_=sA)),
                        reads=sAr, writes=sAr, tag=f"qrc{tt}_{oc}")
                    vb.op("dve", (lambda e, sA=sA: e.tensor_scalar(
                        out=sA, in0=sA, scalar1=0.125, scalar2=None, op0=ALU.mult)),
                        reads=sAr, writes=sAr, tag=f"qsc{tt}_{oc}")
                    vb.op("dve", (lambda e, sA=sA: e.tensor_tensor(
                        out=fap(tmb).rearrange("p (h d) -> p h d", d=64),
                        in0=fap(tmb).rearrange("p (h d) -> p h d", d=64),
                        in1=sA.unsqueeze(2).broadcast_to([P, 8, 64]),
                        op=ALU.mult)),
                        reads=freg(tmb) + sAr, writes=freg(tmb), tag=f"qn{tt}_{oc}")
                    vb.dma("sp", (lambda e, tt=tt, oc=oc: e.dma_start(
                        out=qs_d[tt * P: tt * P + P, 512 * oc: 512 * oc + 512],
                        in_=fap(tmb))),
                        reads=freg(tmb), writes=[("dram", "qs")], tag=f"qst{tt}_{oc}")
                elif oc < 4:
                    vb.op("dve", (lambda e, bank=bank: e.tensor_copy(
                        out=fap(tmb), in_=bank[:, 0:512])),
                        reads=[breg], writes=freg(tmb), tag=f"ke{tt}_{oc}")
                    mc0 = (oc - 2) * 512
                    vb.op("dve", (lambda e, mc0=mc0: e.tensor_tensor(
                        out=fap(mx, mc0, mc0 + 512), in0=fap(mx, mc0, mc0 + 512),
                        in1=fap(tmb), op=ALU.max)),
                        reads=freg(mx, mc0, mc0 + 512) + freg(tmb),
                        writes=freg(mx, mc0, mc0 + 512), tag=f"km{tt}_{oc}")
                    vb.dma("sp", (lambda e, tt=tt, oc=oc: e.dma_start(
                        out=k_d[tt * P: tt * P + P, 512 * (oc - 2): 512 * (oc - 2) + 512],
                        in_=fap(tmb))),
                        reads=freg(tmb), writes=[("dram", "k")], tag=f"kst{tt}_{oc}")
                else:
                    vb.op("dve", (lambda e, bank=bank: e.tensor_copy(
                        out=fap(tmb), in_=bank[:, 0:512])),
                        reads=[breg], writes=freg(tmb), tag=f"ve{tt}_{oc}")
                    vb.dma("sp", (lambda e, tt=tt, oc=oc: e.dma_start(
                        out=v_d[tt * P: tt * P + P, 512 * (oc - 4): 512 * (oc - 4) + 512],
                        in_=fap(tmb))),
                        reads=freg(tmb), writes=[("dram", "v")], tag=f"vst{tt}_{oc}")

        kmf = falloc(DT, "kmf")
        mxb = A.alloc(D, "mxb")
        vb.op("dve", (lambda e: e.tensor_copy(out=mxb.ap(), in_=fap(mx))),
              reads=freg(mx), writes=mxb.reg(), tag="mxc")
        for c in range(DT):
            bank, breg = tpbank()
            vb.op("pe", (lambda e, bank=bank, c=c: e.transpose(
                out=bank[:, 0:128], in_=mxb.ap(c * P, c * P + P), identity=idn.ap())),
                reads=mxb.reg(c * P, c * P + P) + idn.reg(), writes=[breg],
                tag=f"mt{c}")
            vb.op("dve", (lambda e, bank=bank, c=c: e.reduce_max(
                out=fap(kmf, c, c + 1), in_=bank[:, 0:128], axis=AX.X)),
                reads=[breg], writes=freg(kmf, c, c + 1), tag=f"mr{c}")
        vb.dma("sp", (lambda e: e.dma_start(out=km_d, in_=fap(kmf))),
               reads=freg(kmf), writes=[("dram", "km")], tag="kms")
        stats = vb.finalize_and_emit(es, final_queue="pool")
    return nc, stats



def build_linattn_b_program():
    nc = bass.Bass("TRN2", target_bir_lowering=False, debug=False)
    k_dd = nc.dram_tensor("k_tm", [T, D], F32, kind="ExternalInput").ap()
    v_dd = nc.dram_tensor("v_tm", [T, D], F32, kind="ExternalInput").ap()
    km_d = nc.dram_tensor("kmax2", [P, 2 * DT], F32, kind="ExternalInput").ap()
    ones_d = nc.dram_tensor("cst_ones", [P, P], F32, kind="ExternalInput").ap()
    cn_d = nc.dram_tensor("cnum", [D, 65], F32, kind="ExternalOutput").ap()
    mrow_d = nc.dram_tensor("mrow_scr", [D], F32, kind="Internal")
    es = ExitStack()
    with es:
        A = Arena(nc, es, "arn", 100 * 1024, BF16)
        vb = VB(nc)

        def falloc(cols, tag):
            return A.alloc(2 * cols, tag)

        def fap(buf, c0=0, c1=None):
            c1 = buf.cols // 2 if c1 is None else c1
            return buf.arena[:, buf.off + 2 * c0: buf.off + 2 * c1].bitcast(F32)

        def fprt(buf, p0, p1, c0=0, c1=None):
            c1 = buf.cols // 2 if c1 is None else c1
            return buf.arena[p0:p1, buf.off + 2 * c0: buf.off + 2 * c1].bitcast(F32)

        def freg(buf, c0=0, c1=None):
            c1 = buf.cols // 2 if c1 is None else c1
            return buf.reg(2 * c0, 2 * c1)

        ones = falloc(P, "ones")
        kmb = falloc(2 * DT, "kmb")
        mfm = falloc(DT, "mfm")
        mbc = falloc(D, "mbc")
        expk = A.alloc(16 * D, "expk")
        vaug = A.alloc(16 * 16 * 65, "vaug")
        ktb = [falloc(D, f"kt{s}") for s in range(2)]
        vtb = [A.alloc(D, f"vt{s}") for s in range(2)]
        stg = falloc(P, "stg")

        pbf = [es.enter_context(nc.psum_tensor(f"pf{i}", [P, 512], F32)) for i in range(8)]
        PSF = [("psf", i) for i in range(8)]

        vb.dma("sp", lambda e: e.dma_start(out=fap(ones), in_=ones_d),
               writes=freg(ones), tag="ld_ones")
        vb.dma("sp", lambda e: e.dma_start(out=fap(kmb), in_=km_d),
               writes=freg(kmb), tag="ld_km")
        vb.op("dve", (lambda e: e.tensor_tensor(
            out=fap(mfm), in0=fap(kmb, 0, DT), in1=fap(kmb, DT, 2 * DT), op=ALU.max)),
            reads=freg(kmb), writes=freg(mfm), tag="mglob")
        vb.dma("sp", (lambda e: e.dma_start(
            out=mrow_d.ap().rearrange("(c p) -> p c", p=P), in_=fap(mfm),
            allow_slow_non_contiguous=True)),
            reads=freg(mfm), writes=[("dram", "mrow")], tag="mb_out")
        mr = falloc(D, "mr")
        vb.dma("sp", (lambda e: e.dma_start(
            out=fprt(mr, 0, 1), in_=mrow_d.ap().unsqueeze(0))),
            reads=[("dram", "mrow")], writes=freg(mr), tag="mb_in")
        for c in range(2):
            bank, breg = pbf[6], PSF[6]
            vb.op("pe", (lambda e, bank=bank, c=c: e.matmul(
                out=bank[:, 0:512], lhsT=fprt(ones, 0, 1),
                rhs=fprt(mr, 0, 1, c * 512, c * 512 + 512), start=True, stop=True)),
                reads=freg(mr) + freg(ones), writes=[breg], tag=f"mbc{c}")
            vb.op("dve", (lambda e, bank=bank, c=c: e.tensor_copy(
                out=fap(mbc, c * 512, c * 512 + 512), in_=bank[:, 0:512])),
                reads=[breg], writes=freg(mbc, c * 512, c * 512 + 512), tag=f"mbce{c}")

        for tt in range(16):
            s = tt % 2
            vb.dma("sp", (lambda e, s=s, tt=tt: e.dma_start(
                out=fap(ktb[s]), in_=k_dd[tt * P: tt * P + P, :])),
                reads=[("dram", "k")], writes=freg(ktb[s]), tag=f"lk{tt}")
            vb.dma("pool", (lambda e, s=s, tt=tt: e.dma_start(
                out=vtb[s].ap(), in_=v_dd[tt * P: tt * P + P, :])),
                reads=[("dram", "v")], writes=vtb[s].reg(), tag=f"lv{tt}")
            vb.op("dve", (lambda e, s=s: e.tensor_tensor(
                out=fap(ktb[s]), in0=fap(ktb[s]), in1=fap(mbc), op=ALU.subtract)),
                reads=freg(ktb[s]) + freg(mbc), writes=freg(ktb[s]), tag=f"ks{tt}")
            vb.op("act", (lambda e, s=s, tt=tt: e.activation(
                out=expk.ap(tt * D, (tt + 1) * D), in_=fap(ktb[s]), func=AF.Exp)),
                reads=freg(ktb[s]), writes=expk.reg(tt * D, (tt + 1) * D), tag=f"ke{tt}")
            vc0 = tt * 16 * 65
            vb.op("dve", (lambda e, s=s, vc0=vc0: e.tensor_copy(
                out=vaug.ap(vc0, vc0 + 16 * 65).rearrange("p (h d) -> p h d", d=65)[:, :, 0:64],
                in_=vtb[s].ap().rearrange("p (h d) -> p h d", d=64))),
                reads=vtb[s].reg(), writes=vaug.reg(vc0, vc0 + 16 * 65), tag=f"vc{tt}")
            vb.op("dve", (lambda e, vc0=vc0: e.memset(
                vaug.ap(vc0, vc0 + 16 * 65).rearrange("p (h d) -> p h d", d=65)[:, :, 64:65], 1.0)),
                reads=[], writes=vaug.reg(vc0, vc0 + 16 * 65), tag=f"vo{tt}")

        for h in range(16):
            bank, breg = pbf[h % 6], PSF[h % 6]
            for tt in range(16):
                vb.op("pe", (lambda e, bank=bank, h=h, tt=tt: e.matmul(
                    out=bank[0:64, 0:65],
                    lhsT=expk.ap(tt * D + h * 64, tt * D + h * 64 + 64),
                    rhs=vaug.ap(tt * 16 * 65 + h * 65, tt * 16 * 65 + h * 65 + 65),
                    start=(tt == 0), stop=(tt == 15))),
                    reads=expk.reg(tt * D + h * 64, tt * D + h * 64 + 64) + vaug.reg(tt * 16 * 65 + h * 65, tt * 16 * 65 + h * 65 + 65),
                    writes=[breg], tag=f"cm{h}_{tt}")
            vb.op("dve", (lambda e, bank=bank, h=h: e.tensor_copy(
                out=fap(stg, 0, 65)[0:64, :] if False else fprt(stg, 0, 64, 0, 65),
                in_=bank[0:64, 0:65])),
                reads=[breg], writes=freg(stg), tag=f"ce{h}")
            vb.dma("sp", (lambda e, h=h: e.dma_start(
                out=cn_d[h * 64: h * 64 + 64, :], in_=fprt(stg, 0, 64, 0, 65))),
                reads=freg(stg), writes=[("dram", "cn")], tag=f"cs{h}")
        stats = vb.finalize_and_emit(es, final_queue="pool")
    return nc, stats



def build_linattn_c_program():
    nc = bass.Bass("TRN2", target_bir_lowering=False, debug=False)
    qs_dd = nc.dram_tensor("qs_fm", [D, T], F32, kind="ExternalInput").ap()
    cn_d = nc.dram_tensor("cnum2", [2, D, 65], F32, kind="ExternalInput").ap()
    wo_d = nc.dram_tensor("w_out", [D, D], F32, kind="ExternalInput").ap()
    x_d = nc.dram_tensor("x_fm", [D, T], F32, kind="ExternalInput").ap()
    xo_d = nc.dram_tensor("xo_fm", [D, T], F32, kind="ExternalOutput").ap()
    es = ExitStack()
    with es:
        A = Arena(nc, es, "arn", 100 * 1024, BF16)
        vb = VB(nc)

        def falloc(cols, tag):
            return A.alloc(2 * cols, tag)

        def fap(buf, c0=0, c1=None):
            c1 = buf.cols // 2 if c1 is None else c1
            return buf.arena[:, buf.off + 2 * c0: buf.off + 2 * c1].bitcast(F32)

        def fprt(buf, p0, p1, c0=0, c1=None):
            c1 = buf.cols // 2 if c1 is None else c1
            return buf.arena[p0:p1, buf.off + 2 * c0: buf.off + 2 * c1].bitcast(F32)

        def freg(buf, c0=0, c1=None):
            c1 = buf.cols // 2 if c1 is None else c1
            return buf.reg(2 * c0, 2 * c1)

        ca = falloc(DT * 65, "ca")
        cb = falloc(DT * 65, "cb")
        rcp = falloc(DT, "rcp")
        cbf = A.alloc(DT * 64, "cbf")
        qsb = A.alloc(DT * T, "qsb")
        wob = A.alloc(DT * D, "wob")
        aof = A.alloc(DT * 512, "aof")
        xsb = [falloc(512, f"xs{s}") for s in range(2)]

        pbf = [es.enter_context(nc.psum_tensor(f"pf{i}", [P, 512], F32)) for i in range(8)]
        PSF = [("psf", i) for i in range(8)]
        rot = {"mm": 0}

        def mmbank(lo=0, n=6):
            i = lo + rot["mm"] % n
            rot["mm"] += 1
            return pbf[i], PSF[i]

        vb.dma("sp", (lambda e: e.dma_start(
            out=fap(ca).rearrange("p (c u) -> p c u", u=65),
            in_=cn_d[0].rearrange("(c p) u -> p c u", p=P))),
            reads=[("dram", "cn")], writes=freg(ca), tag="ld_ca")
        vb.dma("sp", (lambda e: e.dma_start(
            out=fap(cb).rearrange("p (c u) -> p c u", u=65),
            in_=cn_d[1].rearrange("(c p) u -> p c u", p=P))),
            reads=[("dram", "cn")], writes=freg(cb), tag="ld_cb")
        vb.dma("pool", (lambda e: e.dma_start(
            out=qsb.ap().rearrange("p (dt c) -> p dt c", c=T),
            in_=qs_dd.rearrange("(dt p) c -> p dt c", p=P))),
            reads=[("dram", "qs")], writes=qsb.reg(), tag="ld_qs")
        vb.dma("pool", (lambda e: e.dma_start(
            out=wob.ap().rearrange("p (kt o) -> p kt o", kt=DT),
            in_=wo_d.rearrange("(kt p) o -> p kt o", p=P))),
            reads=[("dram", "wo")], writes=wob.reg(), tag="ld_wo")
        vb.op("dve", (lambda e: e.tensor_tensor(
            out=fap(ca), in0=fap(ca), in1=fap(cb), op=ALU.add)),
            reads=freg(ca) + freg(cb), writes=freg(ca), tag="cadd")
        vb.op("dve", (lambda e: e.reciprocal(
            out=fap(rcp).rearrange("p c -> p c 1", c=DT) if False else fap(rcp),
            in_=fap(ca).rearrange("p (c u) -> p c u", u=65)[:, :, 64:65].rearrange("p c u -> p (c u)"))),
            reads=freg(ca), writes=freg(rcp), tag="crcp")
        vb.op("dve", (lambda e: e.tensor_tensor(
            out=cbf.ap().rearrange("p (c u) -> p c u", u=64),
            in0=fap(ca).rearrange("p (c u) -> p c u", u=65)[:, :, 0:64],
            in1=fap(rcp).rearrange("p (c u) -> p c u", u=1).broadcast_to([P, DT, 64]),
            op=ALU.mult)),
            reads=freg(ca) + freg(rcp), writes=cbf.reg(), tag="cdiv")

        for tc in range(4):
            c0 = tc * 512
            for h in range(16):
                bank, breg = mmbank()
                ch = h // 2
                r0 = (h % 2) * 64
                vb.op("pe", (lambda e, bank=bank, ch=ch, r0=r0, h=h, c0=c0: e.matmul(
                    out=bank[0:64, 0:512],
                    lhsT=cbf.prt(r0, r0 + 64, ch * P + (h % 2) * 64 if False else 0, 0) if False else
                         cbf.prt(r0, r0 + 64, (h // 2) * 128 + (h % 2) * 64, (h // 2) * 128 + (h % 2) * 64 + 64),
                    rhs=qsb.prt(r0, r0 + 64, (h // 2) * T + c0, (h // 2) * T + c0 + 512),
                    start=True, stop=True)),
                    reads=cbf.reg() + qsb.reg((h // 2) * T + c0, (h // 2) * T + c0 + 512),
                    writes=[breg], tag=f"ao{tc}_{h}")
                dst0 = (h // 2) * 512
                vb.op("dve", (lambda e, bank=bank, h=h, dst0=dst0: e.tensor_copy(
                    out=aof.prt((h % 2) * 64, (h % 2) * 64 + 64, dst0, dst0 + 512),
                    in_=bank[0:64, 0:512])),
                    reads=[breg], writes=aof.reg(dst0, dst0 + 512), tag=f"ae{tc}_{h}")
            for ot in range(DT):
                bank, breg = mmbank(6, 2)
                for kt in range(DT):
                    vb.op("pe", (lambda e, bank=bank, ot=ot, kt=kt: e.matmul(
                        out=bank[:, 0:512],
                        lhsT=wob.ap(kt * D + ot * P, kt * D + ot * P + P),
                        rhs=aof.ap(kt * 512, (kt + 1) * 512),
                        start=(kt == 0), stop=(kt == DT - 1))),
                        reads=wob.reg(kt * D + ot * P, kt * D + ot * P + P) + aof.reg(kt * 512, (kt + 1) * 512),
                        writes=[breg], tag=f"wo{tc}_{ot}_{kt}")
                s = ot % 2
                vb.dma("sp", (lambda e, s=s, ot=ot, c0=c0: e.dma_start(
                    out=fap(xsb[s]), in_=x_d[ot * P: ot * P + P, c0: c0 + 512])),
                    reads=[("dram", "x")], writes=freg(xsb[s]), tag=f"lx{tc}_{ot}")
                vb.op("dve", (lambda e, bank=bank, s=s: e.scalar_tensor_tensor(
                    out=fap(xsb[s]), in0=bank[:, 0:512], scalar=1.0, in1=fap(xsb[s]),
                    op0=ALU.mult, op1=ALU.add)),
                    reads=[breg] + freg(xsb[s]), writes=freg(xsb[s]), tag=f"xa{tc}_{ot}")
                vb.dma("sp", (lambda e, s=s, ot=ot, c0=c0: e.dma_start(
                    out=xo_d[ot * P: ot * P + P, c0: c0 + 512], in_=fap(xsb[s]))),
                    reads=freg(xsb[s]), writes=[("dram", "xo")], tag=f"sx{tc}_{ot}")
        stats = vb.finalize_and_emit(es, final_queue="pool")
    return nc, stats



def build_l1qkv_program():
    nc = bass.Bass("TRN2", target_bir_lowering=False, debug=False)
    x_d = nc.dram_tensor("x_fm", [D, T], F32, kind="ExternalInput").ap()
    wq_d = nc.dram_tensor("wqT", [D, D], F32, kind="ExternalInput").ap()
    wk_d = nc.dram_tensor("wkT", [D, D], F32, kind="ExternalInput").ap()
    wv_d = nc.dram_tensor("wvT", [D, D], F32, kind="ExternalInput").ap()
    bq_d = nc.dram_tensor("bq", [P, DT], F32, kind="ExternalInput").ap()
    bk_d = nc.dram_tensor("bk", [P, DT], F32, kind="ExternalInput").ap()
    bv_d = nc.dram_tensor("bv", [P, DT], F32, kind="ExternalInput").ap()
    ones_d = nc.dram_tensor("cst_ones", [P, P], F32, kind="ExternalInput").ap()
    q_o = nc.dram_tensor("q_fm", [D, T], F32, kind="ExternalOutput").ap()
    k_o = nc.dram_tensor("k_fm", [D, T], F32, kind="ExternalOutput").ap()
    v_o = nc.dram_tensor("v_fm", [D, T], F32, kind="ExternalOutput").ap()
    es = ExitStack()
    with es:
        A = Arena(nc, es, "arn", 100 * 1024, BF16)
        vb = VB(nc)

        def falloc(cols, tag):
            return A.alloc(2 * cols, tag)

        def fap(buf, c0=0, c1=None):
            c1 = buf.cols // 2 if c1 is None else c1
            return buf.arena[:, buf.off + 2 * c0: buf.off + 2 * c1].bitcast(F32)

        def fprt(buf, p0, p1, c0=0, c1=None):
            c1 = buf.cols // 2 if c1 is None else c1
            return buf.arena[p0:p1, buf.off + 2 * c0: buf.off + 2 * c1].bitcast(F32)

        def freg(buf, c0=0, c1=None):
            c1 = buf.cols // 2 if c1 is None else c1
            return buf.reg(2 * c0, 2 * c1)

        cstb = falloc(16, "consts")
        ones = falloc(P, "ones")
        xlnb = A.alloc(DT * T, "xlnb")
        wres = [A.alloc(DT * D, f"w{i}") for i in range(3)]
        bcols = [falloc(DT, f"b{i}") for i in range(3)]
        stg = [falloc(512, f"stg{s}") for s in range(2)]
        cst_map = {}

        def cst(value):
            value = float(value)
            if value not in cst_map:
                i = len(cst_map)
                ap = fap(cstb, i, i + 1)
                rg = freg(cstb, i, i + 1)
                vb.op("dve", (lambda e, ap=ap, value=value: e.memset(ap, value)),
                      reads=[], writes=rg, tag=f"cst{value}")
                cst_map[value] = (ap, rg)
            return cst_map[value]

        pbf = [es.enter_context(nc.psum_tensor(f"pf{i}", [P, 512], F32)) for i in range(6)]
        PSF = [("psf", i) for i in range(6)]
        rot = {"mm": 0}

        def mmbank():
            i = rot["mm"] % 4
            rot["mm"] += 1
            return pbf[i], PSF[i]

        vb.dma("sp", lambda e: e.dma_start(out=fap(ones), in_=ones_d),
               writes=freg(ones), tag="ld_ones")
        for i, (wd, bd) in enumerate(((wq_d, bq_d), (wk_d, bk_d), (wv_d, bv_d))):
            vb.dma("pool", (lambda e, i=i, wd=wd: e.dma_start(
                out=wres[i].ap().rearrange("p (dt o) -> p dt o", o=D),
                in_=wd.rearrange("(dt p) o -> p dt o", p=P))),
                writes=wres[i].reg(), tag=f"ld_w{i}")
            vb.dma("sp", (lambda e, i=i, bd=bd: e.dma_start(out=fap(bcols[i]), in_=bd)),
                   writes=freg(bcols[i]), tag=f"ld_b{i}")

        A.push()
        xres = falloc(DT * T, "xres")
        abp = falloc(T, "Ab")
        bbp = falloc(T, "Bb")
        rows = falloc(2 * T, "rows")
        sqs = falloc(512, "sqs")
        vb.dma("sp", (lambda e: e.dma_start(
            out=fap(xres).rearrange("p (dt c) -> p dt c", c=T),
            in_=x_d.rearrange("(dt p) c -> p dt c", p=P))),
            reads=[("dram", "x")], writes=freg(xres), tag="ld_x")

        class _BV:
            def __init__(self, b):
                self.b = b
                self.cols = b.cols // 2

            def ap(self, c0=0, c1=None):
                return fap(self.b, c0, c1)

            def prt(self, p0, p1, c0=0, c1=None):
                return fprt(self.b, p0, p1, c0, c1)

            def reg(self, c0=0, c1=None):
                return freg(self.b, c0, c1)

        class _CA:
            def col(self, v):
                return cst(v)

        emit_ln_fm(vb, pbf[:4], PSF[:4], _BV(xres), xlnb, _BV(rows), _BV(abp), _BV(bbp),
                   ones_col=fprt(ones, 0, P, 0, 1), ones_col_reg=freg(ones, 0, 1),
                   ones_row=fprt(ones, 0, 1, 0, P), ones_row_reg=freg(ones, 0, P),
                   TT=T, eps=1e-5, consts=_CA(), sq_tmp=_BV(sqs), prefix="qln")
        A.pop()

        outs = [q_o, k_o, v_o]
        for i in range(3):
            for ot in range(DT):
                for tc in range(4):
                    c0 = tc * 512
                    bank, breg = mmbank()
                    for dt in range(DT):
                        wc0 = dt * D + ot * P
                        vb.op("pe", (lambda e, bank=bank, i=i, wc0=wc0, dt=dt, c0=c0:
                              e.matmul(out=bank[:, 0:512],
                                       lhsT=wres[i].ap(wc0, wc0 + P),
                                       rhs=xlnb.ap(dt * T + c0, dt * T + c0 + 512),
                                       start=(dt == 0), stop=(dt == DT - 1))),
                              reads=wres[i].reg(wc0, wc0 + P) + xlnb.reg(dt * T + c0, dt * T + c0 + 512),
                              writes=[breg], tag=f"qk{i}_{ot}_{tc}")
                    s = tc % 2
                    vb.op("act", (lambda e, bank=bank, i=i, ot=ot, s=s: e.activation(
                        out=fap(stg[s]), in_=bank[:, 0:512], func=AF.Identity,
                        bias=fap(bcols[i], ot, ot + 1))),
                        reads=[breg] + freg(bcols[i]), writes=freg(stg[s]),
                        tag=f"qe{i}_{ot}_{tc}")
                    vb.dma("sp", (lambda e, i=i, ot=ot, c0=c0, s=s: e.dma_start(
                        out=outs[i][ot * P: ot * P + P, c0: c0 + 512], in_=fap(stg[s]))),
                        reads=freg(stg[s]), writes=[("dram", f"o{i}")], tag=f"qs{i}_{ot}_{tc}")
        stats = vb.finalize_and_emit(es, final_queue="pool")
    return nc, stats



L1W = 3584
L1CB = 640


def host_l1_selAB(coreh):
    selA = np.zeros((1, 128), np.float32)
    selB = np.zeros((1, 128), np.float32)
    for it in range(16):
        for kb in range(8):
            u = 4 * kb - it + 15
            if u in range(11, 17) or u in range(27, 33):
                continue
            d0 = 512 * kb - 128 * it - 2048 * coreh
            if d0 + 511 <= -15:
                selA[0, it * 8 + kb] = 1.0
            elif d0 - 127 >= 15:
                selB[0, it * 8 + kb] = 1.0
            else:
                raise AssertionError("non-saturated off-band block")
    return selA, selB


def host_l1_shift(coreh):
    return np.full((1, 1), -float(L1CB + 2048 * coreh), np.float32)


def host_flip_eye():
    return np.eye(P, dtype=np.float32)[::-1].copy()


def build_l1core_program():
    nc = bass.Bass("TRN2", target_bir_lowering=False, debug=False)
    q_d = nc.dram_tensor("q_fm", [D, T], F32, kind="ExternalInput").ap()
    k_d = nc.dram_tensor("k_fm", [D, 2 * T], F32, kind="ExternalInput").ap()
    v_d = nc.dram_tensor("v_tm", [2 * T, D], F32, kind="ExternalInput").ap()
    x_d = nc.dram_tensor("x_fm", [D, T], F32, kind="ExternalInput").ap()
    wo_d = nc.dram_tensor("woT", [D, D], F32, kind="ExternalInput").ap()
    bo_d = nc.dram_tensor("bo", [P, DT], F32, kind="ExternalInput").ap()
    rb_d = nc.dram_tensor("relb", [31, 16], F32, kind="ExternalInput").ap()
    sa_d = nc.dram_tensor("selA", [1, 128], F32, kind="ExternalInput").ap()
    sb_d = nc.dram_tensor("selB", [1, 128], F32, kind="ExternalInput").ap()
    sh_d = nc.dram_tensor("shiftv", [1, 1], F32, kind="ExternalInput").ap()
    ones_d = nc.dram_tensor("cst_ones", [P, P], F32, kind="ExternalInput").ap()
    eye_d = nc.dram_tensor("cst_eye", [P, P], F32, kind="ExternalInput").ap()
    fey_d = nc.dram_tensor("cst_fey", [P, P], F32, kind="ExternalInput").ap()
    xo_d = nc.dram_tensor("xo_fm", [D, T], F32, kind="ExternalOutput").ap()
    mst_d = nc.dram_tensor("master_scr", [16, L1W], F32, kind="Internal")

    es = ExitStack()
    with es:
        A = Arena(nc, es, "arn", 103 * 1024, BF16)
        vb = VB(nc)

        def falloc(cols, tag):
            return A.alloc(2 * cols, tag)

        def fap(buf, c0=0, c1=None):
            c1 = buf.cols // 2 if c1 is None else c1
            return buf.arena[:, buf.off + 2 * c0: buf.off + 2 * c1].bitcast(F32)

        def fprt(buf, p0, p1, c0=0, c1=None):
            c1 = buf.cols // 2 if c1 is None else c1
            return buf.arena[p0:p1, buf.off + 2 * c0: buf.off + 2 * c1].bitcast(F32)

        def freg(buf, c0=0, c1=None):
            c1 = buf.cols // 2 if c1 is None else c1
            return buf.reg(2 * c0, 2 * c1)

        cstb = falloc(16, "consts")
        ones = falloc(P, "ones")
        fey = A.alloc(P, "fey")
        idn_b = A.alloc(P, "idnb")
        kres = A.alloc(DT * 2 * T, "kres")
        vres = A.alloc(32 * D, "vres")
        wob = A.alloc(DT * D, "wob")
        boc = falloc(DT, "boc")
        relb = falloc(16, "relbv")
        rcol = falloc(1, "rcol")
        shc = falloc(1, "shc")
        sel = falloc(2 * P, "sel")
        CT = A.alloc(128 * 16, "CT")
        c0p = falloc(16, "c0p")
        c30p = falloc(16, "c30p")
        qtb = [A.alloc(DT * P, f"qt{s}") for s in range(2)]
        bnd = A.alloc(2 * 512, "bnd")
        prbss = [A.alloc(8 * 512, f"prbs{s}") for s in range(2)]
        ptbs = [A.alloc(4 * 1024, f"ptb{s}") for s in range(2)]
        nmcs = [falloc(3 * 16, f"nmc{s}") for s in range(2)]
        dnms = [falloc(16, f"dnm{s}") for s in range(2)]
        aot = A.alloc(D, "aot")
        aof = A.alloc(DT * P, "aof")
        stg = [falloc(512, f"stg{s}") for s in range(2)]
        itr = falloc(512, "itr")
        msk = falloc(512, "msk2")

        cst_map = {}

        def cst(value):
            value = float(value)
            if value not in cst_map:
                i = len(cst_map)
                ap = fap(cstb, i, i + 1)
                rg = freg(cstb, i, i + 1)
                vb.op("dve", (lambda e, ap=ap, value=value: e.memset(ap, value)),
                      reads=[], writes=rg, tag=f"cst{value}")
                cst_map[value] = (ap, rg)
            return cst_map[value]

        pbf = [es.enter_context(nc.psum_tensor(f"pf{i}", [P, 512], F32)) for i in range(6)]
        PSF = [("psf", i) for i in range(6)]
        pbb = [es.enter_context(nc.psum_tensor(f"pb{i}", [P, 1024], BF16)) for i in range(2)]
        PSB = [("psb", i) for i in range(2)]
        rot = {"mm": 0, "sim": 0, "tp": 0, "av": 0}

        def simbank():
            i = rot["sim"] % 2
            rot["sim"] += 1
            return pbf[i], PSF[i]

        def avbank():
            i = 2 + rot["av"] % 2
            rot["av"] += 1
            return pbf[i], PSF[i]

        def mmbank():
            i = 4 + rot["mm"] % 2
            rot["mm"] += 1
            return pbf[i], PSF[i]

        def tpbank():
            i = rot["tp"] % 2
            rot["tp"] += 1
            return pbb[i], PSB[i]

        vb.dma("sp", lambda e: e.dma_start(out=fap(ones), in_=ones_d),
               writes=freg(ones), tag="ld_ones")
        vb.dma("pool", lambda e: e.dma_start(out=fey.ap(), in_=fey_d),
               writes=fey.reg(), tag="ld_fey")
        vb.dma("pool", lambda e: e.dma_start(out=idn_b.ap(), in_=eye_d),
               writes=idn_b.reg(), tag="ld_idnb")
        vb.dma("pool", (lambda e: e.dma_start(
            out=kres.ap().rearrange("p (dt c) -> p dt c", c=2 * T),
            in_=k_d.rearrange("(dt p) c -> p dt c", p=P))),
            writes=kres.reg(), tag="ld_k")
        vb.dma("pool", (lambda e: e.dma_start(
            out=vres.ap().rearrange("p (tt c) -> p tt c", c=D),
            in_=v_d.rearrange("(tt p) c -> p tt c", p=P))),
            writes=vres.reg(), tag="ld_v")
        vb.dma("pool", (lambda e: e.dma_start(
            out=wob.ap().rearrange("p (kt o) -> p kt o", kt=DT),
            in_=wo_d.rearrange("(kt p) o -> p kt o", p=P))),
            writes=wob.reg(), tag="ld_wo")
        vb.dma("sp", lambda e: e.dma_start(out=fap(boc), in_=bo_d),
               writes=freg(boc), tag="ld_bo")
        vb.dma("sp", (lambda e: e.dma_start(out=fprt(relb, 0, 31), in_=rb_d)),
               writes=freg(relb), tag="ld_rb")
        relb0 = falloc(16, "relb0")
        relb30 = falloc(16, "relb30")
        vb.dma("sp", (lambda e: e.dma_start(out=fprt(relb0, 0, 1), in_=rb_d[0:1, :])),
               writes=freg(relb0), tag="ld_rb0")
        vb.dma("sp", (lambda e: e.dma_start(out=fprt(relb30, 0, 1), in_=rb_d[30:31, :])),
               writes=freg(relb30), tag="ld_rb30")
        vb.dma("sp", (lambda e: e.dma_start(out=fprt(sel, 0, 1, 0, P), in_=sa_d)),
               writes=freg(sel), tag="ld_sa")
        vb.dma("sp", (lambda e: e.dma_start(out=fprt(sel, 0, 1, P, 2 * P), in_=sb_d)),
               writes=freg(sel), tag="ld_sb")
        vb.dma("sp", (lambda e: e.dma_start(out=fprt(shc, 0, 1), in_=sh_d)),
               writes=freg(shc), tag="ld_sh")

        vb.op("pool", (lambda e: e.iota(out=fap(rcol).bitcast(F32)[0:31, :] if False else fprt(rcol, 0, 31),
                                        pattern=[[0, 1]], base=-15, channel_multiplier=1,
                                        allow_small_or_imprecise_dtypes=True)),
              writes=freg(rcol), tag="rcol")
        shb = falloc(1, "shb")
        bank, breg = mmbank()
        vb.op("pe", (lambda e, bank=bank: e.matmul(
            out=bank[0:31, 0:1], lhsT=fprt(ones, 0, 1, 0, 31), rhs=fprt(shc, 0, 1),
            start=True, stop=True)),
            reads=freg(shc) + freg(ones), writes=[breg], tag="shbm")
        vb.op("dve", (lambda e, bank=bank: e.tensor_copy(
            out=fprt(shb, 0, 31), in_=bank[0:31, 0:1])),
            reads=[breg], writes=freg(shb), tag="shbe")

        for ch in range(L1W // 512):
            vb.op("pool", (lambda e, ch=ch: e.iota(
                out=fprt(itr, 0, 31), pattern=[[1, 512]], base=ch * 512,
                channel_multiplier=0, allow_small_or_imprecise_dtypes=True)),
                writes=freg(itr), tag=f"mi{ch}")
            vb.op("dve", (lambda e: e.tensor_scalar(
                out=fprt(itr, 0, 31), in0=fprt(itr, 0, 31), scalar1=fprt(shb, 0, 31),
                scalar2=15.0, op0=ALU.add, op1=ALU.min)),
                reads=freg(itr) + freg(shb), writes=freg(itr), tag=f"ms{ch}")
            vb.op("dve", (lambda e: e.tensor_scalar(
                out=fprt(itr, 0, 31), in0=fprt(itr, 0, 31), scalar1=-15.0,
                scalar2=None, op0=ALU.max)),
                reads=freg(itr), writes=freg(itr), tag=f"mc{ch}")
            vb.op("dve", (lambda e: e.tensor_scalar(
                out=fprt(msk, 0, 31), in0=fprt(itr, 0, 31), scalar1=fprt(rcol, 0, 31),
                scalar2=None, op0=ALU.is_equal)),
                reads=freg(itr) + freg(rcol), writes=freg(msk), tag=f"me{ch}")
            bank2, breg2 = mmbank()
            vb.op("pe", (lambda e, bank2=bank2: e.matmul(
                out=bank2[0:16, 0:512], lhsT=fprt(relb, 0, 31), rhs=fprt(msk, 0, 31),
                start=True, stop=True)),
                reads=freg(msk) + freg(relb), writes=[breg2], tag=f"mm{ch}")
            vb.op("dve", (lambda e, bank2=bank2: e.tensor_scalar(
                out=fprt(itr, 0, 16), in0=bank2[0:16, 0:512], scalar1=8.0,
                scalar2=None, op0=ALU.mult)),
                reads=[breg2], writes=freg(itr), tag=f"mev{ch}")
            vb.dma("sp", (lambda e, ch=ch: e.dma_start(
                out=mst_d.ap()[:, ch * 512: ch * 512 + 512], in_=fprt(itr, 0, 16))),
                reads=freg(itr), writes=[("dram", "mst")], tag=f"mso{ch}")

        for nm_, srcb, dstb in (("c0", relb0, c0p), ("c30", relb30, c30p)):
            bank3, breg3 = mmbank()
            vb.op("pe", (lambda e, bank3=bank3, srcb=srcb: e.matmul(
                out=bank3[:, 0:16], lhsT=fprt(ones, 0, 1), rhs=fprt(srcb, 0, 1),
                start=True, stop=True)),
                reads=freg(srcb) + freg(ones), writes=[breg3], tag=f"{nm_}m")
            vb.op("dve", (lambda e, bank3=bank3, dstb=dstb: e.tensor_copy(
                out=fap(dstb), in_=bank3[:, 0:16])),
                reads=[breg3], writes=freg(dstb), tag=f"{nm_}e")
        selp = falloc(2 * P, "selp")
        for i in range(2):
            bank4, breg4 = mmbank()
            vb.op("pe", (lambda e, bank4=bank4, i=i: e.matmul(
                out=bank4[:, 0:128], lhsT=fprt(ones, 0, 1),
                rhs=fprt(sel, 0, 1, i * P, (i + 1) * P), start=True, stop=True)),
                reads=freg(sel) + freg(ones), writes=[breg4], tag=f"selm{i}")
            vb.op("dve", (lambda e, bank4=bank4, i=i: e.tensor_copy(
                out=fap(selp, i * P, (i + 1) * P), in_=bank4[:, 0:128])),
                reads=[breg4], writes=freg(selp), tag=f"sele{i}")
        for ik in range(128):
            vb.op("dve", (lambda e, ik=ik: e.tensor_scalar(
                out=fap(cstb, 0, 0) if False else CT.ap(ik * 16, ik * 16 + 16),
                in0=fap(c30p), scalar1=fap(selp, P + ik, P + ik + 1),
                scalar2=None, op0=ALU.mult)),
                reads=freg(c30p) + freg(selp), writes=CT.reg(ik * 16, ik * 16 + 16),
                tag=f"ct1_{ik}")
            vb.op("dve", (lambda e, ik=ik: e.scalar_tensor_tensor(
                out=CT.ap(ik * 16, ik * 16 + 16), in0=fap(c0p),
                scalar=fap(selp, ik, ik + 1), in1=CT.ap(ik * 16, ik * 16 + 16),
                op0=ALU.mult, op1=ALU.add)),
                reads=freg(c0p) + freg(selp) + CT.reg(ik * 16, ik * 16 + 16),
                writes=CT.reg(ik * 16, ik * 16 + 16), tag=f"ct2_{ik}")

        eighth, eighth_rg = cst(0.125)

        BAND_U = set(range(11, 17)) | set(range(27, 33))
        for it in range(16):
            s = it % 2
            vb.dma("pool", (lambda e, s=s, it=it: e.dma_start(
                out=qtb[s].ap().rearrange("p (dt c) -> p dt c", c=P),
                in_=q_d[:, it * P: it * P + P].rearrange("(dt p) c -> p dt c", p=P))),
                reads=[("dram", "q")], writes=qtb[s].reg(), tag=f"lq{it}")
            band_kbs = [kb for kb in range(8) if (4 * kb - it + 15) in BAND_U]
            bslot = {"i": 0}
            for h in range(16):
                hp, r0 = h // 2, (h % 2) * 64
                prbs, ptb = prbss[h % 2], ptbs[h % 2]
                nmc, dnm = nmcs[h % 2], dnms[h % 2]
                nm = fap(nmc, 0, 8)
                nmr = freg(nmc, 0, 8)
                dn = fap(dnm, 0, 8)
                dnr = freg(dnm, 0, 8)
                for kb in range(8):
                    bank, breg = simbank()
                    vb.op("pe", (lambda e, bank=bank, s=s, hp=hp, r0=r0, h=h, kb=kb:
                          e.matmul(out=bank[:, 0:512],
                                   lhsT=qtb[s].prt(r0, r0 + 64, hp * P, hp * P + P),
                                   rhs=kres.prt(r0, r0 + 64, hp * 2 * T + kb * 512, hp * 2 * T + kb * 512 + 512),
                                   start=True, stop=True)),
                          reads=qtb[s].reg(hp * P, hp * P + P) + kres.reg(hp * 2 * T + kb * 512, hp * 2 * T + kb * 512 + 512),
                          writes=[breg], tag=f"sc{it}_{h}_{kb}")
                    if kb in band_kbs:
                        si = bslot["i"] % 2
                        bslot["i"] += 1
                        w0 = L1CB + 512 * kb - 128 * it - 127
                        vb.dma("pool", (lambda e, si=si, w0=w0, h=h: e.dma_start(
                            out=bnd.ap(si * 512, si * 512 + 512),
                            in_=bass_types_AP(tensor=mst_d.ap().tensor,
                                              offset=h * L1W + w0,
                                              ap=[[1, 128], [1, 512]]))),
                            reads=[("dram", "mst")], writes=bnd.reg(si * 512, si * 512 + 512),
                            tag=f"lb{it}_{h}_{kb}")
                        vb.op("pe", (lambda e, bank=bank, si=si: e.matmul(
                            out=bank[:, 0:512], lhsT=fey.ap(),
                            rhs=bnd.ap(si * 512, si * 512 + 512),
                            start=False, stop=True, skip_group_check=True)),
                            reads=fey.reg() + bnd.reg(si * 512, si * 512 + 512),
                            writes=[breg], tag=f"bd{it}_{h}_{kb}")
                    vb.op("dve", (lambda e, bank=bank, nm=nm, kb=kb: e.reduce_max(
                        out=nm[:, kb: kb + 1], in_=bank[:, 0:512], axis=AX.X,
                        negate=True)),
                        reads=[breg], writes=nmr, tag=f"nx{it}_{h}_{kb}")
                    bcol = fap(nmc, 8 + kb, 9 + kb)
                    vb.op("dve", (lambda e, nm=nm, kb=kb, bcol=bcol: e.tensor_scalar(
                        out=bcol, in0=nm[:, kb: kb + 1], scalar1=0.125, scalar2=None,
                        op0=ALU.mult)),
                        reads=nmr, writes=freg(nmc, 8, 16), tag=f"n8{it}_{h}_{kb}")
                    vb.op("act", (lambda e, bank=bank, kb=kb, bcol=bcol, prbs=prbs, dn=dn: e.activation(
                        out=prbs.ap(kb * 512, kb * 512 + 512), in_=bank[:, 0:512],
                        func=AF.Exp, bias=bcol, scale=eighth,
                        accum_out=dn[:, kb: kb + 1])),
                        reads=[breg] + freg(nmc, 8, 16) + eighth_rg,
                        writes=prbs.reg(kb * 512, kb * 512 + 512) + freg(dnm, 0, 8),
                        tag=f"ex{it}_{h}_{kb}")
                    ctc = CT.ap((it * 8 + kb) * 16 + h, (it * 8 + kb) * 16 + h + 1)
                    vb.op("dve", (lambda e, nm=nm, kb=kb, bcol=bcol, ctc=ctc:
                          e.tensor_tensor(out=nm[:, kb: kb + 1], in0=bcol, in1=ctc,
                                          op=ALU.subtract)),
                          reads=freg(nmc, 8, 16) + CT.reg((it * 8 + kb) * 16, (it * 8 + kb) * 16 + 16),
                          writes=nmr, tag=f"nc{it}_{h}_{kb}")
                nmg = fap(nmc, 16, 17)
                nmgr = freg(nmc, 16, 17)
                vb.op("dve", (lambda e, nm=nm, nmg=nmg: e.tensor_reduce(
                    out=nmg, in_=nm, axis=AX.X, op=ALU.min)),
                    reads=nmr, writes=nmgr, tag=f"ng{it}_{h}")
                fct = fap(nmc, 24, 32)
                fctr = freg(nmc, 24, 32)
                vb.op("dve", (lambda e, nm=nm, nmg=nmg, fct=fct: e.tensor_scalar(
                    out=fct, in0=nm, scalar1=-1.0, scalar2=nmg, op0=ALU.mult,
                    op1=ALU.add)),
                    reads=nmr + nmgr, writes=fctr, tag=f"fd{it}_{h}")
                vb.op("act", (lambda e, fct=fct: e.activation(
                    out=fct, in_=fct, func=AF.Exp)),
                    reads=fctr, writes=fctr, tag=f"fe{it}_{h}")
                vb.op("dve", (lambda e, dn=dn, fct=fct: e.tensor_tensor(
                    out=dn, in0=dn, in1=fct, op=ALU.mult)),
                    reads=dnr + fctr, writes=dnr, tag=f"dr{it}_{h}")
                dg = fap(dnm, 8, 9)
                dgr = freg(dnm, 8, 9)
                vb.op("dve", (lambda e, dn=dn, dg=dg: e.reduce_sum(
                    out=dg, in_=dn, axis=AX.X)),
                    reads=dnr, writes=dgr, tag=f"ds{it}_{h}")
                vb.op("dve", (lambda e, dg=dg: e.reciprocal(out=dg, in_=dg)),
                    reads=dgr, writes=dgr, tag=f"dc{it}_{h}")
                for kb in range(8):
                    vb.op("dve", (lambda e, fct=fct, kb=kb, prbs=prbs: e.tensor_scalar(
                        out=prbs.ap(kb * 512, kb * 512 + 512),
                        in0=prbs.ap(kb * 512, kb * 512 + 512),
                        scalar1=fct[:, kb: kb + 1], scalar2=None, op0=ALU.mult)),
                        reads=prbs.reg(kb * 512, kb * 512 + 512) + fctr,
                        writes=prbs.reg(kb * 512, kb * 512 + 512), tag=f"pr{it}_{h}_{kb}")
                abank, abreg = avbank()
                for grp in range(4):
                    tbank, tbreg = tpbank()
                    for cc in range(8):
                        ci = grp * 8 + cc
                        vb.op("pe", (lambda e, tbank=tbank, ci=ci, cc=cc, prbs=prbs: e.transpose(
                            out=tbank[:, cc * 128: cc * 128 + 128],
                            in_=prbs.ap(ci * 128, ci * 128 + 128),
                            identity=idn_b.ap())),
                            reads=prbs.reg(ci * 128, ci * 128 + 128) + idn_b.reg(),
                            writes=[tbreg], tag=f"pt{it}_{h}_{ci}")
                    vb.op("dve", (lambda e, tbank=tbank, grp=grp, ptb=ptb: e.tensor_copy(
                        out=ptb.ap(grp * 1024, grp * 1024 + 1024), in_=tbank[:, 0:1024])),
                        reads=[tbreg], writes=ptb.reg(grp * 1024, grp * 1024 + 1024),
                        tag=f"pe{it}_{h}_{grp}")
                for jt in range(32):
                    vb.op("pe", (lambda e, abank=abank, jt=jt, h=h, ptb=ptb: e.matmul(
                        out=abank[:, 0:64],
                        lhsT=ptb.ap(jt * 128, jt * 128 + 128),
                        rhs=vres.ap(jt * D + h * 64, jt * D + h * 64 + 64),
                        start=(jt == 0), stop=(jt == 31))),
                        reads=ptb.reg(jt * 128, jt * 128 + 128) + vres.reg(jt * D + h * 64, jt * D + h * 64 + 64),
                        writes=[abreg], tag=f"av{it}_{h}_{jt}")
                vb.op("dve", (lambda e, abank=abank, dg=dg, h=h: e.tensor_scalar(
                    out=aot.ap(h * 64, h * 64 + 64), in0=abank[:, 0:64],
                    scalar1=dg, scalar2=None, op0=ALU.mult)),
                    reads=[abreg] + dgr, writes=aot.reg(h * 64, h * 64 + 64),
                    tag=f"ao{it}_{h}")
            for dt in range(DT):
                tbank, tbreg = tpbank()
                vb.op("pe", (lambda e, tbank=tbank, dt=dt: e.transpose(
                    out=tbank[:, 0:128], in_=aot.ap(dt * 128, dt * 128 + 128),
                    identity=idn_b.ap())),
                    reads=aot.reg(dt * 128, dt * 128 + 128) + idn_b.reg(),
                    writes=[tbreg], tag=f"at{it}_{dt}")
                vb.op("dve", (lambda e, tbank=tbank, dt=dt: e.tensor_copy(
                    out=aof.ap(dt * 128, dt * 128 + 128), in_=tbank[:, 0:128])),
                    reads=[tbreg], writes=aof.reg(dt * 128, dt * 128 + 128),
                    tag=f"ae{it}_{dt}")
            for ot in range(DT):
                bank, breg = mmbank()
                for kt in range(DT):
                    vb.op("pe", (lambda e, bank=bank, ot=ot, kt=kt: e.matmul(
                        out=bank[:, 0:128],
                        lhsT=wob.ap(kt * D + ot * P, kt * D + ot * P + P),
                        rhs=aof.ap(kt * P, kt * P + P),
                        start=(kt == 0), stop=(kt == DT - 1))),
                        reads=wob.reg(kt * D + ot * P, kt * D + ot * P + P) + aof.reg(kt * P, kt * P + P),
                        writes=[breg], tag=f"wo{it}_{ot}_{kt}")
                ss = ot % 2
                vb.dma("sp", (lambda e, ss=ss, ot=ot, it=it: e.dma_start(
                    out=fap(stg[ss], 0, 128), in_=x_d[ot * P: ot * P + P, it * P: it * P + P])),
                    reads=[("dram", "x")], writes=freg(stg[ss]), tag=f"lx{it}_{ot}")
                vb.op("dve", (lambda e, bank=bank, ss=ss, ot=ot: e.scalar_tensor_tensor(
                    out=fap(stg[ss], 0, 128), in0=bank[:, 0:128],
                    scalar=fap(boc, ot, ot + 1), in1=fap(stg[ss], 0, 128),
                    op0=ALU.add, op1=ALU.add)),
                    reads=[breg] + freg(boc) + freg(stg[ss]), writes=freg(stg[ss]),
                    tag=f"xa{it}_{ot}")
                vb.dma("sp", (lambda e, ss=ss, ot=ot, it=it: e.dma_start(
                    out=xo_d[ot * P: ot * P + P, it * P: it * P + P],
                    in_=fap(stg[ss], 0, 128))),
                    reads=freg(stg[ss]), writes=[("dram", "xo")], tag=f"sx{it}_{ot}")

        stats = vb.finalize_and_emit(es, final_queue="pool")
    return nc, stats


def host_vec_fm(v):
    return np.ascontiguousarray(v.reshape(-1, P).T)



_PROGRAMS = {}


def _prog(name, builder):
    if name not in _PROGRAMS:
        _PROGRAMS[name] = builder()[0] if isinstance(builder(), tuple) else builder()
    return _PROGRAMS[name]


def _prog2(name, builder):
    if name not in _PROGRAMS:
        r = builder()
        _PROGRAMS[name] = r[0] if isinstance(r, tuple) else r
    return _PROGRAMS[name]


def _shards(X):
    out = []
    for c in range(8):
        b, h = c // 2, c % 2
        out.append(np.ascontiguousarray(X[b, h * T:(h + 1) * T].T))
    return out


def _unshard(parts):
    X = np.empty((B_, SEQ, D), np.float32)
    for c in range(8):
        b, h = c // 2, c % 2
        X[b, h * T:(h + 1) * T] = parts[c].T
    return X


def _launch(nc, in_maps, outs):
    res = run_bass_kernel_spmd(nc, in_maps, core_ids=list(range(8)))
    return [{o: r[o] for o in outs} for r in res.results]


def _l0ffn_call(X, W):
    ones, eye = _cst_ones(), host_eye()
    nc_ln = _prog2("ln", build_ln_program)
    r = _launch(nc_ln, [{"x_fm": s, "cst_ones": ones} for s in _shards(X)], ["y_fm"])
    S = _unshard([x["y_fm"] for x in r])
    nc_l0 = _prog2("l0layer", build_l0layer_program)
    masks = [host_l0_masks(0), host_l0_masks(1)]
    tbls = [host_l0_tables(0), host_l0_tables(1)]
    for i in range(6):
        im = []
        for c in range(8):
            b, h = c // 2, c % 2
            if h == 0:
                halo = np.zeros((HALO, D), np.float32)
            else:
                halo = S[b, T - HALO: T]
            xe = np.concatenate([halo, S[b, h * T:(h + 1) * T]], 0)
            im.append({
                "xe_fm": np.ascontiguousarray(xe.T),
                "w_qkv": W["l0f_wqkv"][i], "w_o": W["l0f_wo"][i],
                "w1p": W["l0f_w1p"][i], "w2p": W["l0f_w2p"][i],
                "qscale": W["l0f_qs"][i], "kscale": W["l0f_ks"][i],
                "ln_g": W["l0f_lng"][i], "ln_b": W["l0f_lnb"][i],
                "b1p": W["l0f_b1p"][i], "b2v": W["l0f_b2v"][i],
                "tbl": tbls[h], "masks": masks[h],
                "cst_eye": eye, "cst_ones": ones,
            })
        r = _launch(nc_l0, im, ["xo_fm"])
        S = _unshard([x["xo_fm"] for x in r])
    nc_ax = _prog2("axpy", build_axpy_program)
    xs = _shards(X)
    ys = _shards(S)
    r = _launch(nc_ax, [{"x_fm": xs[c], "y_fm": ys[c]} for c in range(8)], ["xo_fm"])
    return _unshard([x["xo_fm"] for x in r])


def _l0attn_call(X, W):
    ones, eye = _cst_ones(), host_eye()
    nc_a = _prog2("lina", build_linattn_a_program)
    r = _launch(nc_a, [{"x_fm": s, "w_qkv": W["l0a_wqkv"], "cst_ones": ones,
                        "cst_eye": eye} for s in _shards(X)],
                ["qs_tm", "k_tm", "v_tm", "kmax_fm"])
    nc_b = _prog2("linb", build_linattn_b_program)
    im = []
    for c in range(8):
        pair = c // 2
        km2 = np.concatenate([r[2 * pair]["kmax_fm"], r[2 * pair + 1]["kmax_fm"]], 1)
        im.append({"k_tm": r[c]["k_tm"], "v_tm": r[c]["v_tm"],
                   "kmax2": np.ascontiguousarray(km2), "cst_ones": ones})
    rb = _launch(nc_b, im, ["cnum"])
    nc_c = _prog2("linc", build_linattn_c_program)
    im = []
    xs = _shards(X)
    for c in range(8):
        pair = c // 2
        cn2 = np.stack([rb[2 * pair]["cnum"], rb[2 * pair + 1]["cnum"]], 0)
        im.append({"qs_fm": np.ascontiguousarray(r[c]["qs_tm"].T),
                   "cnum2": np.ascontiguousarray(cn2),
                   "w_out": W["l0a_wout"], "x_fm": xs[c]})
    rc = _launch(nc_c, im, ["xo_fm"])
    return _unshard([x["xo_fm"] for x in rc])


def _l1ffn_call(X, W):
    nc = _prog2("l1ffn", build_l1ffn_program)
    im = [{"x_fm": s, "gate_w": W["l1f_gw"], "e_w1": W["l1f_w1"],
           "e_w2": W["l1f_w2"], "e_b1": W["l1f_b1"], "e_b2": W["l1f_b2"],
           "e_lng": W["l1f_lng"], "e_lnb": W["l1f_lnb"],
           "cst_ones": _cst_ones(), "cst_eye": host_eye()} for s in _shards(X)]
    r = _launch(nc, im, ["xo_fm"])
    return _unshard([x["xo_fm"] for x in r])


def _l1attn_call(X, W):
    ones, eye, fey = _cst_ones(), host_eye(), host_flip_eye()
    nc_a = _prog2("l1qkv", build_l1qkv_program)
    im = [{"x_fm": s, "wqT": W["l1a_wqT"], "wkT": W["l1a_wkT"], "wvT": W["l1a_wvT"],
           "bq": W["l1a_bq"], "bk": W["l1a_bk"], "bv": W["l1a_bv"],
           "cst_ones": ones} for s in _shards(X)]
    r = _launch(nc_a, im, ["q_fm", "k_fm", "v_fm"])
    nc_b = _prog2("l1core", build_l1core_program)
    im = []
    xs = _shards(X)
    for c in range(8):
        pair, h = c // 2, c % 2
        k_full = np.concatenate([r[2 * pair]["k_fm"], r[2 * pair + 1]["k_fm"]], 1)
        v_full = np.concatenate([r[2 * pair]["v_fm"].T, r[2 * pair + 1]["v_fm"].T], 0)
        selA, selB = host_l1_selAB(h)
        im.append({"q_fm": r[c]["q_fm"], "k_fm": np.ascontiguousarray(k_full),
                   "v_tm": np.ascontiguousarray(v_full), "x_fm": xs[c],
                   "woT": W["l1a_woT"], "bo": W["l1a_bo"], "relb": W["l1a_relb"],
                   "selA": selA, "selB": selB, "shiftv": host_l1_shift(h),
                   "cst_ones": ones, "cst_eye": eye, "cst_fey": fey})
    rb = _launch(nc_b, im, ["xo_fm"])
    return _unshard([x["xo_fm"] for x in rb])


def _prep_weights(ip):
    W = {}
    W["l0f_wqkv"] = [np.ascontiguousarray(ip["L0_ffn_Wqkv"][i]) for i in range(6)]
    W["l0f_wo"] = [np.ascontiguousarray(ip["L0_ffn_Wo"][i]) for i in range(6)]
    W["l0f_w1p"] = [host_l0_w1p(ip["L0_ffn_W1"][i]) for i in range(6)]
    W["l0f_w2p"] = [host_l0_w2p(ip["L0_ffn_W2"][i]) for i in range(6)]
    W["l0f_qs"] = [np.ascontiguousarray(ip["L0_ffn_q_scale"][i][None, :]) for i in range(6)]
    W["l0f_ks"] = [np.ascontiguousarray(ip["L0_ffn_k_scale"][i][None, :]) for i in range(6)]
    W["l0f_lng"] = [host_vec_fm(ip["L0_ffn_ln_g"][i]) for i in range(6)]
    W["l0f_lnb"] = [host_vec_fm(ip["L0_ffn_ln_b"][i]) for i in range(6)]
    W["l0f_b1p"] = [host_l0_b1p(ip["L0_ffn_b1"][i]) for i in range(6)]
    W["l0f_b2v"] = [host_vec_fm(ip["L0_ffn_b2"][i]) for i in range(6)]
    W["l0a_wqkv"] = np.ascontiguousarray(ip["L0_attn_w_qkv"])
    W["l0a_wout"] = np.ascontiguousarray(ip["L0_attn_w_out"])
    W["l1f_gw"] = np.ascontiguousarray(ip["L1_ffn_gate_w"])
    W["l1f_w1"] = np.ascontiguousarray(ip["L1_ffn_w1"])
    W["l1f_w2"] = np.ascontiguousarray(ip["L1_ffn_w2"])
    W["l1f_b1"] = np.stack([host_vec_fm(ip["L1_ffn_b1"][e]) for e in range(NE)])
    W["l1f_b2"] = np.stack([host_vec_fm(ip["L1_ffn_b2"][e]) for e in range(NE)])
    W["l1f_lng"] = np.stack([host_vec_fm(ip["L1_ffn_ln_g"][e]) for e in range(NE)])
    W["l1f_lnb"] = np.stack([host_vec_fm(ip["L1_ffn_ln_b"][e]) for e in range(NE)])
    W["l1a_wqT"] = np.ascontiguousarray(ip["L1_attn_Wq"].T)
    W["l1a_wkT"] = np.ascontiguousarray(ip["L1_attn_Wk"].T)
    W["l1a_wvT"] = np.ascontiguousarray(ip["L1_attn_Wv"].T)
    W["l1a_woT"] = np.ascontiguousarray(ip["L1_attn_Wo"].T)
    W["l1a_bq"] = host_vec_fm(ip["L1_attn_bq"])
    W["l1a_bk"] = host_vec_fm(ip["L1_attn_bk"])
    W["l1a_bv"] = host_vec_fm(ip["L1_attn_bv"])
    W["l1a_bo"] = host_vec_fm(ip["L1_attn_bo"])
    W["l1a_relb"] = np.ascontiguousarray(ip["L1_attn_rel_bias"])
    return W


def kernel(**inputs):
    ip = {k: np.asarray(v, dtype=np.float32) for k, v in inputs.items()}
    W = _prep_weights(ip)
    X = np.ascontiguousarray(ip["L0_attn_x"])
    X = _l0ffn_call(X, W)
    X = _l0attn_call(X, W)
    X = _l0ffn_call(X, W)
    X = _l1ffn_call(X, W)
    X = _l1attn_call(X, W)
    X = _l1ffn_call(X, W)
    return X
